# version 18
# baseline (speedup 1.0000x reference)
import contextlib

import numpy as np
import ml_dtypes
import concourse.bass as bass
import concourse.mybir as mybir
from concourse.bass_utils import run_bass_kernel_spmd

F32 = mybir.dt.float32
BF16 = mybir.dt.bfloat16
Alu = mybir.AluOpType
Act = mybir.ActivationFunctionType
AX = mybir.AxisListType
bfl = ml_dtypes.bfloat16

B, S, D = 16, 2048, 1024
H, DK = 16, 64
FF = 4096
T = 2 * S
BLK = 512
NBLK = S // BLK
NTT = BLK // 128
NIC = D // 128
NPAIR = H // 2
NKT_SEQ = S // 128
NHC = FF // 128
MAGIC = float(np.float32(1.5 * 2.0 ** 23))
INV_D = 1.0 / D
INV_NW = 1.0 / (FF * D)
EPS_LN = 1e-5
EPS_BIT = 1e-5


class Prog:
    ENGS = ("sync", "gp", "dve", "scal", "te")
    DSEMS = ("slab0", "slab1", "slab2", "slab3", "slab4", "slab5", "xh0",
             "xh1", "rope", "nrm", "att", "outs", "sto", "misc")

    def __init__(self, nc, stack):
        self.nc = nc
        self.steps = {e: [] for e in self.ENGS}
        self.sems = {}
        self.cnt = {}
        for e in self.ENGS + self.DSEMS:
            self.sems[e] = stack.enter_context(nc.semaphore("sem_" + e))
            self.cnt[e] = 0

    def step(self, engine, fn, waits=()):
        self.steps[engine].append((tuple(waits), fn))

    def run(self, engine_name, eng):
        for waits, fn in self.steps[engine_name]:
            for (sname, val) in waits:
                if val > 0:
                    eng.wait_ge(self.sems[sname], val)
            fn(eng)

    def dma(self, q, out_ap, in_ap, waits=(), sem=None):
        sname = sem if sem is not None else q
        semh = self.sems[sname]

        def fn(eng, o=out_ap, i=in_ap):
            eng.dma_start(out=o, in_=i).then_inc(semh, 16)

        self.step(q, fn, waits)
        self.cnt[sname] += 16
        return self.cnt[sname]

    def comp(self, engine, fn, waits=()):
        sem = self.sems[engine]

        def wrapped(eng, fn=fn):
            last = fn(eng)
            if last is None or not hasattr(last, "then_inc"):
                last = eng.drain()
            last.then_inc(sem, 1)

        self.step(engine, wrapped, waits)
        self.cnt[engine] += 1
        return self.cnt[engine]


def _host_consts():
    inv_freq = np.float32(10000.0) ** (
        np.arange(0, DK, 2, dtype=np.float32) / np.float32(DK))
    ang = np.arange(S, dtype=np.float32)[:, None] / inv_freq[None, :]
    cos = np.cos(ang).astype(np.float32).T
    sin = np.sin(ang).astype(np.float32).T
    cosF = np.empty((DK, S), np.float32)
    sinF = np.empty((DK, S), np.float32)
    cosF[0::2] = cos
    cosF[1::2] = cos
    sinF[0::2] = -sin
    sinF[1::2] = sin
    cosP = np.ascontiguousarray(np.concatenate([cosF, cosF], axis=0))
    sinP = np.ascontiguousarray(np.concatenate([sinF, sinF], axis=0))
    eighth = np.float32(0.125)
    consts = {
        "ropeQc": cosP * eighth, "ropeQs": sinP * eighth,
        "ropeKc": cosP, "ropeKs": sinP,
    }
    identf = np.eye(128, dtype=np.float32)
    identb = np.eye(128, dtype=bfl)
    kl = np.arange(128)[:, None]
    ql = np.arange(128)[None, :]
    tri01 = (kl <= ql).astype(bfl)
    trineg = np.where(ql > kl, np.float32(-1e30), np.float32(0.0)).astype(bfl)
    swapT = np.zeros((128, 128), dtype=bfl)
    for dp in range(128):
        swapT[dp ^ 1, dp] = 1.0
    return consts, identf, tri01, swapT, identb, trineg


def build_program():
    nc = bass.Bass("TRN2", target_bir_lowering=False)
    stack = contextlib.ExitStack()
    stack.__enter__()

    x_h = nc.dram_tensor("x", [T, D], F32, kind="ExternalInput").ap()
    out_h = nc.dram_tensor("out", [T, D], F32, kind="ExternalOutput").ap()
    wqkvT_h = nc.dram_tensor("wqkvT", [D, 3 * D], F32, kind="ExternalInput").ap()
    woT_h = nc.dram_tensor("woT", [D, D], F32, kind="ExternalInput").ap()
    w1T_h = nc.dram_tensor("w1T", [D, FF], F32, kind="ExternalInput").ap()
    w2T_h = nc.dram_tensor("w2T", [FF, D], F32, kind="ExternalInput").ap()
    b1_h = nc.dram_tensor("b1", [1, FF], F32, kind="ExternalInput").ap()
    b2_h = nc.dram_tensor("b2", [1, D], F32, kind="ExternalInput").ap()
    identf_h = nc.dram_tensor("identf", [128, 128], F32, kind="ExternalInput").ap()
    identb_h = nc.dram_tensor("identb", [128, 128], BF16, kind="ExternalInput").ap()
    trineg_h = nc.dram_tensor("trineg", [128, 128], BF16, kind="ExternalInput").ap()
    tri_h = nc.dram_tensor("tri01", [128, 128], BF16, kind="ExternalInput").ap()
    swap_h = nc.dram_tensor("swapT", [128, 128], BF16, kind="ExternalInput").ap()
    rope_h = {k: nc.dram_tensor(k, [128, S], F32, kind="ExternalInput").ap()
              for k in ("ropeQc", "ropeQs", "ropeKc", "ropeKs")}

    scrQ = nc.dram_tensor("scrQ", [128, NIC * 1024], BF16, kind="Internal").ap()
    scrK = nc.dram_tensor("scrK", [128, NIC * 1024], BF16, kind="Internal").ap()
    scrV = nc.dram_tensor("scrV", [128, NIC * 1024], BF16, kind="Internal").ap()
    scrWo = nc.dram_tensor("scrWo", [128, NIC * 1024], BF16, kind="Internal").ap()
    scrW1 = nc.dram_tensor("scrW1", [128, 4 * NIC * 1024], BF16, kind="Internal").ap()
    scrW2 = nc.dram_tensor("scrW2", [128, NHC * 1024], BF16, kind="Internal").ap()
    scrInv = nc.dram_tensor("scrInv", [1, 1024], F32, kind="Internal").ap()

    sbt = lambda name, shape, dt: stack.enter_context(nc.sbuf_tensor(name, shape, dt))
    pst = lambda name, shape: stack.enter_context(nc.psum_tensor(name, shape, F32))

    identf = sbt("identf_s", [128, 128], F32)
    identb = sbt("identb_s", [128, 128], BF16)
    trineg = sbt("trineg_s", [128, 128], BF16)
    swapT = sbt("swapT_s", [128, 128], BF16)
    ones_r = sbt("ones_r", [1, 128], F32)
    ones_c = sbt("ones_c", [128, 1], F32)
    b1b = sbt("b1b", [128, FF], F32)
    b2b = sbt("b2b", [128, D], F32)
    mh = sbt("mh", [128, 24], F32)
    wsb = sbt("wsb", [128, 8], F32)
    stats = sbt("stats", [128, 32], F32)
    wslab = sbt("wslab", [128, 6 * 1024], BF16)
    ropeT = sbt("ropeT", [128, 2 * BLK], F32)
    outtA = ropeT

    kfm = sbt("kfm", [128, NPAIR * S], BF16)
    vex = sbt("vex", [128, NKT_SEQ * H * 65], BF16)
    xblk = sbt("xblk", [128, NTT * D], F32)
    gx = sbt("gx", [128, 2 * FF], F32)
    xnf = gx[:, 0:4096]
    gbuf = gx
    xnqfm = sbt("xnqfm", [128, NIC * BLK], BF16)
    qrot = sbt("qrot", [128, NIC * BLK], BF16)
    qpre = sbt("qpre", [128, NIC * BLK], BF16)
    attn_fm = qpre
    ptb = sbt("ptb", [128, 6 * BLK], BF16)
    stage = sbt("stage", [128, BLK], BF16)
    invs = sbt("invs", [128, 2 * BLK], F32)
    outtB = invs
    scrA = sbt("scrA", [128, 512], F32)
    hraw = sbt("hraw", [128, 2 * 512], F32)
    t1b = hraw[:, 0:512]
    t2b = hraw[:, 512:1024]
    hq_fm = sbt("hq_fm", [128, NHC * 256], BF16)
    ternb = hq_fm[:, 0:4096]
    sums = hraw[0:1, 0:1024]
    rmap = invs[0:64, 1008:1024]

    pbd = [pst("pbd%d" % i, [128, 1024]) for i in range(4)]
    pb = [pbd[i // 2][:, 512 * (i % 2):512 * (i % 2) + 512] for i in range(8)]

    p = Prog(nc, stack)

    d_c = p.dma("sync", identf[:], identf_h, sem="misc")
    d_c = p.dma("sync", swapT[:], swap_h, sem="misc")
    d_c = p.dma("sync", identb[:], identb_h, sem="misc")
    d_c = p.dma("sync", trineg[:], trineg_h, sem="misc")

    def fn(eng):
        eng.memset(ones_r[:], 1.0)
        eng.memset(ones_c[:], 1.0)
        eng.memset(stats[:], 0.0)
        eng.drain()
        eng.memset(stats[:, 31:32], EPS_LN)
    v_ones = p.comp("dve", fn, waits=[("misc", d_c)])

    d_b1 = p.dma("sync", gx[0:1, 0:FF], b1_h, sem="misc")
    d_b2 = p.dma("sync", gx[0:1, 4096:4096 + D], b2_h, sem="misc")

    def fn(eng):
        for seg in range(8):
            last = eng.matmul(pb[seg], ones_r[0:1, 0:128],
                              gx[0:1, 512 * seg:512 * (seg + 1)], start=True, stop=True)
        return last
    t_w = p.comp("te", fn, waits=[("misc", d_b2), ("dve", v_ones)])

    def fn(eng):
        for seg in range(8):
            eng.copy(b1b[:, 512 * seg:512 * (seg + 1)], pb[seg])
    s_w = p.comp("scal", fn, waits=[("te", t_w)])

    def fn(eng):
        for seg in range(2):
            last = eng.matmul(pb[seg], ones_r[0:1, 0:128],
                              gx[0:1, 4096 + 512 * seg:4096 + 512 * (seg + 1)],
                              start=True, stop=True)
        return last
    t_w = p.comp("te", fn, waits=[("misc", d_b2), ("scal", s_w)])

    def fn(eng):
        for seg in range(2):
            eng.copy(b2b[:, 512 * seg:512 * (seg + 1)], pb[seg])
    s_w = p.comp("scal", fn, waits=[("te", t_w)])

    def big_load(q, buf_ap, src_ap, width, waits, sem):
        d = 0
        wq = width // 4
        for a in range(4):
            d = p.dma(q, buf_ap[:, wq * a:wq * (a + 1)],
                      src_ap[:, wq * a:wq * (a + 1)],
                      waits=waits if a == 0 else (), sem=sem)
        return d

    slabs = [xblk[:, 0:4096], gx[:, 4096:8192]]
    slab_free = [("dve", 0), ("te", t_w)]
    tern_free = ("sto", 0)

    s_sq = s_w
    t_norm = 0
    for ic in range(NIC):
        bi = ic % 2
        buf = slabs[bi][:, 0:3072]
        sname = "slab%d" % bi
        d_s = big_load("sync", buf, wqkvT_h[128 * ic:128 * (ic + 1), :], 3072,
                       [slab_free[bi]], sname)
        v_c = p.comp("dve", lambda eng, b=buf: eng.tensor_copy(ternb[:, 0:3072], b),
                     waits=[(sname, d_s), tern_free])
        g_l = p.dma("gp", scrQ[:, 1024 * ic:1024 * (ic + 1)], ternb[:, 0:1024], waits=[("dve", v_c)], sem="sto")
        g_l = p.dma("gp", scrK[:, 1024 * ic:1024 * (ic + 1)], ternb[:, 1024:2048], sem="sto")
        g_l = p.dma("gp", scrV[:, 1024 * ic:1024 * (ic + 1)], ternb[:, 2048:3072], sem="sto")
        tern_free = ("sto", g_l)
        s_sq = p.comp("scal", lambda eng, b=buf: eng.activation(
            gx[:, 0:2048], b[:, 0:2048], Act.Square),
            waits=[(sname, d_s), ("scal", s_sq), ("te", t_norm)])

        def fn(eng, ic=ic):
            for seg in range(4):
                last = eng.matmul(pb[2 + seg][0:1, :], ones_c[:, 0:1],
                                  gx[:, 512 * seg:512 * (seg + 1)],
                                  start=(ic == 0), stop=(ic == NIC - 1))
            return last
        t_norm = p.comp("te", fn, waits=[("scal", s_sq)])
        slab_free[bi] = ("scal", s_sq)

    def fn(eng):
        for h in range(16):
            eng.reduce_sum(sums[0:1, h:h + 1],
                           pb[2 + (64 * h) // 512][0:1, (64 * h) % 512:(64 * h) % 512 + 64],
                           axis=AX.X)
        for h in range(16):
            c = 1024 + 64 * h
            eng.reduce_sum(sums[0:1, 16 + h:17 + h],
                           pb[2 + c // 512][0:1, c % 512:c % 512 + 64], axis=AX.X)
        eng.drain()
        eng.tensor_mul(sums[0:1, 32:48], sums[0:1, 0:16], sums[0:1, 16:32])
    v_fro = p.comp("dve", fn, waits=[("te", t_norm)])
    s_fr = p.comp("scal", lambda eng: eng.activation(sums[0:1, 48:64],
                                                     sums[0:1, 32:48], Act.Sqrt),
                  waits=[("dve", v_fro)])
    v_nm = p.comp("dve", lambda eng: eng.tensor_scalar(
        sums[0:1, 64:80], sums[0:1, 48:64], -1.02 * 128.0, None, Alu.mult),
        waits=[("scal", s_fr)])
    t_mh = p.comp("te", lambda eng: eng.matmul(pb[6][:, 0:16], ones_r[0:1, 0:128],
                                               sums[0:1, 64:80], start=True, stop=True),
                  waits=[("dve", v_nm)])
    s_mh = p.comp("scal", lambda eng: eng.copy(mh[:, 0:16], pb[6][:, 0:16]),
                  waits=[("te", t_mh)])
    def fn(eng):
        for c in range(8):
            eng.tensor_tensor(mh[:, 16 + c:17 + c], mh[:, 2 * c:2 * c + 1],
                              mh[:, 2 * c + 1:2 * c + 2], Alu.min)
    s_mh = p.comp("dve", fn, waits=[("scal", s_mh)])

    for fc in range(NIC):
        bi = fc % 2
        buf = slabs[bi][:, 0:1024]
        sname = "slab%d" % bi
        d_s = big_load("sync", buf, woT_h[128 * fc:128 * (fc + 1), :], 1024,
                       [slab_free[bi]], sname)
        v_c = p.comp("dve", lambda eng, b=buf: eng.tensor_copy(ternb[:, 0:1024], b),
                     waits=[(sname, d_s), tern_free])
        g_l = p.dma("gp", scrWo[:, 1024 * fc:1024 * (fc + 1)], ternb[:, 0:1024], waits=[("dve", v_c)], sem="sto")
        tern_free = ("sto", g_l)
        slab_free[bi] = ("dve", v_c)

    v_a = p.comp("dve", lambda eng: eng.memset(stats[:, 28:31], 0.0),
                 waits=[("scal", s_mh)])
    slabs3 = [xblk[:, 0:4096], gx[:, 4096:8192], gx[:, 0:4096]]
    free3 = [slab_free[0], slab_free[1], ("scal", s_mh)]
    for i in range(NIC + NHC):
        w2 = i >= NIC
        width = 1024 if w2 else 4096
        src_ap = (w2T_h[128 * (i - NIC):128 * (i - NIC + 1), :] if w2
                  else w1T_h[128 * i:128 * (i + 1), :])
        bi = i % 3
        buf = slabs3[bi][:, 0:width]
        q = "sync" if i % 2 == 0 else "gp"
        sname = "slab%d" % bi
        d_s = big_load(q, buf, src_ap, width, [free3[bi]], sname)

        def fn(eng, b=buf, w2=w2):
            eng.reduce_sum(stats[:, 30:31], b, axis=AX.X, apply_absolute_value=True)
            eng.drain()
            acc = stats[:, 29:30] if w2 else stats[:, 28:29]
            eng.tensor_add(acc, acc, stats[:, 30:31])
        v_a = p.comp("dve", fn, waits=[(sname, d_s), ("dve", v_a)])
        free3[bi] = ("dve", v_a)
    slab_free[0] = free3[0]
    slab_free[1] = free3[1]
    gx_lo_free = free3[2]

    t_w = p.comp("te", lambda eng: (
        eng.matmul(pb[0][0:1, 0:1], ones_c[:, 0:1], stats[:, 28:29], start=True, stop=True),
        eng.matmul(pb[0][0:1, 1:2], ones_c[:, 0:1], stats[:, 29:30], start=True, stop=True))[-1],
        waits=[("dve", v_a)])
    s_w = p.comp("scal", lambda eng: eng.copy(sums[0:1, 100:102], pb[0][0:1, 0:2]),
                 waits=[("te", t_w)])

    def fn(eng):
        eng.tensor_scalar(sums[0:1, 102:104], sums[0:1, 100:102], INV_NW, EPS_BIT,
                          Alu.mult, Alu.max)
        eng.drain()
        eng.tensor_scalar(sums[0:1, 104:105], sums[0:1, 102:103], 0.5, None, Alu.mult)
        eng.tensor_scalar(sums[0:1, 105:106], sums[0:1, 102:103], -0.5, None, Alu.mult)
        eng.tensor_scalar(sums[0:1, 106:107], sums[0:1, 102:103], 1.0 / 127.0, None, Alu.mult)
        eng.tensor_scalar(sums[0:1, 107:108], sums[0:1, 103:104], 0.5, None, Alu.mult)
        eng.tensor_scalar(sums[0:1, 108:109], sums[0:1, 103:104], -0.5, None, Alu.mult)
        eng.tensor_scalar(sums[0:1, 109:110], sums[0:1, 103:104], 1.0 / 127.0, None, Alu.mult)
    v_ws = p.comp("dve", fn, waits=[("scal", s_w)])
    t_w = p.comp("te", lambda eng: eng.matmul(pb[1][:, 0:6], ones_r[0:1, 0:128],
                                              sums[0:1, 104:110], start=True, stop=True),
                 waits=[("dve", v_ws)])
    s_wsb = p.comp("scal", lambda eng: eng.copy(wsb[:, 0:6], pb[1][:, 0:6]),
                   waits=[("te", t_w)])

    for ic in range(NIC):
        bi = ic % 2
        buf = slabs[bi][:, 0:4096]
        sname = "slab%d" % bi
        d_s = big_load("sync" if ic % 2 == 0 else "gp", buf,
                       w1T_h[128 * ic:128 * (ic + 1), :], 4096,
                       [slab_free[bi], ("scal", s_wsb)] + ([gx_lo_free] if ic == 0 else []),
                       sname)

        def fn(eng, b=buf):
            eng.tensor_single_scalar(gx[:, 0:4096], b, wsb[:, 0:1], Alu.is_gt)
            eng.drain()
            eng.tensor_single_scalar(b, b, wsb[:, 1:2], Alu.is_lt)
            eng.drain()
            eng.tensor_sub(ternb[:, 0:4096], gx[:, 0:4096], b)
        v_t = p.comp("dve", fn, waits=[(sname, d_s), tern_free])
        g_l = 0
        for q in range(4):
            g_l = p.dma("gp",
                        scrW1[:, 1024 * (NIC * q + ic):1024 * (NIC * q + ic) + 1024],
                        ternb[:, 1024 * q:1024 * (q + 1)],
                        waits=[("dve", v_t)] if q == 0 else (), sem="sto")
        tern_free = ("sto", g_l)
        slab_free[bi] = ("dve", v_t)
    for hc in range(NHC):
        bi = hc % 2
        buf = slabs[bi][:, 0:1024]
        sname = "slab%d" % bi
        d_s = big_load("sync" if hc % 2 == 0 else "gp", buf,
                       w2T_h[128 * hc:128 * (hc + 1), :], 1024,
                       [slab_free[bi]], sname)

        def fn(eng, b=buf):
            eng.tensor_single_scalar(gx[:, 0:1024], b, wsb[:, 3:4], Alu.is_gt)
            eng.drain()
            eng.tensor_single_scalar(b, b, wsb[:, 4:5], Alu.is_lt)
            eng.drain()
            eng.tensor_sub(ternb[:, 0:1024], gx[:, 0:1024], b)
        v_t = p.comp("dve", fn, waits=[(sname, d_s), tern_free])
        g_l = p.dma("gp", scrW2[:, 1024 * hc:1024 * (hc + 1)], ternb[:, 0:1024], waits=[("dve", v_t)], sem="sto")
        tern_free = ("sto", g_l)
        slab_free[bi] = ("dve", v_t)

    prep_end = [("sto", p.cnt["sto"]), ("dve", p.cnt["dve"]),
                ("scal", p.cnt["scal"]), ("te", p.cnt["te"])]
    first_stream_extra = [("sto", p.cnt["sto"])]

    slab_hist = {0: 0, 1: 0, 2: 0, 3: 0, 4: 0, 5: 0}

    def slab_load(i, region_ap, extra=()):
        sl = i % 6
        sname = "slab%d" % sl
        dwl = [("te", slab_hist[sl])] + list(extra)
        d = p.dma("sync", wslab[:, 1024 * sl:1024 * (sl + 1)], region_ap,
                  waits=dwl, sem=sname)
        return sl, sname, d

    def stream_pass(n_slabs, region_fn, mm_fn, ic0_extra):
        t = 0
        for i in range(n_slabs):
            extra = first_stream_extra if (first_stream_extra and i < 6) else ()
            sl, sname, d = slab_load(i, region_fn(i), extra)
            wl = [(sname, d)]
            if i == 0:
                wl += ic0_extra
            t = p.comp("te", mm_fn(i, 1024 * sl), waits=wl)
            slab_hist[sl] = t
        if first_stream_extra:
            first_stream_extra.clear()
        return t

    xhalf_free = [("dve", 0), ("dve", 0)]
    rope_free = ("dve", 0)
    g_out = 0
    g_out_ropefree = 0
    out_final = 0
    pend_norm = [None]

    for seq in range(2):
        for blk in range(NBLK):
            r0 = S * seq + BLK * blk
            nkt = 4 * (blk + 1)

            g_x = [0] * NTT
            for tt in range(NTT):
                wx = [xhalf_free[tt // 2]]
                if seq == 0 and blk == 0 and tt == 0:
                    wx = list(prep_end)
                for sub in range(2):
                    g_x[tt] = p.dma(
                        "gp", xblk[64 * sub:64 * (sub + 1), D * tt:D * (tt + 1)],
                        x_h[r0 + 128 * tt + 64 * sub:r0 + 128 * tt + 64 * (sub + 1), :],
                        waits=wx if ((tt == 0 or tt == 2) and sub == 0) else (),
                        sem="xh%d" % (tt // 2))
            g_x = [g_x[1], g_x[1], g_x[3], g_x[3]]
            d_rope = 0
            for i, k in enumerate(("ropeQc", "ropeQs")):
                d_rope = p.dma("sync", ropeT[:, BLK * i:BLK * (i + 1)],
                               rope_h[k][:, BLK * blk:BLK * (blk + 1)],
                               waits=[rope_free, ("outs", g_out_ropefree)] if i == 0 else (),
                               sem="rope")

            s_sq = [0] * NTT
            v_var = [0] * NTT
            s_nrm = [0] * NTT
            t_tr = p.cnt["te"]
            s_ev = {0: p.cnt["scal"], 1: p.cnt["scal"]}

            def emit_sq(tt):
                xt = xblk[:, D * tt:D * (tt + 1)]
                s_sq[tt] = p.comp("scal", lambda eng, xt=xt, tt=tt: eng.activation(
                    gx[:, 4096:5120], xt, Act.Square,
                    accum_out=stats[:, 26 + tt:27 + tt]),
                    waits=[("xh%d" % (tt // 2), g_x[tt])])

            def emit_ln(tt):
                xt = xblk[:, D * tt:D * (tt + 1)]
                xnt = xnf[:, D * tt:D * (tt + 1)]

                def fn(eng, xt=xt, tt=tt):
                    eng.reduce_sum(stats[:, 30:31], xt, axis=AX.X)
                    eng.drain()
                    eng.tensor_scalar(stats[:, tt:tt + 1], stats[:, 30:31],
                                      -INV_D, None, Alu.mult)
                    eng.drain()
                    eng.tensor_mul(stats[:, 30:31], stats[:, tt:tt + 1],
                                   stats[:, tt:tt + 1])
                    eng.drain()
                    eng.scalar_tensor_tensor(stats[:, 26 + tt:27 + tt],
                                             stats[:, 26 + tt:27 + tt], INV_D,
                                             stats[:, 30:31], Alu.mult, Alu.subtract)
                v_var[tt] = p.comp("dve", fn, waits=[("xh%d" % (tt // 2), g_x[tt]),
                                                     ("scal", s_sq[tt])])

                def fn(eng, xt=xt, xnt=xnt, tt=tt):
                    eng.activation(wsb[:, 6:7], stats[:, 26 + tt:27 + tt], Act.Ln,
                                   bias=stats[:, 31:32])
                    eng.drain()
                    eng.activation(stats[:, 4 + tt:5 + tt], wsb[:, 6:7],
                                   Act.Exp, scale=-0.5)
                    eng.drain()
                    eng.activation(wsb[:, 7:8], stats[:, tt:tt + 1], Act.Identity,
                                   scale=stats[:, 4 + tt:5 + tt])
                    eng.drain()
                    eng.activation(xnt, xt, Act.Identity, bias=wsb[:, 7:8],
                                   scale=stats[:, 4 + tt:5 + tt])
                s_nrm[tt] = p.comp("scal", fn, waits=[("dve", v_var[tt])])

            def emit_tr(tt, t_tr):
                xnt = xnf[:, D * tt:D * (tt + 1)]
                for g2 in range(2):
                    def fn(eng, xnt=xnt, g2=g2):
                        for q in range(4):
                            icx = 4 * g2 + q
                            last = eng.transpose(pb[6 + g2][:, 128 * q:128 * (q + 1)],
                                                 xnt[:, 128 * icx:128 * (icx + 1)], identf[:])
                        return last
                    t_tr = p.comp("te", fn,
                                  waits=[("scal", max(s_nrm[tt], s_ev[g2]))])

                    def fn(eng, tt=tt, g2=g2):
                        for q in range(4):
                            icx = 4 * g2 + q
                            eng.copy(
                                xnqfm[:, BLK * icx + 128 * tt:BLK * icx + 128 * (tt + 1)],
                                pb[6 + g2][:, 128 * q:128 * (q + 1)])
                    s_ev[g2] = p.comp("scal", fn, waits=[("te", t_tr)])
                return t_tr

            emit_sq(0)
            emit_sq(1)
            emit_ln(0)
            t_tr = emit_tr(0, t_tr)
            emit_ln(1)
            t_tr = emit_tr(1, t_tr)
            emit_sq(2)
            emit_sq(3)
            emit_ln(2)
            t_tr = emit_tr(2, t_tr)
            emit_ln(3)
            t_tr = emit_tr(3, t_tr)

            for tt in range(NTT):
                xnt = xnf[:, D * tt:D * (tt + 1)]

                def fn(eng, xnt=xnt, tt=tt):
                    eng.reduce_max(stats[:, 8 + tt:9 + tt], xnt, axis=AX.X,
                                   apply_absolute_value=True)
                    eng.drain()
                    eng.tensor_scalar(stats[:, 8 + tt:9 + tt],
                                      stats[:, 8 + tt:9 + tt],
                                      EPS_BIT, None, Alu.max)
                    eng.drain()
                    eng.reciprocal(stats[:, 30:31], stats[:, 8 + tt:9 + tt])
                    eng.drain()
                    eng.tensor_scalar(stats[:, 12 + tt:13 + tt], stats[:, 30:31],
                                      127.0, None, Alu.mult)
                    eng.tensor_mul(stats[:, 16 + tt:17 + tt],
                                   stats[:, 8 + tt:9 + tt], wsb[:, 2:3])
                p.comp("dve", fn, waits=[("scal", s_nrm[tt])])
            v_xnfm = max(s_ev[0], s_ev[1])

            def qkv_pass(region, kind, ic0_waits):
                def mk(ic, cb, kind=kind):
                    def fn(eng, ic=ic, cb=cb, kind=kind):
                        if kind == "V":
                            for tt in range(NTT):
                                for hf in range(2):
                                    last = eng.matmul(
                                        pb[2 * tt + hf],
                                        xnqfm[:, BLK * ic + 128 * tt:BLK * ic + 128 * (tt + 1)],
                                        wslab[:, cb + 512 * hf:cb + 512 * (hf + 1)],
                                        start=(ic == 0), stop=(ic == NIC - 1))
                        else:
                            for oc in range(8):
                                last = eng.matmul(
                                    pb[oc],
                                    wslab[:, cb + 128 * oc:cb + 128 * (oc + 1)],
                                    xnqfm[:, BLK * ic:BLK * (ic + 1)],
                                    start=(ic == 0), stop=(ic == NIC - 1))
                        return last
                    return fn
                return stream_pass(NIC, lambda i: region[:, 1024 * i:1024 * (i + 1)],
                                   mk, ic0_waits)

            def rope_apply(scale8, dst, t_pass, v_prev):
                s_ev = 0
                t_sw = 0
                for c in range(NPAIR):
                    wl = [("te", t_pass if c == 0 else t_sw)]
                    s_ev = p.comp("scal", lambda eng, c=c: eng.copy(
                        qpre[:, BLK * c:BLK * (c + 1)], pb[c]), waits=wl)
                    t_sw = p.comp("te", lambda eng, c=c: eng.matmul(
                        pb[c], swapT[:], qpre[:, BLK * c:BLK * (c + 1)],
                        start=True, stop=True), waits=[("scal", s_ev)])

                    def fn(eng, c=c, scale8=scale8):
                        if scale8:
                            eng.scalar_tensor_tensor(
                                t1b[:, :], qpre[:, BLK * c:BLK * (c + 1)], 8.0,
                                ropeT[:, 0:BLK], Alu.mult, Alu.mult)
                            eng.scalar_tensor_tensor(
                                t2b[:, :], pb[c], 8.0,
                                ropeT[:, BLK:2 * BLK], Alu.mult, Alu.mult)
                        else:
                            eng.tensor_mul(t1b[:, :], qpre[:, BLK * c:BLK * (c + 1)],
                                           ropeT[:, 0:BLK])
                            eng.tensor_mul(t2b[:, :], pb[c],
                                           ropeT[:, BLK:2 * BLK])
                        eng.drain()
                        eng.tensor_add(dst(c), t1b[:, :], t2b[:, :])
                    v_prev = p.comp("dve", fn, waits=[("te", t_sw), ("rope", d_rope),
                                                      ("dve", v_prev)])
                return v_prev, t_sw

            t_q = qkv_pass(scrQ, "QK", [("scal", v_xnfm)])
            v_qrot, t_sw = rope_apply(False,
                                      lambda c: qrot[:, BLK * c:BLK * (c + 1)],
                                      t_q, p.cnt["dve"])
            t_k = qkv_pass(scrK, "QK", [("dve", v_qrot)])
            v_krot, t_sw = rope_apply(True,
                                      lambda c, blk=blk: kfm[:, S * c + BLK * blk:
                                                             S * c + BLK * (blk + 1)],
                                      t_k, v_qrot)
            t_v = qkv_pass(scrV, "V", [("dve", v_krot)])

            v_pre = v_krot
            if blk == 0:
                def fn(eng):
                    for kt in range(NKT_SEQ):
                        for h in range(H):
                            eng.memset(vex[:, 1040 * kt + 65 * h + 64:
                                            1040 * kt + 65 * h + 65], 1.0)
                v_pre = p.comp("dve", fn, waits=[("dve", v_krot), ("te", t_v)])

            s_vex = {}
            for tt in (2, 3, 1, 0):
                def fn(eng, tt=tt, blk=blk):
                    kt = 4 * blk + tt
                    for hf in range(2):
                        for hh in range(8):
                            h = 8 * hf + hh
                            eng.copy(
                                vex[:, 1040 * kt + 65 * h:1040 * kt + 65 * h + 64],
                                pb[2 * tt + hf][:, 64 * hh:64 * (hh + 1)])
                s_vex[tt] = p.comp("scal", fn,
                                   waits=[("te", t_v)] if tt == 2 else ())
            s_vex_fin = s_vex[0]

            v_q8 = p.cnt["dve"]
            t_tr = p.cnt["te"]
            t_tr_prev = {0: 0, 1: 0}
            s_ev_par = {0: 0, 1: 0}
            for tt in range(NTT):
                for q8 in range(2):
                    par = (2 * tt + q8) & 1
                    dstb = hraw[:, 512:1024] if par == 0 else hraw[:, 0:512]

                    def fn(eng, tt=tt, q8=q8, dstb=dstb):
                        eng.tensor_scalar(scrA[:, :],
                                          xnf[:, D * tt + 512 * q8:D * tt + 512 * (q8 + 1)],
                                          stats[:, 12 + tt:13 + tt], MAGIC,
                                          Alu.mult, Alu.add)
                        eng.drain()
                        eng.tensor_scalar(dstb[:, :], scrA[:, :], -MAGIC, None, Alu.add)
                    v_q8 = p.comp("dve", fn, waits=[("dve", v_q8),
                                                    ("te", t_tr_prev[par])])

                    def fn(eng, par=par, dstb=dstb):
                        for t4 in range(4):
                            last = eng.transpose(pb[4 + par][:, 128 * t4:128 * (t4 + 1)],
                                                 dstb[:, 128 * t4:128 * (t4 + 1)], identf[:])
                        return last
                    t_tr = p.comp("te", fn, waits=[
                        ("dve", v_q8),
                        ("scal", s_vex[2] if tt == 0 else s_ev_par[par])])
                    t_tr_prev[par] = t_tr

                    def fn(eng, tt=tt, q8=q8, par=par):
                        for t4 in range(4):
                            icx = 4 * q8 + t4
                            eng.copy(
                                xnqfm[:, BLK * icx + 128 * tt:BLK * icx + 128 * (tt + 1)],
                                pb[4 + par][:, 128 * t4:128 * (t4 + 1)])
                    v_q8s = p.comp("scal", fn, waits=[("te", t_tr)])
                    s_ev_par[par] = v_q8s
            v_xq = v_q8s
            t_xnf_last = t_tr

            s_pv_prev = p.cnt["dve"]
            v_attn_fin = p.cnt["dve"]
            g_att_fin = 0
            bank_free = {}
            pt_ready = {}
            t_pv_hist = {}
            prev_last_exp = 0
            norm_state = {"g_r1": 0, "v_attn": p.cnt["dve"], "g_att": 0}

            def norm_emit(pc_n, s_sm_n, s_pv_n):
                st = norm_state
                g_r1 = p.dma("gp", rmap[:, :], invs[64:65, 0:1024],
                             waits=[("dve", s_sm_n)], sem="nrm")
                v_rc = p.comp("dve", lambda eng: eng.reciprocal(rmap[:, :], rmap[:, :]),
                              waits=[("nrm", g_r1)])
                g_w = p.dma("gp", scrInv, rmap[:, :], waits=[("dve", v_rc)], sem="nrm")
                g_bc = p.dma("gp", invs[0:64, 0:1024],
                             bass.AP(scrInv.tensor, 0, [[0, 64], [1, 1024]]),
                             waits=[("nrm", g_w)], sem="nrm")

                def fn(eng, pc_n=pc_n):
                    a = attn_fm[0:64, BLK * pc_n:BLK * (pc_n + 1)]
                    eng.tensor_mul(a, a, invs[0:64, 0:512])
                    eng.tensor_mul(stage[0:64, :], stage[0:64, :],
                                   invs[0:64, 512:1024])
                v_at = p.comp("dve", fn, waits=[("nrm", g_bc), ("dve", s_pv_n)])
                g_at = p.dma("gp", attn_fm[64:128, BLK * pc_n:BLK * (pc_n + 1)],
                             stage[0:64, :], waits=[("dve", v_at)], sem="att")
                st.update(g_r1=g_r1, v_attn=v_at, g_att=g_at)
                return v_at, g_at

            pend = None
            SCB = [(pb[0], pb[1]), (pb[2], pb[3]), (pb[6], pb[7])]
            SCD = [pbd[0], pbd[1], pbd[3]]
            for pc in range(NPAIR):
                hA, hB = 2 * pc, 2 * pc + 1
                ptAB = [ptb[:, 1024 * i:1024 * (i + 1)] for i in range(3)]
                ptA = [ptAB[i][:, 0:512] for i in range(3)]
                ptB = [ptAB[i][:, 512:1024] for i in range(3)]

                def sc_emit(kt):
                    par = kt % 3
                    sA, sB = SCB[par]
                    jd = kt - 4 * blk
                    band = jd >= 0
                    w0 = 128 * jd if band else 0

                    def fn(eng, pc=pc, kt=kt, sA=sA, sB=sB, band=band, w0=w0):
                        last = eng.matmul(sA,
                                   kfm[0:64, S * pc + 128 * kt:S * pc + 128 * (kt + 1)],
                                   qrot[0:64, BLK * pc:BLK * (pc + 1)],
                                   start=True, stop=not band, tile_position=(0, 0))
                        last = eng.matmul(sB,
                                   kfm[64:128, S * pc + 128 * kt:S * pc + 128 * (kt + 1)],
                                   qrot[64:128, BLK * pc:BLK * (pc + 1)],
                                   start=True, stop=not band, tile_position=(64, 0))
                        if band:
                            for s in (sA, sB):
                                last = eng.matmul(s[:, w0:w0 + 128], trineg[:],
                                                  identb[:], start=False,
                                                  stop=True)
                        return last
                    wl = []
                    if (pc, kt - 3) in bank_free:
                        wl.append(("scal", bank_free[(pc, kt - 3)]))
                    elif kt < 3:
                        wl.append(("scal", prev_last_exp))
                    if pc == 0 and kt <= 2:
                        wl.append(("scal", v_xq))
                        wl.append(("dve", v_pre))
                    t_sc = p.comp("te", fn, waits=wl)

                    def fn(eng, par=par, pc=pc, band=band, w0=w0):
                        pt, sd = ptAB[par], SCD[par]
                        if band and w0 > 0:
                            eng.activation(pt[:, 0:w0], sd[:, 0:w0],
                                           Act.Copy, scale=0.0)
                            eng.activation(pt[:, 512:512 + w0],
                                           sd[:, 512:512 + w0],
                                           Act.Copy, scale=0.0)
                            eng.activation(pt[:, w0:512], sd[:, w0:512],
                                           Act.Exp,
                                           bias=mh[:, 16 + pc:17 + pc],
                                           scale=1.0)
                            eng.activation(pt[:, 512 + w0:1024],
                                           sd[:, 512 + w0:1024], Act.Exp,
                                           bias=mh[:, 16 + pc:17 + pc],
                                           scale=1.0)
                        else:
                            eng.activation(pt[:, :], sd[:, :], Act.Exp,
                                           bias=mh[:, 16 + pc:17 + pc],
                                           scale=1.0)
                    wl = [("te", t_sc)]
                    if (pc, kt - 3) in t_pv_hist:
                        wl.append(("te", t_pv_hist[(pc, kt - 3)]))
                    s_exp = p.comp("scal", fn, waits=wl)
                    bank_free[(pc, kt)] = s_exp
                    pt_ready[(pc, kt)] = ("scal", s_exp)

                def pv_emit(kt):
                    par = kt % 3
                    wl = [pt_ready[(pc, kt)]]
                    if kt == 0:
                        wl.append(("dve", s_pv_prev))
                        if pc == 0:
                            wl.append(("dve", v_pre))
                            wl.append(("scal", s_vex_fin))

                    def fn(eng, kt=kt, par=par, hA=hA, hB=hB, nkt=nkt):
                        eng.matmul(pb[4][0:65, :],
                                   vex[:, 1040 * kt + 65 * hA:1040 * kt + 65 * (hA + 1)],
                                   ptA[par][:, :], start=(kt == 0), stop=(kt == nkt - 1))
                        return eng.matmul(pb[5][0:65, :],
                                   vex[:, 1040 * kt + 65 * hB:1040 * kt + 65 * (hB + 1)],
                                   ptB[par][:, :], start=(kt == 0), stop=(kt == nkt - 1))
                    t_pv = p.comp("te", fn, waits=wl)
                    t_pv_hist[(pc, kt)] = t_pv
                    return t_pv

                for kt in range(nkt):
                    sc_emit(kt)
                    if kt >= 2:
                        pv_emit(kt - 2)
                    if kt == 3 and pend is not None:
                        norm_emit(*pend)
                        pend = None
                pv_emit(nkt - 2)
                t_pv = pv_emit(nkt - 1)
                prev_last_exp = bank_free[(pc, nkt - 1)]

                wl_sm = [("te", t_pv), ("nrm", norm_state["g_r1"])]
                if pc == 0:
                    wl_sm.append(("outs", g_out))
                s_sm = p.comp("dve", lambda eng: eng.tensor_copy(
                    invs[64:65, 0:1024], pbd[2][64:65, 0:1024]), waits=wl_sm)

                def fn(eng, pc=pc):
                    eng.tensor_copy(attn_fm[0:64, BLK * pc:BLK * (pc + 1)],
                                    pb[4][0:64, :])
                    eng.tensor_copy(stage[0:64, :], pb[5][0:64, :])
                s_pv = p.comp("dve", fn,
                              waits=[("att", norm_state["g_att"]),
                                     ("dve", norm_state["v_attn"])])
                s_pv_prev = s_pv
                if pend is not None:
                    norm_emit(*pend)
                pend = (pc, s_sm, s_pv)
            v_attn_fin, g_att_fin = norm_emit(*pend)
            pend = None

            def mk_wo(fc, cb):
                def fn(eng, fc=fc, cb=cb):
                    for tt in range(NTT):
                        for oh in range(2):
                            last = eng.matmul(
                                pb[2 * tt + oh],
                                attn_fm[:, BLK * fc + 128 * tt:BLK * fc + 128 * (tt + 1)],
                                wslab[:, cb + 512 * oh:cb + 512 * (oh + 1)],
                                start=(fc == 0), stop=(fc == NIC - 1))
                    return last
                return fn
            t_wo = stream_pass(NIC, lambda i: scrWo[:, 1024 * i:1024 * (i + 1)],
                               mk_wo, [("att", g_att_fin), ("dve", v_attn_fin)])

            def fn(eng):
                for tt in range(NTT):
                    for oh in range(2):
                        dst = xblk[:, D * tt + 512 * oh:D * tt + 512 * (oh + 1)]
                        eng.tensor_add(dst, pb[2 * tt + oh], dst)
            v_acc = p.comp("dve", fn, waits=[("te", t_wo)])

            v_g_hist = {-2: v_acc, -1: v_acc}
            v_free_h = [0, 0]
            for hb in range(2):
                s_g = p.cnt["scal"]
                for pss in range(4):
                    base = 4 * (pss % 2)

                    def mk_f1(ic, cb, hb=hb, base=base):
                        def fn(eng, ic=ic, cb=cb, hb=hb, base=base):
                            for tti in range(2):
                                tt = 2 * hb + tti
                                for fbh in range(2):
                                    last = eng.matmul(
                                        pb[base + 2 * tti + fbh],
                                        xnqfm[:, BLK * ic + 128 * tt:BLK * ic + 128 * (tt + 1)],
                                        wslab[:, cb + 512 * fbh:cb + 512 * (fbh + 1)],
                                        start=(ic == 0), stop=(ic == NIC - 1))
                            return last
                        return fn
                    extra = [("dve", v_g_hist[pss - 2])]
                    if pss == 0 and hb == 0:
                        extra.append(("scal", v_xq))
                    t_f1 = stream_pass(
                        NIC,
                        lambda i, pss=pss: scrW1[:, 1024 * (NIC * pss + i):
                                                 1024 * (NIC * pss + i + 1)],
                        mk_f1, extra)

                    v_g = v_g_hist[pss - 1]
                    for tti in range(2):
                        tt = 2 * hb + tti
                        for fbh in range(2):
                            fb = 2 * pss + fbh

                            def fn(eng, tti=tti, fbh=fbh, tt=tt, fb=fb, base=base):
                                eng.scalar_tensor_tensor(
                                    hraw[:, 512 * tti:512 * (tti + 1)],
                                    pb[base + 2 * tti + fbh], stats[:, 16 + tt:17 + tt],
                                    b1b[:, 512 * fb:512 * (fb + 1)], Alu.mult, Alu.add)
                            v_g = p.comp("dve", fn, waits=[("te", t_f1), ("scal", s_g)])

                            def fn(eng, tti=tti, fb=fb):
                                eng.activation(
                                    gbuf[:, FF * tti + 512 * fb:FF * tti + 512 * (fb + 1)],
                                    hraw[:, 512 * tti:512 * (tti + 1)], Act.Gelu)
                            s_g = p.comp("scal", fn, waits=[("dve", v_g)])
                    v_g_hist[pss] = v_g

                v_hq = p.cnt["dve"]
                s_hq = p.cnt["scal"]
                t_tr2_prev = {0: 0, 1: 0}
                for tti in range(2):
                    gt_ = gbuf[:, FF * tti:FF * (tti + 1)]

                    def fn(eng, gt_=gt_, tti=tti):
                        eng.reduce_max(stats[:, 20 + tti:21 + tti], gt_, axis=AX.X,
                                       apply_absolute_value=True)
                        eng.drain()
                        eng.tensor_scalar(stats[:, 20 + tti:21 + tti],
                                          stats[:, 20 + tti:21 + tti], EPS_BIT, None,
                                          Alu.max)
                        eng.drain()
                        eng.reciprocal(stats[:, 30:31], stats[:, 20 + tti:21 + tti])
                        eng.drain()
                        eng.tensor_scalar(stats[:, 22 + tti:23 + tti],
                                          stats[:, 30:31], 127.0, None, Alu.mult)
                        eng.tensor_mul(stats[:, 24 + tti:25 + tti],
                                       stats[:, 20 + tti:21 + tti], wsb[:, 5:6])
                    v_hq = p.comp("dve", fn, waits=[("scal", s_g), ("dve", v_hq)])

                t_f2 = 0
                for c8 in range(8):
                    for tti in range(2):
                        gt_ = gbuf[:, FF * tti:FF * (tti + 1)]
                        par = (2 * c8 + tti) & 1
                        dstb = hraw[:, 512:1024] if par == 0 else hraw[:, 0:512]

                        def fn(eng, gt_=gt_, c8=c8, tti=tti, dstb=dstb):
                            eng.tensor_scalar(scrA[:, :], gt_[:, 512 * c8:512 * (c8 + 1)],
                                              stats[:, 22 + tti:23 + tti], MAGIC,
                                              Alu.mult, Alu.add)
                            eng.drain()
                            eng.tensor_scalar(dstb[:, :], scrA[:, :], -MAGIC, None,
                                              Alu.add)
                        v_hq = p.comp("dve", fn, waits=[("dve", v_hq),
                                                        ("te", t_tr2_prev[par])])

                        def fn(eng, par=par, dstb=dstb):
                            for t4 in range(4):
                                last = eng.transpose(pb[6 + par][:, 128 * t4:128 * (t4 + 1)],
                                                     dstb[:, 128 * t4:128 * (t4 + 1)],
                                                     identf[:])
                            return last
                        t_tr2 = p.comp("te", fn, waits=[("dve", v_hq)])
                        t_tr2_prev[par] = t_tr2

                        def fn(eng, c8=c8, tti=tti, par=par):
                            for t4 in range(4):
                                hc = 4 * c8 + t4
                                eng.copy(
                                    hq_fm[:, 256 * hc + 128 * tti:256 * hc + 128 * (tti + 1)],
                                    pb[6 + par][:, 128 * t4:128 * (t4 + 1)])
                        s_hq = p.comp("scal", fn, waits=[("te", t_tr2)])

                    for j in range(4):
                        hc = 4 * c8 + j
                        sl, sname, d = slab_load(hc, scrW2[:, 1024 * hc:1024 * (hc + 1)])
                        cb = 1024 * sl

                        def fn(eng, hc=hc, cb=cb):
                            for tti in range(2):
                                for dh in range(2):
                                    last = eng.matmul(
                                        pb[2 * tti + dh],
                                        hq_fm[:, 256 * hc + 128 * tti:256 * hc + 128 * (tti + 1)],
                                        wslab[:, cb + 512 * dh:cb + 512 * (dh + 1)],
                                        start=(hc == 0), stop=(hc == NHC - 1))
                            return last
                        wl = [(sname, d), ("scal", s_hq)]
                        if hc == 0:
                            wl.append(("dve", v_g_hist[2]))
                        slab_hist[sl] = p.comp("te", fn, waits=wl)
                        t_f2 = slab_hist[sl]

                def fn(eng, hb=hb):
                    for tti in range(2):
                        tt = 2 * hb + tti
                        ob = outtA if tti == 0 else outtB
                        for dh in range(2):
                            eng.scalar_tensor_tensor(
                                scrA[:, :], pb[2 * tti + dh],
                                stats[:, 24 + tti:25 + tti],
                                b2b[:, 512 * dh:512 * (dh + 1)], Alu.mult, Alu.add)
                            eng.drain()
                            eng.tensor_add(
                                ob[:, 512 * dh:512 * (dh + 1)],
                                scrA[:, :],
                                xblk[:, D * tt + 512 * dh:D * tt + 512 * (dh + 1)])
                            eng.drain()
                v_free = p.comp("dve", fn, waits=[("te", t_f2), ("outs", g_out)])
                v_free_h[hb] = v_free
                v_g_hist[2] = v_free
                v_g_hist[-2] = v_free
                v_g_hist[-1] = v_free

                for tti in range(2):
                    tt = 2 * hb + tti
                    ob = outtA if tti == 0 else outtB
                    g_out = p.dma("gp", out_h[r0 + 128 * tt:r0 + 128 * (tt + 1), :],
                                  ob[:, 0:D],
                                  waits=[("dve", v_free)] if tti == 0 else (),
                                  sem="outs")
                    if hb == 1 and tti == 0:
                        g_out_ropefree = g_out

            out_final = g_out
            xhalf_free = [("dve", v_free_h[0]), ("dve", v_free_h[1])]
            rope_free = ("dve", v_krot)

    p.step("gp", lambda eng: None, waits=[("outs", out_final)])

    with nc.Block() as block:
        @block.sync
        def _(eng):
            p.run("sync", eng)

        @block.gpsimd
        def _(eng):
            p.run("gp", eng)

        @block.vector
        def _(eng):
            p.run("dve", eng)

        @block.scalar
        def _(eng):
            p.run("scal", eng)

        @block.tensor
        def _(eng):
            p.run("te", eng)

    stack.close()
    return nc


_NC_CACHE = None


def kernel(L0_attn_x, L0_attn_W_qkv, L0_attn_W_o, L0_ffn_W1, L0_ffn_b1,
           L0_ffn_W2, L0_ffn_b2):
    global _NC_CACHE
    x = np.ascontiguousarray(np.asarray(L0_attn_x, dtype=np.float32))
    wqkvT = np.ascontiguousarray(np.asarray(L0_attn_W_qkv, dtype=np.float32).T)
    woT = np.ascontiguousarray(np.asarray(L0_attn_W_o, dtype=np.float32).T)
    w1T = np.ascontiguousarray(np.asarray(L0_ffn_W1, dtype=np.float32).T)
    w2T = np.ascontiguousarray(np.asarray(L0_ffn_W2, dtype=np.float32).T)
    b1 = np.ascontiguousarray(np.asarray(L0_ffn_b1, dtype=np.float32).reshape(1, FF))
    b2 = np.ascontiguousarray(np.asarray(L0_ffn_b2, dtype=np.float32).reshape(1, D))
    consts, identf, tri01, swapT, identb, trineg = _host_consts()

    if _NC_CACHE is None:
        _NC_CACHE = build_program()
    nc = _NC_CACHE

    common = {
        "wqkvT": wqkvT, "woT": woT, "w1T": w1T, "w2T": w2T,
        "b1": b1, "b2": b2, "identf": identf, "tri01": tri01, "swapT": swapT,
        "identb": identb, "trineg": trineg,
        **consts,
    }
    in_maps = []
    for c in range(8):
        shard = np.ascontiguousarray(x[2 * c:2 * c + 2].reshape(T, D))
        m = dict(common)
        m["x"] = shard
        in_maps.append(m)

    res = run_bass_kernel_spmd(nc, in_maps, core_ids=list(range(8)))
    outs = [res.results[c]["out"].reshape(2, S, D) for c in range(8)]
    return np.ascontiguousarray(np.concatenate(outs, axis=0))


# version 19
# speedup vs baseline: 1.0215x; 1.0215x over previous
import contextlib

import numpy as np
import ml_dtypes
import concourse.bass as bass
import concourse.mybir as mybir
from concourse.bass_utils import run_bass_kernel_spmd

F32 = mybir.dt.float32
BF16 = mybir.dt.bfloat16
Alu = mybir.AluOpType
Act = mybir.ActivationFunctionType
AX = mybir.AxisListType
bfl = ml_dtypes.bfloat16

B, S, D = 16, 2048, 1024
H, DK = 16, 64
FF = 4096
T = 2 * S
BLK = 512
NBLK = S // BLK
NTT = BLK // 128
NIC = D // 128
NPAIR = H // 2
NKT_SEQ = S // 128
NHC = FF // 128
MAGIC = float(np.float32(1.5 * 2.0 ** 23))
INV_D = 1.0 / D
INV_NW = 1.0 / (FF * D)
EPS_LN = 1e-5
EPS_BIT = 1e-5


class Prog:
    ENGS = ("sync", "gp", "dve", "scal", "te")
    DSEMS = ("slab0", "slab1", "slab2", "slab3", "slab4", "slab5", "xh0",
             "xh1", "rope", "nrm", "att", "outs", "sto", "misc")

    def __init__(self, nc, stack):
        self.nc = nc
        self.steps = {e: [] for e in self.ENGS}
        self.sems = {}
        self.cnt = {}
        for e in self.ENGS + self.DSEMS:
            self.sems[e] = stack.enter_context(nc.semaphore("sem_" + e))
            self.cnt[e] = 0

    def step(self, engine, fn, waits=()):
        self.steps[engine].append((tuple(waits), fn))

    def run(self, engine_name, eng):
        for waits, fn in self.steps[engine_name]:
            for (sname, val) in waits:
                if val > 0:
                    eng.wait_ge(self.sems[sname], val)
            fn(eng)

    def dma(self, q, out_ap, in_ap, waits=(), sem=None):
        sname = sem if sem is not None else q
        semh = self.sems[sname]

        def fn(eng, o=out_ap, i=in_ap):
            eng.dma_start(out=o, in_=i).then_inc(semh, 16)

        self.step(q, fn, waits)
        self.cnt[sname] += 16
        return self.cnt[sname]

    def comp(self, engine, fn, waits=()):
        sem = self.sems[engine]

        def wrapped(eng, fn=fn):
            last = fn(eng)
            if last is None or not hasattr(last, "then_inc"):
                last = eng.drain()
            last.then_inc(sem, 1)

        self.step(engine, wrapped, waits)
        self.cnt[engine] += 1
        return self.cnt[engine]


def _host_consts():
    inv_freq = np.float32(10000.0) ** (
        np.arange(0, DK, 2, dtype=np.float32) / np.float32(DK))
    ang = np.arange(S, dtype=np.float32)[:, None] / inv_freq[None, :]
    cos = np.cos(ang).astype(np.float32).T
    sin = np.sin(ang).astype(np.float32).T
    cosF = np.empty((DK, S), np.float32)
    sinF = np.empty((DK, S), np.float32)
    cosF[0::2] = cos
    cosF[1::2] = cos
    sinF[0::2] = -sin
    sinF[1::2] = sin
    cosP = np.ascontiguousarray(np.concatenate([cosF, cosF], axis=0))
    sinP = np.ascontiguousarray(np.concatenate([sinF, sinF], axis=0))
    eighth = np.float32(0.125)
    consts = {
        "ropeQc": cosP * eighth, "ropeQs": sinP * eighth,
        "ropeKc": cosP, "ropeKs": sinP,
    }
    identf = np.eye(128, dtype=np.float32)
    identb = np.eye(128, dtype=bfl)
    kl = np.arange(128)[:, None]
    ql = np.arange(128)[None, :]
    tri01 = (kl <= ql).astype(bfl)
    trineg = np.where(ql > kl, np.float32(-1e30), np.float32(0.0)).astype(bfl)
    swapT = np.zeros((128, 128), dtype=bfl)
    for dp in range(128):
        swapT[dp ^ 1, dp] = 1.0
    return consts, identf, tri01, swapT, identb, trineg


def build_program():
    nc = bass.Bass("TRN2", target_bir_lowering=False)
    stack = contextlib.ExitStack()
    stack.__enter__()

    x_h = nc.dram_tensor("x", [T, D], F32, kind="ExternalInput").ap()
    out_h = nc.dram_tensor("out", [T, D], F32, kind="ExternalOutput").ap()
    wqkvT_h = nc.dram_tensor("wqkvT", [D, 3 * D], F32, kind="ExternalInput").ap()
    woT_h = nc.dram_tensor("woT", [D, D], F32, kind="ExternalInput").ap()
    w1T_h = nc.dram_tensor("w1T", [D, FF], F32, kind="ExternalInput").ap()
    w2T_h = nc.dram_tensor("w2T", [FF, D], F32, kind="ExternalInput").ap()
    b1_h = nc.dram_tensor("b1", [1, FF], F32, kind="ExternalInput").ap()
    b2_h = nc.dram_tensor("b2", [1, D], F32, kind="ExternalInput").ap()
    identf_h = nc.dram_tensor("identf", [128, 128], F32, kind="ExternalInput").ap()
    identb_h = nc.dram_tensor("identb", [128, 128], BF16, kind="ExternalInput").ap()
    trineg_h = nc.dram_tensor("trineg", [128, 128], BF16, kind="ExternalInput").ap()
    tri_h = nc.dram_tensor("tri01", [128, 128], BF16, kind="ExternalInput").ap()
    swap_h = nc.dram_tensor("swapT", [128, 128], BF16, kind="ExternalInput").ap()
    rope_h = {k: nc.dram_tensor(k, [128, S], F32, kind="ExternalInput").ap()
              for k in ("ropeQc", "ropeQs", "ropeKc", "ropeKs")}

    scrQ = nc.dram_tensor("scrQ", [128, NIC * 1024], BF16, kind="Internal").ap()
    scrK = nc.dram_tensor("scrK", [128, NIC * 1024], BF16, kind="Internal").ap()
    scrV = nc.dram_tensor("scrV", [128, NIC * 1024], BF16, kind="Internal").ap()
    scrWo = nc.dram_tensor("scrWo", [128, NIC * 1024], BF16, kind="Internal").ap()
    scrW1 = nc.dram_tensor("scrW1", [128, 4 * NIC * 1024], BF16, kind="Internal").ap()
    scrW2 = nc.dram_tensor("scrW2", [128, NHC * 1024], BF16, kind="Internal").ap()
    scrInv = nc.dram_tensor("scrInv", [1, 1024], F32, kind="Internal").ap()

    sbt = lambda name, shape, dt: stack.enter_context(nc.sbuf_tensor(name, shape, dt))
    pst = lambda name, shape: stack.enter_context(nc.psum_tensor(name, shape, F32))

    identf = sbt("identf_s", [128, 128], F32)
    identb = sbt("identb_s", [128, 128], BF16)
    trineg = sbt("trineg_s", [128, 128], BF16)
    swapT = sbt("swapT_s", [128, 128], BF16)
    ones_r = sbt("ones_r", [1, 128], F32)
    ones_c = sbt("ones_c", [128, 1], F32)
    b1b = sbt("b1b", [128, FF], F32)
    b2b = sbt("b2b", [128, D], F32)
    mh = sbt("mh", [128, 24], F32)
    wsb = sbt("wsb", [128, 8], F32)
    stats = sbt("stats", [128, 32], F32)
    wslab = sbt("wslab", [128, 6 * 1024], BF16)
    ropeT = sbt("ropeT", [128, 2 * BLK], F32)
    outtA = ropeT

    kfm = sbt("kfm", [128, NPAIR * S], BF16)
    vex = sbt("vex", [128, NKT_SEQ * H * 65], BF16)
    xblk = sbt("xblk", [128, NTT * D], F32)
    gx = sbt("gx", [128, 2 * FF], F32)
    xnf = gx[:, 0:4096]
    gbuf = gx
    xnqfm = sbt("xnqfm", [128, NIC * BLK], BF16)
    qrot = sbt("qrot", [128, NIC * BLK], BF16)
    qpre = sbt("qpre", [128, NIC * BLK], BF16)
    attn_fm = qpre
    ptb = sbt("ptb", [128, 6 * BLK], BF16)
    stage = sbt("stage", [128, BLK], BF16)
    invs = sbt("invs", [128, 2 * BLK], F32)
    outtB = invs
    scrA = sbt("scrA", [128, 512], F32)
    hraw = sbt("hraw", [128, 2 * 512], F32)
    t1b = hraw[:, 0:512]
    t2b = hraw[:, 512:1024]
    hq_fm = sbt("hq_fm", [128, NHC * 256], BF16)
    ternb = hq_fm[:, 0:4096]
    sums = hraw[0:1, 0:1024]
    rmap = invs[0:64, 1008:1024]

    pbd = [pst("pbd%d" % i, [128, 1024]) for i in range(4)]
    pb = [pbd[i // 2][:, 512 * (i % 2):512 * (i % 2) + 512] for i in range(8)]

    p = Prog(nc, stack)

    d_c = p.dma("sync", identf[:], identf_h, sem="misc")
    d_c = p.dma("sync", swapT[:], swap_h, sem="misc")
    d_c = p.dma("sync", identb[:], identb_h, sem="misc")
    d_c = p.dma("sync", trineg[:], trineg_h, sem="misc")

    def fn(eng):
        eng.memset(ones_r[:], 1.0)
        eng.memset(ones_c[:], 1.0)
        eng.memset(stats[:], 0.0)
        eng.drain()
        eng.memset(stats[:, 31:32], EPS_LN)
    v_ones = p.comp("dve", fn, waits=[("misc", d_c)])

    d_b1 = p.dma("sync", gx[0:1, 0:FF], b1_h, sem="misc")
    d_b2 = p.dma("sync", gx[0:1, 4096:4096 + D], b2_h, sem="misc")

    def fn(eng):
        for seg in range(8):
            last = eng.matmul(pb[seg], ones_r[0:1, 0:128],
                              gx[0:1, 512 * seg:512 * (seg + 1)], start=True, stop=True)
        return last
    t_w = p.comp("te", fn, waits=[("misc", d_b2), ("dve", v_ones)])

    def fn(eng):
        for seg in range(8):
            eng.copy(b1b[:, 512 * seg:512 * (seg + 1)], pb[seg])
    s_w = p.comp("scal", fn, waits=[("te", t_w)])

    def fn(eng):
        for seg in range(2):
            last = eng.matmul(pb[seg], ones_r[0:1, 0:128],
                              gx[0:1, 4096 + 512 * seg:4096 + 512 * (seg + 1)],
                              start=True, stop=True)
        return last
    t_w = p.comp("te", fn, waits=[("misc", d_b2), ("scal", s_w)])

    def fn(eng):
        for seg in range(2):
            eng.copy(b2b[:, 512 * seg:512 * (seg + 1)], pb[seg])
    s_w = p.comp("scal", fn, waits=[("te", t_w)])

    def big_load(q, buf_ap, src_ap, width, waits, sem):
        d = 0
        wq = width // 4
        for a in range(4):
            d = p.dma(q, buf_ap[:, wq * a:wq * (a + 1)],
                      src_ap[:, wq * a:wq * (a + 1)],
                      waits=waits if a == 0 else (), sem=sem)
        return d

    slabs = [xblk[:, 0:4096], gx[:, 4096:8192]]
    slab_free = [("dve", 0), ("te", t_w)]
    tern_free = ("sto", 0)

    s_sq = s_w
    t_norm = 0
    for ic in range(NIC):
        bi = ic % 2
        buf = slabs[bi][:, 0:3072]
        sname = "slab%d" % bi
        d_s = big_load("sync", buf, wqkvT_h[128 * ic:128 * (ic + 1), :], 3072,
                       [slab_free[bi]], sname)
        v_c = p.comp("dve", lambda eng, b=buf: eng.tensor_copy(ternb[:, 0:3072], b),
                     waits=[(sname, d_s), tern_free])
        g_l = p.dma("gp", scrQ[:, 1024 * ic:1024 * (ic + 1)], ternb[:, 0:1024], waits=[("dve", v_c)], sem="sto")
        g_l = p.dma("gp", scrK[:, 1024 * ic:1024 * (ic + 1)], ternb[:, 1024:2048], sem="sto")
        g_l = p.dma("gp", scrV[:, 1024 * ic:1024 * (ic + 1)], ternb[:, 2048:3072], sem="sto")
        tern_free = ("sto", g_l)
        s_sq = p.comp("scal", lambda eng, b=buf: eng.activation(
            gx[:, 0:2048], b[:, 0:2048], Act.Square),
            waits=[(sname, d_s), ("scal", s_sq), ("te", t_norm)])

        def fn(eng, ic=ic):
            for seg in range(4):
                last = eng.matmul(pb[2 + seg][0:1, :], ones_c[:, 0:1],
                                  gx[:, 512 * seg:512 * (seg + 1)],
                                  start=(ic == 0), stop=(ic == NIC - 1))
            return last
        t_norm = p.comp("te", fn, waits=[("scal", s_sq)])
        slab_free[bi] = ("scal", s_sq)

    def fn(eng):
        for h in range(16):
            eng.reduce_sum(sums[0:1, h:h + 1],
                           pb[2 + (64 * h) // 512][0:1, (64 * h) % 512:(64 * h) % 512 + 64],
                           axis=AX.X)
        for h in range(16):
            c = 1024 + 64 * h
            eng.reduce_sum(sums[0:1, 16 + h:17 + h],
                           pb[2 + c // 512][0:1, c % 512:c % 512 + 64], axis=AX.X)
        eng.drain()
        eng.tensor_mul(sums[0:1, 32:48], sums[0:1, 0:16], sums[0:1, 16:32])
    v_fro = p.comp("dve", fn, waits=[("te", t_norm)])
    s_fr = p.comp("scal", lambda eng: eng.activation(sums[0:1, 48:64],
                                                     sums[0:1, 32:48], Act.Sqrt),
                  waits=[("dve", v_fro)])
    v_nm = p.comp("dve", lambda eng: eng.tensor_scalar(
        sums[0:1, 64:80], sums[0:1, 48:64], -1.02 * 128.0, None, Alu.mult),
        waits=[("scal", s_fr)])
    t_mh = p.comp("te", lambda eng: eng.matmul(pb[6][:, 0:16], ones_r[0:1, 0:128],
                                               sums[0:1, 64:80], start=True, stop=True),
                  waits=[("dve", v_nm)])
    s_mh = p.comp("scal", lambda eng: eng.copy(mh[:, 0:16], pb[6][:, 0:16]),
                  waits=[("te", t_mh)])
    def fn(eng):
        for c in range(8):
            eng.tensor_tensor(mh[:, 16 + c:17 + c], mh[:, 2 * c:2 * c + 1],
                              mh[:, 2 * c + 1:2 * c + 2], Alu.min)
    s_mh = p.comp("dve", fn, waits=[("scal", s_mh)])

    for fc in range(NIC):
        bi = fc % 2
        buf = slabs[bi][:, 0:1024]
        sname = "slab%d" % bi
        d_s = big_load("sync", buf, woT_h[128 * fc:128 * (fc + 1), :], 1024,
                       [slab_free[bi]], sname)
        v_c = p.comp("dve", lambda eng, b=buf: eng.tensor_copy(ternb[:, 0:1024], b),
                     waits=[(sname, d_s), tern_free])
        g_l = p.dma("gp", scrWo[:, 1024 * fc:1024 * (fc + 1)], ternb[:, 0:1024], waits=[("dve", v_c)], sem="sto")
        tern_free = ("sto", g_l)
        slab_free[bi] = ("dve", v_c)

    v_a = p.comp("dve", lambda eng: eng.memset(stats[:, 28:31], 0.0),
                 waits=[("scal", s_mh)])
    slabs3 = [xblk[:, 0:4096], gx[:, 4096:8192], gx[:, 0:4096]]
    free3 = [slab_free[0], slab_free[1], ("scal", s_mh)]
    for i in range(NIC + NHC):
        w2 = i >= NIC
        width = 1024 if w2 else 4096
        src_ap = (w2T_h[128 * (i - NIC):128 * (i - NIC + 1), :] if w2
                  else w1T_h[128 * i:128 * (i + 1), :])
        bi = i % 3
        buf = slabs3[bi][:, 0:width]
        q = "sync" if i % 2 == 0 else "gp"
        sname = "slab%d" % bi
        d_s = big_load(q, buf, src_ap, width, [free3[bi]], sname)

        def fn(eng, b=buf, w2=w2):
            eng.reduce_sum(stats[:, 30:31], b, axis=AX.X, apply_absolute_value=True)
            eng.drain()
            acc = stats[:, 29:30] if w2 else stats[:, 28:29]
            eng.tensor_add(acc, acc, stats[:, 30:31])
        v_a = p.comp("dve", fn, waits=[(sname, d_s), ("dve", v_a)])
        free3[bi] = ("dve", v_a)
    slab_free[0] = free3[0]
    slab_free[1] = free3[1]
    gx_lo_free = free3[2]

    t_w = p.comp("te", lambda eng: (
        eng.matmul(pb[0][0:1, 0:1], ones_c[:, 0:1], stats[:, 28:29], start=True, stop=True),
        eng.matmul(pb[0][0:1, 1:2], ones_c[:, 0:1], stats[:, 29:30], start=True, stop=True))[-1],
        waits=[("dve", v_a)])
    s_w = p.comp("scal", lambda eng: eng.copy(sums[0:1, 100:102], pb[0][0:1, 0:2]),
                 waits=[("te", t_w)])

    def fn(eng):
        eng.tensor_scalar(sums[0:1, 102:104], sums[0:1, 100:102], INV_NW, EPS_BIT,
                          Alu.mult, Alu.max)
        eng.drain()
        eng.tensor_scalar(sums[0:1, 104:105], sums[0:1, 102:103], 0.5, None, Alu.mult)
        eng.tensor_scalar(sums[0:1, 105:106], sums[0:1, 102:103], -0.5, None, Alu.mult)
        eng.tensor_scalar(sums[0:1, 106:107], sums[0:1, 102:103], 1.0 / 127.0, None, Alu.mult)
        eng.tensor_scalar(sums[0:1, 107:108], sums[0:1, 103:104], 0.5, None, Alu.mult)
        eng.tensor_scalar(sums[0:1, 108:109], sums[0:1, 103:104], -0.5, None, Alu.mult)
        eng.tensor_scalar(sums[0:1, 109:110], sums[0:1, 103:104], 1.0 / 127.0, None, Alu.mult)
    v_ws = p.comp("dve", fn, waits=[("scal", s_w)])
    t_w = p.comp("te", lambda eng: eng.matmul(pb[1][:, 0:6], ones_r[0:1, 0:128],
                                              sums[0:1, 104:110], start=True, stop=True),
                 waits=[("dve", v_ws)])
    s_wsb = p.comp("scal", lambda eng: eng.copy(wsb[:, 0:6], pb[1][:, 0:6]),
                   waits=[("te", t_w)])

    for ic in range(NIC):
        bi = ic % 2
        buf = slabs[bi][:, 0:4096]
        sname = "slab%d" % bi
        d_s = big_load("sync" if ic % 2 == 0 else "gp", buf,
                       w1T_h[128 * ic:128 * (ic + 1), :], 4096,
                       [slab_free[bi], ("scal", s_wsb)] + ([gx_lo_free] if ic == 0 else []),
                       sname)

        def fn(eng, b=buf):
            eng.tensor_single_scalar(gx[:, 0:4096], b, wsb[:, 0:1], Alu.is_gt)
            eng.drain()
            eng.tensor_single_scalar(b, b, wsb[:, 1:2], Alu.is_lt)
            eng.drain()
            eng.tensor_sub(ternb[:, 0:4096], gx[:, 0:4096], b)
        v_t = p.comp("dve", fn, waits=[(sname, d_s), tern_free])
        g_l = 0
        for q in range(4):
            g_l = p.dma("gp",
                        scrW1[:, 1024 * (NIC * q + ic):1024 * (NIC * q + ic) + 1024],
                        ternb[:, 1024 * q:1024 * (q + 1)],
                        waits=[("dve", v_t)] if q == 0 else (), sem="sto")
        tern_free = ("sto", g_l)
        slab_free[bi] = ("dve", v_t)
    for hc in range(NHC):
        bi = hc % 2
        buf = slabs[bi][:, 0:1024]
        sname = "slab%d" % bi
        d_s = big_load("sync" if hc % 2 == 0 else "gp", buf,
                       w2T_h[128 * hc:128 * (hc + 1), :], 1024,
                       [slab_free[bi]], sname)

        def fn(eng, b=buf):
            eng.tensor_single_scalar(gx[:, 0:1024], b, wsb[:, 3:4], Alu.is_gt)
            eng.drain()
            eng.tensor_single_scalar(b, b, wsb[:, 4:5], Alu.is_lt)
            eng.drain()
            eng.tensor_sub(ternb[:, 0:1024], gx[:, 0:1024], b)
        v_t = p.comp("dve", fn, waits=[(sname, d_s), tern_free])
        g_l = p.dma("gp", scrW2[:, 1024 * hc:1024 * (hc + 1)], ternb[:, 0:1024], waits=[("dve", v_t)], sem="sto")
        tern_free = ("sto", g_l)
        slab_free[bi] = ("dve", v_t)

    prep_end = [("sto", p.cnt["sto"]), ("dve", p.cnt["dve"]),
                ("scal", p.cnt["scal"]), ("te", p.cnt["te"])]
    first_stream_extra = [("sto", p.cnt["sto"])]

    slab_hist = {0: 0, 1: 0, 2: 0, 3: 0, 4: 0, 5: 0}

    def slab_load(i, region_ap, extra=()):
        sl = i % 6
        sname = "slab%d" % sl
        dwl = [("te", slab_hist[sl])] + list(extra)
        d = p.dma("sync", wslab[:, 1024 * sl:1024 * (sl + 1)], region_ap,
                  waits=dwl, sem=sname)
        return sl, sname, d

    def stream_pass(n_slabs, region_fn, mm_fn, ic0_extra):
        t = 0
        for i in range(n_slabs):
            extra = first_stream_extra if (first_stream_extra and i < 6) else ()
            sl, sname, d = slab_load(i, region_fn(i), extra)
            wl = [(sname, d)]
            if i == 0:
                wl += ic0_extra
            t = p.comp("te", mm_fn(i, 1024 * sl), waits=wl)
            slab_hist[sl] = t
        if first_stream_extra:
            first_stream_extra.clear()
        return t

    xhalf_free = [("dve", 0), ("dve", 0)]
    rope_free = ("dve", 0)
    g_out = 0
    g_out_ropefree = 0
    x_pre = None
    out_final = 0
    pend_norm = [None]

    for seq in range(2):
        for blk in range(NBLK):
            r0 = S * seq + BLK * blk
            nkt = 4 * (blk + 1)

            def x_tile_load(tt, r0t, wx):
                g = 0
                for ps in range(2):
                    for cs in range(2):
                        g = p.dma(
                            "gp",
                            xblk[64 * ps:64 * (ps + 1),
                                 D * tt + 512 * cs:D * tt + 512 * (cs + 1)],
                            x_h[r0t + 128 * tt + 64 * ps:
                                r0t + 128 * tt + 64 * (ps + 1),
                                512 * cs:512 * (cs + 1)],
                            waits=wx if (ps == 0 and cs == 0) else (),
                            sem="xh%d" % (tt // 2))
                return g

            if x_pre is None:
                wx = list(prep_end) if (seq == 0 and blk == 0) else [xhalf_free[0]]
                x_tile_load(0, r0, wx)
                gh0 = x_tile_load(1, r0, ())
            else:
                gh0 = x_pre
            x_tile_load(2, r0, [xhalf_free[1]])
            gh1 = x_tile_load(3, r0, ())
            g_x = [gh0, gh0, gh1, gh1]
            d_rope = 0
            for i, k in enumerate(("ropeQc", "ropeQs")):
                d_rope = p.dma("sync", ropeT[:, BLK * i:BLK * (i + 1)],
                               rope_h[k][:, BLK * blk:BLK * (blk + 1)],
                               waits=[rope_free, ("outs", g_out_ropefree)] if i == 0 else (),
                               sem="rope")

            s_sq = [0] * NTT
            v_var = [0] * NTT
            s_nrm = [0] * NTT
            t_tr = p.cnt["te"]
            s_ev = {0: p.cnt["scal"], 1: p.cnt["scal"]}

            def emit_sq(tt):
                xt = xblk[:, D * tt:D * (tt + 1)]
                s_sq[tt] = p.comp("scal", lambda eng, xt=xt, tt=tt: eng.activation(
                    gx[:, 4096:5120], xt, Act.Square,
                    accum_out=stats[:, 26 + tt:27 + tt]),
                    waits=[("xh%d" % (tt // 2), g_x[tt])])

            def emit_ln(tt):
                xt = xblk[:, D * tt:D * (tt + 1)]
                xnt = xnf[:, D * tt:D * (tt + 1)]

                def fn(eng, xt=xt, tt=tt):
                    eng.reduce_sum(stats[:, 30:31], xt, axis=AX.X)
                    eng.drain()
                    eng.tensor_scalar(stats[:, tt:tt + 1], stats[:, 30:31],
                                      -INV_D, None, Alu.mult)
                    eng.drain()
                    eng.tensor_mul(stats[:, 30:31], stats[:, tt:tt + 1],
                                   stats[:, tt:tt + 1])
                    eng.drain()
                    eng.scalar_tensor_tensor(stats[:, 26 + tt:27 + tt],
                                             stats[:, 26 + tt:27 + tt], INV_D,
                                             stats[:, 30:31], Alu.mult, Alu.subtract)
                v_var[tt] = p.comp("dve", fn, waits=[("xh%d" % (tt // 2), g_x[tt]),
                                                     ("scal", s_sq[tt])])

                def fn(eng, xt=xt, xnt=xnt, tt=tt):
                    eng.activation(wsb[:, 6:7], stats[:, 26 + tt:27 + tt], Act.Ln,
                                   bias=stats[:, 31:32])
                    eng.drain()
                    eng.activation(stats[:, 4 + tt:5 + tt], wsb[:, 6:7],
                                   Act.Exp, scale=-0.5)
                    eng.drain()
                    eng.activation(wsb[:, 7:8], stats[:, tt:tt + 1], Act.Identity,
                                   scale=stats[:, 4 + tt:5 + tt])
                    eng.drain()
                    eng.activation(xnt, xt, Act.Identity, bias=wsb[:, 7:8],
                                   scale=stats[:, 4 + tt:5 + tt])
                s_nrm[tt] = p.comp("scal", fn, waits=[("dve", v_var[tt])])

            def emit_tr(tt, t_tr):
                xnt = xnf[:, D * tt:D * (tt + 1)]
                for g2 in range(2):
                    def fn(eng, xnt=xnt, g2=g2):
                        for q in range(4):
                            icx = 4 * g2 + q
                            last = eng.transpose(pb[6 + g2][:, 128 * q:128 * (q + 1)],
                                                 xnt[:, 128 * icx:128 * (icx + 1)], identf[:])
                        return last
                    t_tr = p.comp("te", fn,
                                  waits=[("scal", max(s_nrm[tt], s_ev[g2]))])

                    def fn(eng, tt=tt, g2=g2):
                        for q in range(4):
                            icx = 4 * g2 + q
                            eng.copy(
                                xnqfm[:, BLK * icx + 128 * tt:BLK * icx + 128 * (tt + 1)],
                                pb[6 + g2][:, 128 * q:128 * (q + 1)])
                    s_ev[g2] = p.comp("scal", fn, waits=[("te", t_tr)])
                return t_tr

            emit_sq(0)
            emit_sq(1)
            emit_ln(0)
            t_tr = emit_tr(0, t_tr)
            emit_ln(1)
            t_tr = emit_tr(1, t_tr)
            emit_sq(2)
            emit_sq(3)
            emit_ln(2)
            t_tr = emit_tr(2, t_tr)
            emit_ln(3)
            t_tr = emit_tr(3, t_tr)

            for tt in range(NTT):
                xnt = xnf[:, D * tt:D * (tt + 1)]

                def fn(eng, xnt=xnt, tt=tt):
                    eng.reduce_max(stats[:, 8 + tt:9 + tt], xnt, axis=AX.X,
                                   apply_absolute_value=True)
                    eng.drain()
                    eng.tensor_scalar(stats[:, 8 + tt:9 + tt],
                                      stats[:, 8 + tt:9 + tt],
                                      EPS_BIT, None, Alu.max)
                    eng.drain()
                    eng.reciprocal(stats[:, 30:31], stats[:, 8 + tt:9 + tt])
                    eng.drain()
                    eng.tensor_scalar(stats[:, 12 + tt:13 + tt], stats[:, 30:31],
                                      127.0, None, Alu.mult)
                    eng.tensor_mul(stats[:, 16 + tt:17 + tt],
                                   stats[:, 8 + tt:9 + tt], wsb[:, 2:3])
                p.comp("dve", fn, waits=[("scal", s_nrm[tt])])
            v_xnfm = max(s_ev[0], s_ev[1])

            def qkv_pass(region, kind, ic0_waits):
                def mk(ic, cb, kind=kind):
                    def fn(eng, ic=ic, cb=cb, kind=kind):
                        if kind == "V":
                            for tt in range(NTT):
                                for hf in range(2):
                                    last = eng.matmul(
                                        pb[2 * tt + hf],
                                        xnqfm[:, BLK * ic + 128 * tt:BLK * ic + 128 * (tt + 1)],
                                        wslab[:, cb + 512 * hf:cb + 512 * (hf + 1)],
                                        start=(ic == 0), stop=(ic == NIC - 1))
                        else:
                            for oc in range(8):
                                last = eng.matmul(
                                    pb[oc],
                                    wslab[:, cb + 128 * oc:cb + 128 * (oc + 1)],
                                    xnqfm[:, BLK * ic:BLK * (ic + 1)],
                                    start=(ic == 0), stop=(ic == NIC - 1))
                        return last
                    return fn
                return stream_pass(NIC, lambda i: region[:, 1024 * i:1024 * (i + 1)],
                                   mk, ic0_waits)

            def rope_apply(scale8, dst, t_pass, v_prev):
                s_ev = 0
                t_sw = 0
                for c in range(NPAIR):
                    wl = [("te", t_pass if c == 0 else t_sw)]
                    s_ev = p.comp("scal", lambda eng, c=c: eng.copy(
                        qpre[:, BLK * c:BLK * (c + 1)], pb[c]), waits=wl)
                    t_sw = p.comp("te", lambda eng, c=c: eng.matmul(
                        pb[c], swapT[:], qpre[:, BLK * c:BLK * (c + 1)],
                        start=True, stop=True), waits=[("scal", s_ev)])

                    def fn(eng, c=c, scale8=scale8):
                        if scale8:
                            eng.scalar_tensor_tensor(
                                t1b[:, :], qpre[:, BLK * c:BLK * (c + 1)], 8.0,
                                ropeT[:, 0:BLK], Alu.mult, Alu.mult)
                            eng.scalar_tensor_tensor(
                                t2b[:, :], pb[c], 8.0,
                                ropeT[:, BLK:2 * BLK], Alu.mult, Alu.mult)
                        else:
                            eng.tensor_mul(t1b[:, :], qpre[:, BLK * c:BLK * (c + 1)],
                                           ropeT[:, 0:BLK])
                            eng.tensor_mul(t2b[:, :], pb[c],
                                           ropeT[:, BLK:2 * BLK])
                        eng.drain()
                        eng.tensor_add(dst(c), t1b[:, :], t2b[:, :])
                    v_prev = p.comp("dve", fn, waits=[("te", t_sw), ("rope", d_rope),
                                                      ("dve", v_prev)])
                return v_prev, t_sw

            t_q = qkv_pass(scrQ, "QK", [("scal", v_xnfm)])
            v_qrot, t_sw = rope_apply(False,
                                      lambda c: qrot[:, BLK * c:BLK * (c + 1)],
                                      t_q, p.cnt["dve"])
            t_k = qkv_pass(scrK, "QK", [("dve", v_qrot)])
            v_krot, t_sw = rope_apply(True,
                                      lambda c, blk=blk: kfm[:, S * c + BLK * blk:
                                                             S * c + BLK * (blk + 1)],
                                      t_k, v_qrot)
            t_v = qkv_pass(scrV, "V", [("dve", v_krot)])

            v_pre = v_krot
            if blk == 0:
                def fn(eng):
                    for kt in range(NKT_SEQ):
                        for h in range(H):
                            eng.memset(vex[:, 1040 * kt + 65 * h + 64:
                                            1040 * kt + 65 * h + 65], 1.0)
                v_pre = p.comp("dve", fn, waits=[("dve", v_krot), ("te", t_v)])

            s_vex = {}
            for tt in (2, 3, 1, 0):
                def fn(eng, tt=tt, blk=blk):
                    kt = 4 * blk + tt
                    for hf in range(2):
                        for hh in range(8):
                            h = 8 * hf + hh
                            eng.copy(
                                vex[:, 1040 * kt + 65 * h:1040 * kt + 65 * h + 64],
                                pb[2 * tt + hf][:, 64 * hh:64 * (hh + 1)])
                s_vex[tt] = p.comp("scal", fn,
                                   waits=[("te", t_v)] if tt == 2 else ())
            s_vex_fin = s_vex[0]

            v_q8 = p.cnt["dve"]
            t_tr = p.cnt["te"]
            t_tr_prev = {0: 0, 1: 0}
            s_ev_par = {0: 0, 1: 0}
            for tt in range(NTT):
                for q8 in range(2):
                    par = (2 * tt + q8) & 1
                    dstb = hraw[:, 512:1024] if par == 0 else hraw[:, 0:512]

                    def fn(eng, tt=tt, q8=q8, dstb=dstb):
                        eng.tensor_scalar(scrA[:, :],
                                          xnf[:, D * tt + 512 * q8:D * tt + 512 * (q8 + 1)],
                                          stats[:, 12 + tt:13 + tt], MAGIC,
                                          Alu.mult, Alu.add)
                        eng.drain()
                        eng.tensor_scalar(dstb[:, :], scrA[:, :], -MAGIC, None, Alu.add)
                    v_q8 = p.comp("dve", fn, waits=[("dve", v_q8),
                                                    ("te", t_tr_prev[par])])

                    def fn(eng, par=par, dstb=dstb):
                        for t4 in range(4):
                            last = eng.transpose(pb[4 + par][:, 128 * t4:128 * (t4 + 1)],
                                                 dstb[:, 128 * t4:128 * (t4 + 1)], identf[:])
                        return last
                    t_tr = p.comp("te", fn, waits=[
                        ("dve", v_q8),
                        ("scal", s_vex[2] if tt == 0 else s_ev_par[par])])
                    t_tr_prev[par] = t_tr

                    def fn(eng, tt=tt, q8=q8, par=par):
                        for t4 in range(4):
                            icx = 4 * q8 + t4
                            eng.copy(
                                xnqfm[:, BLK * icx + 128 * tt:BLK * icx + 128 * (tt + 1)],
                                pb[4 + par][:, 128 * t4:128 * (t4 + 1)])
                    v_q8s = p.comp("scal", fn, waits=[("te", t_tr)])
                    s_ev_par[par] = v_q8s
            v_xq = v_q8s
            t_xnf_last = t_tr

            s_pv_prev = p.cnt["dve"]
            v_attn_fin = p.cnt["dve"]
            g_att_fin = 0
            bank_free = {}
            pt_ready = {}
            t_pv_hist = {}
            prev_last_exp = 0
            norm_state = {"g_r1": 0, "v_attn": p.cnt["dve"], "g_att": 0}

            def norm_emit(pc_n, s_sm_n, s_pv_n):
                st = norm_state
                g_r1 = p.dma("gp", rmap[:, :], invs[64:65, 0:1024],
                             waits=[("dve", s_sm_n)], sem="nrm")
                v_rc = p.comp("dve", lambda eng: eng.reciprocal(rmap[:, :], rmap[:, :]),
                              waits=[("nrm", g_r1)])
                g_w = p.dma("gp", scrInv, rmap[:, :], waits=[("dve", v_rc)], sem="nrm")
                g_bc = p.dma("gp", invs[0:64, 0:1024],
                             bass.AP(scrInv.tensor, 0, [[0, 64], [1, 1024]]),
                             waits=[("nrm", g_w)], sem="nrm")

                def fn(eng, pc_n=pc_n):
                    a = attn_fm[0:64, BLK * pc_n:BLK * (pc_n + 1)]
                    eng.tensor_mul(a, a, invs[0:64, 0:512])
                    eng.tensor_mul(stage[0:64, :], stage[0:64, :],
                                   invs[0:64, 512:1024])
                v_at = p.comp("dve", fn, waits=[("nrm", g_bc), ("dve", s_pv_n)])
                g_at = p.dma("gp", attn_fm[64:128, BLK * pc_n:BLK * (pc_n + 1)],
                             stage[0:64, :], waits=[("dve", v_at)], sem="att")
                st.update(g_r1=g_r1, v_attn=v_at, g_att=g_at)
                return v_at, g_at

            pend = None
            SCB = [(pb[0], pb[1]), (pb[2], pb[3]), (pb[6], pb[7])]
            SCD = [pbd[0], pbd[1], pbd[3]]
            for pc in range(NPAIR):
                hA, hB = 2 * pc, 2 * pc + 1
                ptAB = [ptb[:, 1024 * i:1024 * (i + 1)] for i in range(3)]
                ptA = [ptAB[i][:, 0:512] for i in range(3)]
                ptB = [ptAB[i][:, 512:1024] for i in range(3)]

                def sc_emit(kt):
                    par = kt % 3
                    sA, sB = SCB[par]
                    jd = kt - 4 * blk
                    band = jd >= 0
                    w0 = 128 * jd if band else 0

                    def fn(eng, pc=pc, kt=kt, sA=sA, sB=sB, band=band, w0=w0):
                        last = eng.matmul(sA,
                                   kfm[0:64, S * pc + 128 * kt:S * pc + 128 * (kt + 1)],
                                   qrot[0:64, BLK * pc:BLK * (pc + 1)],
                                   start=True, stop=not band, tile_position=(0, 0))
                        last = eng.matmul(sB,
                                   kfm[64:128, S * pc + 128 * kt:S * pc + 128 * (kt + 1)],
                                   qrot[64:128, BLK * pc:BLK * (pc + 1)],
                                   start=True, stop=not band, tile_position=(64, 0))
                        if band:
                            for s in (sA, sB):
                                last = eng.matmul(s[:, w0:w0 + 128], trineg[:],
                                                  identb[:], start=False,
                                                  stop=True)
                        return last
                    wl = []
                    if (pc, kt - 3) in bank_free:
                        wl.append(("scal", bank_free[(pc, kt - 3)]))
                    elif kt < 3:
                        wl.append(("scal", prev_last_exp))
                    if pc == 0 and kt <= 2:
                        wl.append(("scal", v_xq))
                        wl.append(("dve", v_pre))
                    t_sc = p.comp("te", fn, waits=wl)

                    def fn(eng, par=par, pc=pc, band=band, w0=w0):
                        pt, sd = ptAB[par], SCD[par]
                        if band and w0 > 0:
                            eng.activation(pt[:, 0:w0], sd[:, 0:w0],
                                           Act.Copy, scale=0.0)
                            eng.activation(pt[:, 512:512 + w0],
                                           sd[:, 512:512 + w0],
                                           Act.Copy, scale=0.0)
                            eng.activation(pt[:, w0:512], sd[:, w0:512],
                                           Act.Exp,
                                           bias=mh[:, 16 + pc:17 + pc],
                                           scale=1.0)
                            eng.activation(pt[:, 512 + w0:1024],
                                           sd[:, 512 + w0:1024], Act.Exp,
                                           bias=mh[:, 16 + pc:17 + pc],
                                           scale=1.0)
                        else:
                            eng.activation(pt[:, :], sd[:, :], Act.Exp,
                                           bias=mh[:, 16 + pc:17 + pc],
                                           scale=1.0)
                    wl = [("te", t_sc)]
                    if (pc, kt - 3) in t_pv_hist:
                        wl.append(("te", t_pv_hist[(pc, kt - 3)]))
                    s_exp = p.comp("scal", fn, waits=wl)
                    bank_free[(pc, kt)] = s_exp
                    pt_ready[(pc, kt)] = ("scal", s_exp)

                def pv_emit(kt):
                    par = kt % 3
                    wl = [pt_ready[(pc, kt)]]
                    if kt == 0:
                        wl.append(("dve", s_pv_prev))
                        if pc == 0:
                            wl.append(("dve", v_pre))
                            wl.append(("scal", s_vex_fin))

                    def fn(eng, kt=kt, par=par, hA=hA, hB=hB, nkt=nkt):
                        eng.matmul(pb[4][0:65, :],
                                   vex[:, 1040 * kt + 65 * hA:1040 * kt + 65 * (hA + 1)],
                                   ptA[par][:, :], start=(kt == 0), stop=(kt == nkt - 1))
                        return eng.matmul(pb[5][0:65, :],
                                   vex[:, 1040 * kt + 65 * hB:1040 * kt + 65 * (hB + 1)],
                                   ptB[par][:, :], start=(kt == 0), stop=(kt == nkt - 1))
                    t_pv = p.comp("te", fn, waits=wl)
                    t_pv_hist[(pc, kt)] = t_pv
                    return t_pv

                for kt in range(nkt):
                    sc_emit(kt)
                    if kt >= 2:
                        pv_emit(kt - 2)
                    if kt == 3 and pend is not None:
                        norm_emit(*pend)
                        pend = None
                pv_emit(nkt - 2)
                t_pv = pv_emit(nkt - 1)
                prev_last_exp = bank_free[(pc, nkt - 1)]

                wl_sm = [("te", t_pv), ("nrm", norm_state["g_r1"])]
                if pc == 0:
                    wl_sm.append(("outs", g_out))
                s_sm = p.comp("dve", lambda eng: eng.tensor_copy(
                    invs[64:65, 0:1024], pbd[2][64:65, 0:1024]), waits=wl_sm)

                def fn(eng, pc=pc):
                    eng.tensor_copy(attn_fm[0:64, BLK * pc:BLK * (pc + 1)],
                                    pb[4][0:64, :])
                    eng.tensor_copy(stage[0:64, :], pb[5][0:64, :])
                s_pv = p.comp("dve", fn,
                              waits=[("att", norm_state["g_att"]),
                                     ("dve", norm_state["v_attn"])])
                s_pv_prev = s_pv
                if pend is not None:
                    norm_emit(*pend)
                pend = (pc, s_sm, s_pv)
            v_attn_fin, g_att_fin = norm_emit(*pend)
            pend = None

            def mk_wo(fc, cb):
                def fn(eng, fc=fc, cb=cb):
                    for tt in range(NTT):
                        for oh in range(2):
                            last = eng.matmul(
                                pb[2 * tt + oh],
                                attn_fm[:, BLK * fc + 128 * tt:BLK * fc + 128 * (tt + 1)],
                                wslab[:, cb + 512 * oh:cb + 512 * (oh + 1)],
                                start=(fc == 0), stop=(fc == NIC - 1))
                    return last
                return fn
            t_wo = stream_pass(NIC, lambda i: scrWo[:, 1024 * i:1024 * (i + 1)],
                               mk_wo, [("att", g_att_fin), ("dve", v_attn_fin)])

            def fn(eng):
                for tt in range(NTT):
                    for oh in range(2):
                        dst = xblk[:, D * tt + 512 * oh:D * tt + 512 * (oh + 1)]
                        eng.tensor_add(dst, pb[2 * tt + oh], dst)
            v_acc = p.comp("dve", fn, waits=[("te", t_wo)])

            v_g_hist = {-2: v_acc, -1: v_acc}
            v_free_h = [0, 0]
            for hb in range(2):
                s_g = p.cnt["scal"]
                for pss in range(4):
                    base = 4 * (pss % 2)

                    def mk_f1(ic, cb, hb=hb, base=base):
                        def fn(eng, ic=ic, cb=cb, hb=hb, base=base):
                            for tti in range(2):
                                tt = 2 * hb + tti
                                for fbh in range(2):
                                    last = eng.matmul(
                                        pb[base + 2 * tti + fbh],
                                        xnqfm[:, BLK * ic + 128 * tt:BLK * ic + 128 * (tt + 1)],
                                        wslab[:, cb + 512 * fbh:cb + 512 * (fbh + 1)],
                                        start=(ic == 0), stop=(ic == NIC - 1))
                            return last
                        return fn
                    extra = [("dve", v_g_hist[pss - 2])]
                    if pss == 0 and hb == 0:
                        extra.append(("scal", v_xq))
                    t_f1 = stream_pass(
                        NIC,
                        lambda i, pss=pss: scrW1[:, 1024 * (NIC * pss + i):
                                                 1024 * (NIC * pss + i + 1)],
                        mk_f1, extra)

                    v_g = v_g_hist[pss - 1]
                    for tti in range(2):
                        tt = 2 * hb + tti
                        for fbh in range(2):
                            fb = 2 * pss + fbh

                            def fn(eng, tti=tti, fbh=fbh, tt=tt, fb=fb, base=base):
                                eng.scalar_tensor_tensor(
                                    hraw[:, 512 * tti:512 * (tti + 1)],
                                    pb[base + 2 * tti + fbh], stats[:, 16 + tt:17 + tt],
                                    b1b[:, 512 * fb:512 * (fb + 1)], Alu.mult, Alu.add)
                            v_g = p.comp("dve", fn, waits=[("te", t_f1), ("scal", s_g)])

                            def fn(eng, tti=tti, fb=fb):
                                eng.activation(
                                    gbuf[:, FF * tti + 512 * fb:FF * tti + 512 * (fb + 1)],
                                    hraw[:, 512 * tti:512 * (tti + 1)], Act.Gelu)
                            s_g = p.comp("scal", fn, waits=[("dve", v_g)])
                    v_g_hist[pss] = v_g

                v_hq = p.cnt["dve"]
                s_hq = p.cnt["scal"]
                t_tr2_prev = {0: 0, 1: 0}
                for tti in range(2):
                    gt_ = gbuf[:, FF * tti:FF * (tti + 1)]

                    def fn(eng, gt_=gt_, tti=tti):
                        eng.reduce_max(stats[:, 20 + tti:21 + tti], gt_, axis=AX.X,
                                       apply_absolute_value=True)
                        eng.drain()
                        eng.tensor_scalar(stats[:, 20 + tti:21 + tti],
                                          stats[:, 20 + tti:21 + tti], EPS_BIT, None,
                                          Alu.max)
                        eng.drain()
                        eng.reciprocal(stats[:, 30:31], stats[:, 20 + tti:21 + tti])
                        eng.drain()
                        eng.tensor_scalar(stats[:, 22 + tti:23 + tti],
                                          stats[:, 30:31], 127.0, None, Alu.mult)
                        eng.tensor_mul(stats[:, 24 + tti:25 + tti],
                                       stats[:, 20 + tti:21 + tti], wsb[:, 5:6])
                    v_hq = p.comp("dve", fn, waits=[("scal", s_g), ("dve", v_hq)])

                t_f2 = 0
                for c8 in range(8):
                    for tti in range(2):
                        gt_ = gbuf[:, FF * tti:FF * (tti + 1)]
                        par = (2 * c8 + tti) & 1
                        dstb = hraw[:, 512:1024] if par == 0 else hraw[:, 0:512]

                        def fn(eng, gt_=gt_, c8=c8, tti=tti, dstb=dstb):
                            eng.tensor_scalar(scrA[:, :], gt_[:, 512 * c8:512 * (c8 + 1)],
                                              stats[:, 22 + tti:23 + tti], MAGIC,
                                              Alu.mult, Alu.add)
                            eng.drain()
                            eng.tensor_scalar(dstb[:, :], scrA[:, :], -MAGIC, None,
                                              Alu.add)
                        v_hq = p.comp("dve", fn, waits=[("dve", v_hq),
                                                        ("te", t_tr2_prev[par])])

                        def fn(eng, par=par, dstb=dstb):
                            for t4 in range(4):
                                last = eng.transpose(pb[6 + par][:, 128 * t4:128 * (t4 + 1)],
                                                     dstb[:, 128 * t4:128 * (t4 + 1)],
                                                     identf[:])
                            return last
                        t_tr2 = p.comp("te", fn, waits=[("dve", v_hq)])
                        t_tr2_prev[par] = t_tr2

                        def fn(eng, c8=c8, tti=tti, par=par):
                            for t4 in range(4):
                                hc = 4 * c8 + t4
                                eng.copy(
                                    hq_fm[:, 256 * hc + 128 * tti:256 * hc + 128 * (tti + 1)],
                                    pb[6 + par][:, 128 * t4:128 * (t4 + 1)])
                        s_hq = p.comp("scal", fn, waits=[("te", t_tr2)])

                    for j in range(4):
                        hc = 4 * c8 + j
                        sl, sname, d = slab_load(hc, scrW2[:, 1024 * hc:1024 * (hc + 1)])
                        cb = 1024 * sl

                        def fn(eng, hc=hc, cb=cb):
                            for tti in range(2):
                                for dh in range(2):
                                    last = eng.matmul(
                                        pb[2 * tti + dh],
                                        hq_fm[:, 256 * hc + 128 * tti:256 * hc + 128 * (tti + 1)],
                                        wslab[:, cb + 512 * dh:cb + 512 * (dh + 1)],
                                        start=(hc == 0), stop=(hc == NHC - 1))
                            return last
                        wl = [(sname, d), ("scal", s_hq)]
                        if hc == 0:
                            wl.append(("dve", v_g_hist[2]))
                        slab_hist[sl] = p.comp("te", fn, waits=wl)
                        t_f2 = slab_hist[sl]

                def fn(eng, hb=hb):
                    for tti in range(2):
                        tt = 2 * hb + tti
                        ob = outtA if tti == 0 else outtB
                        for dh in range(2):
                            eng.scalar_tensor_tensor(
                                scrA[:, :], pb[2 * tti + dh],
                                stats[:, 24 + tti:25 + tti],
                                b2b[:, 512 * dh:512 * (dh + 1)], Alu.mult, Alu.add)
                            eng.drain()
                            eng.tensor_add(
                                ob[:, 512 * dh:512 * (dh + 1)],
                                scrA[:, :],
                                xblk[:, D * tt + 512 * dh:D * tt + 512 * (dh + 1)])
                            eng.drain()
                v_free = p.comp("dve", fn, waits=[("te", t_f2), ("outs", g_out)])
                v_free_h[hb] = v_free
                v_g_hist[2] = v_free
                v_g_hist[-2] = v_free
                v_g_hist[-1] = v_free

                for tti in range(2):
                    tt = 2 * hb + tti
                    ob = outtA if tti == 0 else outtB
                    g_out = p.dma("gp", out_h[r0 + 128 * tt:r0 + 128 * (tt + 1), :],
                                  ob[:, 0:D],
                                  waits=[("dve", v_free)] if tti == 0 else (),
                                  sem="outs")
                    if hb == 1 and tti == 0:
                        g_out_ropefree = g_out

                if hb == 0 and not (seq == 1 and blk == NBLK - 1):
                    r0n = (r0 + BLK) if blk < NBLK - 1 else S * (seq + 1)
                    x_tile_load(0, r0n, [("dve", v_free_h[0])])
                    x_pre = x_tile_load(1, r0n, ())

            out_final = g_out
            xhalf_free = [("dve", v_free_h[0]), ("dve", v_free_h[1])]
            rope_free = ("dve", v_krot)

    p.step("gp", lambda eng: None, waits=[("outs", out_final)])

    with nc.Block() as block:
        @block.sync
        def _(eng):
            p.run("sync", eng)

        @block.gpsimd
        def _(eng):
            p.run("gp", eng)

        @block.vector
        def _(eng):
            p.run("dve", eng)

        @block.scalar
        def _(eng):
            p.run("scal", eng)

        @block.tensor
        def _(eng):
            p.run("te", eng)

    stack.close()
    return nc


_NC_CACHE = None


def kernel(L0_attn_x, L0_attn_W_qkv, L0_attn_W_o, L0_ffn_W1, L0_ffn_b1,
           L0_ffn_W2, L0_ffn_b2):
    global _NC_CACHE
    x = np.ascontiguousarray(np.asarray(L0_attn_x, dtype=np.float32))
    wqkvT = np.ascontiguousarray(np.asarray(L0_attn_W_qkv, dtype=np.float32).T)
    woT = np.ascontiguousarray(np.asarray(L0_attn_W_o, dtype=np.float32).T)
    w1T = np.ascontiguousarray(np.asarray(L0_ffn_W1, dtype=np.float32).T)
    w2T = np.ascontiguousarray(np.asarray(L0_ffn_W2, dtype=np.float32).T)
    b1 = np.ascontiguousarray(np.asarray(L0_ffn_b1, dtype=np.float32).reshape(1, FF))
    b2 = np.ascontiguousarray(np.asarray(L0_ffn_b2, dtype=np.float32).reshape(1, D))
    consts, identf, tri01, swapT, identb, trineg = _host_consts()

    if _NC_CACHE is None:
        _NC_CACHE = build_program()
    nc = _NC_CACHE

    common = {
        "wqkvT": wqkvT, "woT": woT, "w1T": w1T, "w2T": w2T,
        "b1": b1, "b2": b2, "identf": identf, "tri01": tri01, "swapT": swapT,
        "identb": identb, "trineg": trineg,
        **consts,
    }
    in_maps = []
    for c in range(8):
        shard = np.ascontiguousarray(x[2 * c:2 * c + 2].reshape(T, D))
        m = dict(common)
        m["x"] = shard
        in_maps.append(m)

    res = run_bass_kernel_spmd(nc, in_maps, core_ids=list(range(8)))
    outs = [res.results[c]["out"].reshape(2, S, D) for c in range(8)]
    return np.ascontiguousarray(np.concatenate(outs, axis=0))


# version 20
# speedup vs baseline: 1.0358x; 1.0140x over previous
import contextlib

import numpy as np
import ml_dtypes
import concourse.bass as bass
import concourse.mybir as mybir
from concourse.bass_utils import run_bass_kernel_spmd

F32 = mybir.dt.float32
BF16 = mybir.dt.bfloat16
Alu = mybir.AluOpType
Act = mybir.ActivationFunctionType
AX = mybir.AxisListType
bfl = ml_dtypes.bfloat16

B, S, D = 16, 2048, 1024
H, DK = 16, 64
FF = 4096
T = 2 * S
BLK = 512
NBLK = S // BLK
NTT = BLK // 128
NIC = D // 128
NPAIR = H // 2
NKT_SEQ = S // 128
NHC = FF // 128
MAGIC = float(np.float32(1.5 * 2.0 ** 23))
INV_D = 1.0 / D
INV_NW = 1.0 / (FF * D)
EPS_LN = 1e-5
EPS_BIT = 1e-5


class Prog:
    ENGS = ("sync", "gp", "dve", "scal", "te")
    DSEMS = ("slab0", "slab1", "slab2", "slab3", "slab4", "slab5", "xh0",
             "xh1", "rope", "nrm", "att", "outs", "sto", "misc")

    def __init__(self, nc, stack):
        self.nc = nc
        self.steps = {e: [] for e in self.ENGS}
        self.sems = {}
        self.cnt = {}
        for e in self.ENGS + self.DSEMS:
            self.sems[e] = stack.enter_context(nc.semaphore("sem_" + e))
            self.cnt[e] = 0

    def step(self, engine, fn, waits=()):
        self.steps[engine].append((tuple(waits), fn))

    def run(self, engine_name, eng):
        for waits, fn in self.steps[engine_name]:
            for (sname, val) in waits:
                if val > 0:
                    eng.wait_ge(self.sems[sname], val)
            fn(eng)

    def dma(self, q, out_ap, in_ap, waits=(), sem=None):
        sname = sem if sem is not None else q
        semh = self.sems[sname]

        def fn(eng, o=out_ap, i=in_ap):
            eng.dma_start(out=o, in_=i).then_inc(semh, 16)

        self.step(q, fn, waits)
        self.cnt[sname] += 16
        return self.cnt[sname]

    def comp(self, engine, fn, waits=()):
        sem = self.sems[engine]

        def wrapped(eng, fn=fn):
            last = fn(eng)
            if last is None or not hasattr(last, "then_inc"):
                last = eng.drain()
            last.then_inc(sem, 1)

        self.step(engine, wrapped, waits)
        self.cnt[engine] += 1
        return self.cnt[engine]


def _host_consts():
    inv_freq = np.float32(10000.0) ** (
        np.arange(0, DK, 2, dtype=np.float32) / np.float32(DK))
    ang = np.arange(S, dtype=np.float32)[:, None] / inv_freq[None, :]
    cos = np.cos(ang).astype(np.float32).T
    sin = np.sin(ang).astype(np.float32).T
    cosF = np.empty((DK, S), np.float32)
    sinF = np.empty((DK, S), np.float32)
    cosF[0::2] = cos
    cosF[1::2] = cos
    sinF[0::2] = -sin
    sinF[1::2] = sin
    cosP = np.ascontiguousarray(np.concatenate([cosF, cosF], axis=0))
    sinP = np.ascontiguousarray(np.concatenate([sinF, sinF], axis=0))
    eighth = np.float32(0.125)
    consts = {
        "ropeQc": cosP * eighth, "ropeQs": sinP * eighth,
        "ropeKc": cosP, "ropeKs": sinP,
    }
    identf = np.eye(128, dtype=np.float32)
    identb = np.eye(128, dtype=bfl)
    kl = np.arange(128)[:, None]
    ql = np.arange(128)[None, :]
    tri01 = (kl <= ql).astype(bfl)
    trineg = np.where(ql > kl, np.float32(-1e30), np.float32(0.0)).astype(bfl)
    swapT = np.zeros((128, 128), dtype=bfl)
    for dp in range(128):
        swapT[dp ^ 1, dp] = 1.0
    return consts, identf, tri01, swapT, identb, trineg


def build_program():
    nc = bass.Bass("TRN2", target_bir_lowering=False)
    stack = contextlib.ExitStack()
    stack.__enter__()

    x_h = nc.dram_tensor("x", [T, D], F32, kind="ExternalInput").ap()
    out_h = nc.dram_tensor("out", [T, D], F32, kind="ExternalOutput").ap()
    wqkvT_h = nc.dram_tensor("wqkvT", [D, 3 * D], F32, kind="ExternalInput").ap()
    woT_h = nc.dram_tensor("woT", [D, D], F32, kind="ExternalInput").ap()
    w1T_h = nc.dram_tensor("w1T", [D, FF], F32, kind="ExternalInput").ap()
    w2T_h = nc.dram_tensor("w2T", [FF, D], F32, kind="ExternalInput").ap()
    b1_h = nc.dram_tensor("b1", [1, FF], F32, kind="ExternalInput").ap()
    b2_h = nc.dram_tensor("b2", [1, D], F32, kind="ExternalInput").ap()
    identf_h = nc.dram_tensor("identf", [128, 128], F32, kind="ExternalInput").ap()
    identb_h = nc.dram_tensor("identb", [128, 128], BF16, kind="ExternalInput").ap()
    trineg_h = nc.dram_tensor("trineg", [128, 128], BF16, kind="ExternalInput").ap()
    tri_h = nc.dram_tensor("tri01", [128, 128], BF16, kind="ExternalInput").ap()
    swap_h = nc.dram_tensor("swapT", [128, 128], BF16, kind="ExternalInput").ap()
    rope_h = {k: nc.dram_tensor(k, [128, S], F32, kind="ExternalInput").ap()
              for k in ("ropeQc", "ropeQs", "ropeKc", "ropeKs")}

    scrQ = nc.dram_tensor("scrQ", [128, NIC * 1024], BF16, kind="Internal").ap()
    scrK = nc.dram_tensor("scrK", [128, NIC * 1024], BF16, kind="Internal").ap()
    scrV = nc.dram_tensor("scrV", [128, NIC * 1024], BF16, kind="Internal").ap()
    scrWo = nc.dram_tensor("scrWo", [128, NIC * 1024], BF16, kind="Internal").ap()
    scrW1 = nc.dram_tensor("scrW1", [128, 4 * NIC * 1024], BF16, kind="Internal").ap()
    scrW2 = nc.dram_tensor("scrW2", [128, NHC * 1024], BF16, kind="Internal").ap()
    scrInv = nc.dram_tensor("scrInv", [1, 1024], F32, kind="Internal").ap()

    sbt = lambda name, shape, dt: stack.enter_context(nc.sbuf_tensor(name, shape, dt))
    pst = lambda name, shape: stack.enter_context(nc.psum_tensor(name, shape, F32))

    identf = sbt("identf_s", [128, 128], F32)
    identb = sbt("identb_s", [128, 128], BF16)
    trineg = sbt("trineg_s", [128, 128], BF16)
    swapT = sbt("swapT_s", [128, 128], BF16)
    ones_r = sbt("ones_r", [1, 128], F32)
    ones_c = sbt("ones_c", [128, 1], F32)
    b1b = sbt("b1b", [128, FF], F32)
    b2b = sbt("b2b", [128, D], F32)
    mh = sbt("mh", [128, 24], F32)
    wsb = sbt("wsb", [128, 8], F32)
    stats = sbt("stats", [128, 32], F32)
    wslab = sbt("wslab", [128, 6 * 1024], BF16)
    ropeT = sbt("ropeT", [128, 2 * BLK], F32)
    outtA = ropeT

    kfm = sbt("kfm", [128, NPAIR * S], BF16)
    vex = sbt("vex", [128, NKT_SEQ * H * 65], BF16)
    xblk = sbt("xblk", [128, NTT * D], F32)
    gx = sbt("gx", [128, 2 * FF], F32)
    xnf = gx[:, 0:4096]
    gbuf = gx
    xnqfm = sbt("xnqfm", [128, NIC * BLK], BF16)
    qrot = sbt("qrot", [128, NIC * BLK], BF16)
    qpre = sbt("qpre", [128, NIC * BLK], BF16)
    attn_fm = qpre
    ptb = sbt("ptb", [128, 6 * BLK], BF16)
    stage = sbt("stage", [128, BLK], BF16)
    invs = sbt("invs", [128, 2 * BLK], F32)
    outtB = invs
    scrA = sbt("scrA", [128, 512], F32)
    hraw = sbt("hraw", [128, 2 * 512], F32)
    t1b = hraw[:, 0:512]
    t2b = hraw[:, 512:1024]
    hq_fm = sbt("hq_fm", [128, NHC * 256], BF16)
    ternb = hq_fm[:, 0:4096]
    sums = hraw[0:1, 0:1024]
    rmap = invs[0:64, 1008:1024]

    pbd = [pst("pbd%d" % i, [128, 1024]) for i in range(4)]
    pb = [pbd[i // 2][:, 512 * (i % 2):512 * (i % 2) + 512] for i in range(8)]

    p = Prog(nc, stack)

    d_c = p.dma("sync", identf[:], identf_h, sem="misc")
    d_c = p.dma("sync", swapT[:], swap_h, sem="misc")
    d_c = p.dma("sync", identb[:], identb_h, sem="misc")
    d_c = p.dma("sync", trineg[:], trineg_h, sem="misc")

    def fn(eng):
        eng.memset(ones_r[:], 1.0)
        eng.memset(ones_c[:], 1.0)
        eng.memset(stats[:], 0.0)
        eng.drain()
        eng.memset(stats[:, 31:32], EPS_LN)
    v_ones = p.comp("dve", fn, waits=[("misc", d_c)])

    d_b1 = p.dma("sync", gx[0:1, 0:FF], b1_h, sem="misc")
    d_b2 = p.dma("sync", gx[0:1, 4096:4096 + D], b2_h, sem="misc")

    def fn(eng):
        for seg in range(8):
            last = eng.matmul(pb[seg], ones_r[0:1, 0:128],
                              gx[0:1, 512 * seg:512 * (seg + 1)], start=True, stop=True)
        return last
    t_w = p.comp("te", fn, waits=[("misc", d_b2), ("dve", v_ones)])

    def fn(eng):
        for seg in range(8):
            eng.copy(b1b[:, 512 * seg:512 * (seg + 1)], pb[seg])
    s_w = p.comp("scal", fn, waits=[("te", t_w)])

    def fn(eng):
        for seg in range(2):
            last = eng.matmul(pb[seg], ones_r[0:1, 0:128],
                              gx[0:1, 4096 + 512 * seg:4096 + 512 * (seg + 1)],
                              start=True, stop=True)
        return last
    t_w = p.comp("te", fn, waits=[("misc", d_b2), ("scal", s_w)])

    def fn(eng):
        for seg in range(2):
            eng.copy(b2b[:, 512 * seg:512 * (seg + 1)], pb[seg])
    s_w = p.comp("scal", fn, waits=[("te", t_w)])

    def big_load(q, buf_ap, src_ap, width, waits, sem):
        d = 0
        wq = width // 4
        for a in range(4):
            d = p.dma(q, buf_ap[:, wq * a:wq * (a + 1)],
                      src_ap[:, wq * a:wq * (a + 1)],
                      waits=waits if a == 0 else (), sem=sem)
        return d

    slabs = [xblk[:, 0:4096], gx[:, 4096:8192]]
    slab_free = [("dve", 0), ("te", t_w)]
    tern_free = ("sto", 0)

    s_sq = s_w
    t_norm = 0
    for ic in range(NIC):
        bi = ic % 2
        buf = slabs[bi][:, 0:3072]
        sname = "slab%d" % bi
        d_s = big_load("sync", buf, wqkvT_h[128 * ic:128 * (ic + 1), :], 3072,
                       [slab_free[bi]], sname)
        v_c = p.comp("dve", lambda eng, b=buf: eng.tensor_copy(ternb[:, 0:3072], b),
                     waits=[(sname, d_s), tern_free])
        g_l = p.dma("gp", scrQ[:, 1024 * ic:1024 * (ic + 1)], ternb[:, 0:1024], waits=[("dve", v_c)], sem="sto")
        g_l = p.dma("gp", scrK[:, 1024 * ic:1024 * (ic + 1)], ternb[:, 1024:2048], sem="sto")
        g_l = p.dma("gp", scrV[:, 1024 * ic:1024 * (ic + 1)], ternb[:, 2048:3072], sem="sto")
        tern_free = ("sto", g_l)
        s_sq = p.comp("scal", lambda eng, b=buf: eng.activation(
            gx[:, 0:2048], b[:, 0:2048], Act.Square),
            waits=[(sname, d_s), ("scal", s_sq), ("te", t_norm)])

        def fn(eng, ic=ic):
            for seg in range(4):
                last = eng.matmul(pb[2 + seg][0:1, :], ones_c[:, 0:1],
                                  gx[:, 512 * seg:512 * (seg + 1)],
                                  start=(ic == 0), stop=(ic == NIC - 1))
            return last
        t_norm = p.comp("te", fn, waits=[("scal", s_sq)])
        slab_free[bi] = ("scal", s_sq)

    def fn(eng):
        for h in range(16):
            eng.reduce_sum(sums[0:1, h:h + 1],
                           pb[2 + (64 * h) // 512][0:1, (64 * h) % 512:(64 * h) % 512 + 64],
                           axis=AX.X)
        for h in range(16):
            c = 1024 + 64 * h
            eng.reduce_sum(sums[0:1, 16 + h:17 + h],
                           pb[2 + c // 512][0:1, c % 512:c % 512 + 64], axis=AX.X)
        eng.drain()
        eng.tensor_mul(sums[0:1, 32:48], sums[0:1, 0:16], sums[0:1, 16:32])
    v_fro = p.comp("dve", fn, waits=[("te", t_norm)])
    s_fr = p.comp("scal", lambda eng: eng.activation(sums[0:1, 48:64],
                                                     sums[0:1, 32:48], Act.Sqrt),
                  waits=[("dve", v_fro)])
    v_nm = p.comp("dve", lambda eng: eng.tensor_scalar(
        sums[0:1, 64:80], sums[0:1, 48:64], -1.02 * 128.0, None, Alu.mult),
        waits=[("scal", s_fr)])
    t_mh = p.comp("te", lambda eng: eng.matmul(pb[6][:, 0:16], ones_r[0:1, 0:128],
                                               sums[0:1, 64:80], start=True, stop=True),
                  waits=[("dve", v_nm)])
    s_mh = p.comp("scal", lambda eng: eng.copy(mh[:, 0:16], pb[6][:, 0:16]),
                  waits=[("te", t_mh)])
    def fn(eng):
        for c in range(8):
            eng.tensor_tensor(mh[:, 16 + c:17 + c], mh[:, 2 * c:2 * c + 1],
                              mh[:, 2 * c + 1:2 * c + 2], Alu.min)
    s_mh = p.comp("dve", fn, waits=[("scal", s_mh)])

    for fc in range(NIC):
        bi = fc % 2
        buf = slabs[bi][:, 0:1024]
        sname = "slab%d" % bi
        d_s = big_load("sync", buf, woT_h[128 * fc:128 * (fc + 1), :], 1024,
                       [slab_free[bi]], sname)
        v_c = p.comp("dve", lambda eng, b=buf: eng.tensor_copy(ternb[:, 0:1024], b),
                     waits=[(sname, d_s), tern_free])
        g_l = p.dma("gp", scrWo[:, 1024 * fc:1024 * (fc + 1)], ternb[:, 0:1024], waits=[("dve", v_c)], sem="sto")
        tern_free = ("sto", g_l)
        slab_free[bi] = ("dve", v_c)

    v_a = p.comp("dve", lambda eng: eng.memset(stats[:, 28:31], 0.0),
                 waits=[("scal", s_mh)])
    slabs3 = [xblk[:, 0:4096], gx[:, 4096:8192], gx[:, 0:4096]]
    free3 = [slab_free[0], slab_free[1], ("scal", s_mh)]
    for i in range(NIC + NHC):
        w2 = i >= NIC
        width = 1024 if w2 else 4096
        src_ap = (w2T_h[128 * (i - NIC):128 * (i - NIC + 1), :] if w2
                  else w1T_h[128 * i:128 * (i + 1), :])
        bi = i % 3
        buf = slabs3[bi][:, 0:width]
        q = "sync" if i % 2 == 0 else "gp"
        sname = "slab%d" % bi
        d_s = big_load(q, buf, src_ap, width, [free3[bi]], sname)

        def fn(eng, b=buf, w2=w2):
            eng.reduce_sum(stats[:, 30:31], b, axis=AX.X, apply_absolute_value=True)
            eng.drain()
            acc = stats[:, 29:30] if w2 else stats[:, 28:29]
            eng.tensor_add(acc, acc, stats[:, 30:31])
        v_a = p.comp("dve", fn, waits=[(sname, d_s), ("dve", v_a)])
        free3[bi] = ("dve", v_a)
    slab_free[0] = free3[0]
    slab_free[1] = free3[1]
    gx_lo_free = free3[2]

    t_w = p.comp("te", lambda eng: (
        eng.matmul(pb[0][0:1, 0:1], ones_c[:, 0:1], stats[:, 28:29], start=True, stop=True),
        eng.matmul(pb[0][0:1, 1:2], ones_c[:, 0:1], stats[:, 29:30], start=True, stop=True))[-1],
        waits=[("dve", v_a)])
    s_w = p.comp("scal", lambda eng: eng.copy(sums[0:1, 100:102], pb[0][0:1, 0:2]),
                 waits=[("te", t_w)])

    def fn(eng):
        eng.tensor_scalar(sums[0:1, 102:104], sums[0:1, 100:102], INV_NW, EPS_BIT,
                          Alu.mult, Alu.max)
        eng.drain()
        eng.tensor_scalar(sums[0:1, 104:105], sums[0:1, 102:103], 0.5, None, Alu.mult)
        eng.tensor_scalar(sums[0:1, 105:106], sums[0:1, 102:103], -0.5, None, Alu.mult)
        eng.tensor_scalar(sums[0:1, 106:107], sums[0:1, 102:103], 1.0 / 127.0, None, Alu.mult)
        eng.tensor_scalar(sums[0:1, 107:108], sums[0:1, 103:104], 0.5, None, Alu.mult)
        eng.tensor_scalar(sums[0:1, 108:109], sums[0:1, 103:104], -0.5, None, Alu.mult)
        eng.tensor_scalar(sums[0:1, 109:110], sums[0:1, 103:104], 1.0 / 127.0, None, Alu.mult)
    v_ws = p.comp("dve", fn, waits=[("scal", s_w)])
    t_w = p.comp("te", lambda eng: eng.matmul(pb[1][:, 0:6], ones_r[0:1, 0:128],
                                              sums[0:1, 104:110], start=True, stop=True),
                 waits=[("dve", v_ws)])
    s_wsb = p.comp("scal", lambda eng: eng.copy(wsb[:, 0:6], pb[1][:, 0:6]),
                   waits=[("te", t_w)])

    for ic in range(NIC):
        bi = ic % 2
        buf = slabs[bi][:, 0:4096]
        sname = "slab%d" % bi
        d_s = big_load("sync" if ic % 2 == 0 else "gp", buf,
                       w1T_h[128 * ic:128 * (ic + 1), :], 4096,
                       [slab_free[bi], ("scal", s_wsb)] + ([gx_lo_free] if ic == 0 else []),
                       sname)

        def fn(eng, b=buf):
            eng.tensor_single_scalar(gx[:, 0:4096], b, wsb[:, 0:1], Alu.is_gt)
            eng.drain()
            eng.tensor_single_scalar(b, b, wsb[:, 1:2], Alu.is_lt)
            eng.drain()
            eng.tensor_sub(ternb[:, 0:4096], gx[:, 0:4096], b)
        v_t = p.comp("dve", fn, waits=[(sname, d_s), tern_free])
        g_l = 0
        for q in range(4):
            g_l = p.dma("gp",
                        scrW1[:, 1024 * (NIC * q + ic):1024 * (NIC * q + ic) + 1024],
                        ternb[:, 1024 * q:1024 * (q + 1)],
                        waits=[("dve", v_t)] if q == 0 else (), sem="sto")
        tern_free = ("sto", g_l)
        slab_free[bi] = ("dve", v_t)
    for hc in range(NHC):
        bi = hc % 2
        buf = slabs[bi][:, 0:1024]
        sname = "slab%d" % bi
        d_s = big_load("sync" if hc % 2 == 0 else "gp", buf,
                       w2T_h[128 * hc:128 * (hc + 1), :], 1024,
                       [slab_free[bi]], sname)

        def fn(eng, b=buf):
            eng.tensor_single_scalar(gx[:, 0:1024], b, wsb[:, 3:4], Alu.is_gt)
            eng.drain()
            eng.tensor_single_scalar(b, b, wsb[:, 4:5], Alu.is_lt)
            eng.drain()
            eng.tensor_sub(ternb[:, 0:1024], gx[:, 0:1024], b)
        v_t = p.comp("dve", fn, waits=[(sname, d_s), tern_free])
        g_l = p.dma("gp", scrW2[:, 1024 * hc:1024 * (hc + 1)], ternb[:, 0:1024], waits=[("dve", v_t)], sem="sto")
        tern_free = ("sto", g_l)
        slab_free[bi] = ("dve", v_t)

    prep_end = [("sto", p.cnt["sto"]), ("dve", p.cnt["dve"]),
                ("scal", p.cnt["scal"]), ("te", p.cnt["te"])]
    first_stream_extra = [("sto", p.cnt["sto"])]

    slab_hist = {0: 0, 1: 0, 2: 0, 3: 0, 4: 0, 5: 0}

    def slab_load(i, region_ap, extra=()):
        sl = i % 6
        sname = "slab%d" % sl
        dwl = [("te", slab_hist[sl])] + list(extra)
        d = p.dma("sync", wslab[:, 1024 * sl:1024 * (sl + 1)], region_ap,
                  waits=dwl, sem=sname)
        return sl, sname, d

    def stream_pass(n_slabs, region_fn, mm_fn, ic0_extra):
        t = 0
        for i in range(n_slabs):
            extra = first_stream_extra if (first_stream_extra and i < 6) else ()
            sl, sname, d = slab_load(i, region_fn(i), extra)
            wl = [(sname, d)]
            if i == 0:
                wl += ic0_extra
            t = p.comp("te", mm_fn(i, 1024 * sl), waits=wl)
            slab_hist[sl] = t
        if first_stream_extra:
            first_stream_extra.clear()
        return t

    xhalf_free = [("dve", 0), ("dve", 0)]
    rope_free = ("dve", 0)
    g_out = 0
    g_out_ropefree = 0
    x_pre = None
    out_final = 0
    pend_norm = [None]

    for seq in range(2):
        for blk in range(NBLK):
            r0 = S * seq + BLK * blk
            nkt = 4 * (blk + 1)

            def x_tile_load(tt, r0t, wx):
                g = 0
                for ps in range(2):
                    for cs in range(2):
                        g = p.dma(
                            "gp",
                            xblk[64 * ps:64 * (ps + 1),
                                 D * tt + 512 * cs:D * tt + 512 * (cs + 1)],
                            x_h[r0t + 128 * tt + 64 * ps:
                                r0t + 128 * tt + 64 * (ps + 1),
                                512 * cs:512 * (cs + 1)],
                            waits=wx if (ps == 0 and cs == 0) else (),
                            sem="xh%d" % (tt // 2))
                return g

            if x_pre is None:
                wx = list(prep_end) if (seq == 0 and blk == 0) else [xhalf_free[0]]
                x_tile_load(0, r0, wx)
                gh0 = x_tile_load(1, r0, ())
            else:
                gh0 = x_pre
            x_tile_load(2, r0, [xhalf_free[1]])
            gh1 = x_tile_load(3, r0, ())
            g_x = [gh0, gh0, gh1, gh1]
            d_rope = 0
            for i, k in enumerate(("ropeQc", "ropeQs")):
                d_rope = p.dma("sync", ropeT[:, BLK * i:BLK * (i + 1)],
                               rope_h[k][:, BLK * blk:BLK * (blk + 1)],
                               waits=[rope_free, ("outs", g_out_ropefree)] if i == 0 else (),
                               sem="rope")

            s_sq = [0] * NTT
            v_var = [0] * NTT
            s_nrm = [0] * NTT
            t_tr = p.cnt["te"]
            s_ev = {0: p.cnt["scal"], 1: p.cnt["scal"]}

            def emit_sq(tt):
                xt = xblk[:, D * tt:D * (tt + 1)]
                s_sq[tt] = p.comp("scal", lambda eng, xt=xt, tt=tt: eng.activation(
                    gx[:, 4096:5120], xt, Act.Square,
                    accum_out=stats[:, 26 + tt:27 + tt]),
                    waits=[("xh%d" % (tt // 2), g_x[tt])])

            def emit_ln(tt):
                xt = xblk[:, D * tt:D * (tt + 1)]
                xnt = xnf[:, D * tt:D * (tt + 1)]

                def fn(eng, xt=xt, tt=tt):
                    eng.reduce_sum(stats[:, 30:31], xt, axis=AX.X)
                    eng.drain()
                    eng.tensor_scalar(stats[:, tt:tt + 1], stats[:, 30:31],
                                      -INV_D, None, Alu.mult)
                    eng.drain()
                    eng.tensor_mul(stats[:, 30:31], stats[:, tt:tt + 1],
                                   stats[:, tt:tt + 1])
                    eng.drain()
                    eng.scalar_tensor_tensor(stats[:, 26 + tt:27 + tt],
                                             stats[:, 26 + tt:27 + tt], INV_D,
                                             stats[:, 30:31], Alu.mult, Alu.subtract)
                v_var[tt] = p.comp("dve", fn, waits=[("xh%d" % (tt // 2), g_x[tt]),
                                                     ("scal", s_sq[tt])])

                def fn(eng, xt=xt, xnt=xnt, tt=tt):
                    eng.activation(wsb[:, 6:7], stats[:, 26 + tt:27 + tt], Act.Ln,
                                   bias=stats[:, 31:32])
                    eng.drain()
                    eng.activation(stats[:, 4 + tt:5 + tt], wsb[:, 6:7],
                                   Act.Exp, scale=-0.5)
                    eng.drain()
                    eng.activation(wsb[:, 7:8], stats[:, tt:tt + 1], Act.Identity,
                                   scale=stats[:, 4 + tt:5 + tt])
                    eng.drain()
                    eng.activation(xnt, xt, Act.Identity, bias=wsb[:, 7:8],
                                   scale=stats[:, 4 + tt:5 + tt])
                s_nrm[tt] = p.comp("scal", fn, waits=[("dve", v_var[tt])])

            def emit_tr(tt, t_tr):
                xnt = xnf[:, D * tt:D * (tt + 1)]
                for g2 in range(2):
                    def fn(eng, xnt=xnt, g2=g2):
                        for q in range(4):
                            icx = 4 * g2 + q
                            last = eng.transpose(pb[6 + g2][:, 128 * q:128 * (q + 1)],
                                                 xnt[:, 128 * icx:128 * (icx + 1)], identf[:])
                        return last
                    t_tr = p.comp("te", fn,
                                  waits=[("scal", max(s_nrm[tt], s_ev[g2]))])

                    def fn(eng, tt=tt, g2=g2):
                        for q in range(4):
                            icx = 4 * g2 + q
                            eng.copy(
                                xnqfm[:, BLK * icx + 128 * tt:BLK * icx + 128 * (tt + 1)],
                                pb[6 + g2][:, 128 * q:128 * (q + 1)])
                    s_ev[g2] = p.comp("scal", fn, waits=[("te", t_tr)])
                return t_tr

            emit_sq(0)
            emit_sq(1)
            emit_ln(0)
            t_tr = emit_tr(0, t_tr)
            emit_ln(1)
            t_tr = emit_tr(1, t_tr)
            emit_sq(2)
            emit_sq(3)
            emit_ln(2)
            t_tr = emit_tr(2, t_tr)
            emit_ln(3)
            t_tr = emit_tr(3, t_tr)

            for tt in range(NTT):
                xnt = xnf[:, D * tt:D * (tt + 1)]

                def fn(eng, xnt=xnt, tt=tt):
                    eng.reduce_max(stats[:, 8 + tt:9 + tt], xnt, axis=AX.X,
                                   apply_absolute_value=True)
                    eng.drain()
                    eng.tensor_scalar(stats[:, 8 + tt:9 + tt],
                                      stats[:, 8 + tt:9 + tt],
                                      EPS_BIT, None, Alu.max)
                    eng.drain()
                    eng.reciprocal(stats[:, 30:31], stats[:, 8 + tt:9 + tt])
                    eng.drain()
                    eng.tensor_scalar(stats[:, 12 + tt:13 + tt], stats[:, 30:31],
                                      127.0, None, Alu.mult)
                    eng.tensor_mul(stats[:, 16 + tt:17 + tt],
                                   stats[:, 8 + tt:9 + tt], wsb[:, 2:3])
                p.comp("dve", fn, waits=[("scal", s_nrm[tt])])
            v_xnfm = max(s_ev[0], s_ev[1])

            def qkv_pass(region, kind, ic0_waits):
                def mk(ic, cb, kind=kind):
                    def fn(eng, ic=ic, cb=cb, kind=kind):
                        if kind == "V":
                            for tt in range(NTT):
                                for hf in range(2):
                                    last = eng.matmul(
                                        pb[2 * tt + hf],
                                        xnqfm[:, BLK * ic + 128 * tt:BLK * ic + 128 * (tt + 1)],
                                        wslab[:, cb + 512 * hf:cb + 512 * (hf + 1)],
                                        start=(ic == 0), stop=(ic == NIC - 1))
                        else:
                            for oc in range(8):
                                last = eng.matmul(
                                    pb[oc],
                                    wslab[:, cb + 128 * oc:cb + 128 * (oc + 1)],
                                    xnqfm[:, BLK * ic:BLK * (ic + 1)],
                                    start=(ic == 0), stop=(ic == NIC - 1))
                        return last
                    return fn
                return stream_pass(NIC, lambda i: region[:, 1024 * i:1024 * (i + 1)],
                                   mk, ic0_waits)

            def rope_apply(scale8, dst, t_pass, v_prev):
                s_ev = 0
                t_sw = 0
                for c in range(NPAIR):
                    wl = [("te", t_pass if c == 0 else t_sw)]
                    s_ev = p.comp("scal", lambda eng, c=c: eng.copy(
                        qpre[:, BLK * c:BLK * (c + 1)], pb[c]), waits=wl)
                    t_sw = p.comp("te", lambda eng, c=c: eng.matmul(
                        pb[c], swapT[:], qpre[:, BLK * c:BLK * (c + 1)],
                        start=True, stop=True), waits=[("scal", s_ev)])

                    def fn(eng, c=c, scale8=scale8):
                        if scale8:
                            eng.scalar_tensor_tensor(
                                t1b[:, :], qpre[:, BLK * c:BLK * (c + 1)], 8.0,
                                ropeT[:, 0:BLK], Alu.mult, Alu.mult)
                            eng.scalar_tensor_tensor(
                                t2b[:, :], pb[c], 8.0,
                                ropeT[:, BLK:2 * BLK], Alu.mult, Alu.mult)
                        else:
                            eng.tensor_mul(t1b[:, :], qpre[:, BLK * c:BLK * (c + 1)],
                                           ropeT[:, 0:BLK])
                            eng.tensor_mul(t2b[:, :], pb[c],
                                           ropeT[:, BLK:2 * BLK])
                        eng.drain()
                        eng.tensor_add(dst(c), t1b[:, :], t2b[:, :])
                    v_prev = p.comp("dve", fn, waits=[("te", t_sw), ("rope", d_rope),
                                                      ("dve", v_prev)])
                return v_prev, t_sw

            t_q = qkv_pass(scrQ, "QK", [("scal", v_xnfm)])
            v_qrot, t_sw = rope_apply(False,
                                      lambda c: qrot[:, BLK * c:BLK * (c + 1)],
                                      t_q, p.cnt["dve"])
            t_k = qkv_pass(scrK, "QK", [("dve", v_qrot)])
            v_krot, t_sw = rope_apply(True,
                                      lambda c, blk=blk: kfm[:, S * c + BLK * blk:
                                                             S * c + BLK * (blk + 1)],
                                      t_k, v_qrot)
            t_v = qkv_pass(scrV, "V", [("dve", v_krot)])

            v_pre = v_krot
            if blk == 0:
                def fn(eng):
                    for kt in range(NKT_SEQ):
                        for h in range(H):
                            eng.memset(vex[:, 1040 * kt + 65 * h + 64:
                                            1040 * kt + 65 * h + 65], 1.0)
                v_pre = p.comp("dve", fn, waits=[("dve", v_krot), ("te", t_v)])

            s_vex = {}
            for tt in (2, 3, 1, 0):
                def fn(eng, tt=tt, blk=blk):
                    kt = 4 * blk + tt
                    for hf in range(2):
                        for hh in range(8):
                            h = 8 * hf + hh
                            eng.copy(
                                vex[:, 1040 * kt + 65 * h:1040 * kt + 65 * h + 64],
                                pb[2 * tt + hf][:, 64 * hh:64 * (hh + 1)])
                s_vex[tt] = p.comp("scal", fn,
                                   waits=[("te", t_v)] if tt == 2 else ())
            s_vex_fin = s_vex[0]

            v_q8 = p.cnt["dve"]
            t_tr = p.cnt["te"]
            t_tr_prev = {0: 0, 1: 0}
            s_ev_par = {0: 0, 1: 0}
            for tt in range(NTT):
                for q8 in range(2):
                    par = (2 * tt + q8) & 1
                    dstb = hraw[:, 512:1024] if par == 0 else hraw[:, 0:512]

                    def fn(eng, tt=tt, q8=q8, dstb=dstb):
                        eng.tensor_scalar(scrA[:, :],
                                          xnf[:, D * tt + 512 * q8:D * tt + 512 * (q8 + 1)],
                                          stats[:, 12 + tt:13 + tt], MAGIC,
                                          Alu.mult, Alu.add)
                        eng.drain()
                        eng.tensor_scalar(dstb[:, :], scrA[:, :], -MAGIC, None, Alu.add)
                    v_q8 = p.comp("dve", fn, waits=[("dve", v_q8),
                                                    ("te", t_tr_prev[par])])

                    def fn(eng, par=par, dstb=dstb):
                        for t4 in range(4):
                            last = eng.transpose(pb[4 + par][:, 128 * t4:128 * (t4 + 1)],
                                                 dstb[:, 128 * t4:128 * (t4 + 1)], identf[:])
                        return last
                    t_tr = p.comp("te", fn, waits=[
                        ("dve", v_q8),
                        ("scal", s_vex[2] if tt == 0 else s_ev_par[par])])
                    t_tr_prev[par] = t_tr

                    def fn(eng, tt=tt, q8=q8, par=par):
                        for t4 in range(4):
                            icx = 4 * q8 + t4
                            eng.copy(
                                xnqfm[:, BLK * icx + 128 * tt:BLK * icx + 128 * (tt + 1)],
                                pb[4 + par][:, 128 * t4:128 * (t4 + 1)])
                    v_q8s = p.comp("scal", fn, waits=[("te", t_tr)])
                    s_ev_par[par] = v_q8s
            v_xq = v_q8s
            t_xnf_last = t_tr

            s_pv_prev = p.cnt["dve"]
            v_attn_fin = p.cnt["dve"]
            g_att_fin = 0
            bank_free = {}
            pt_ready = {}
            t_pv_hist = {}
            prev_last_exp = 0
            norm_state = {"g_r1": 0, "v_attn": p.cnt["dve"], "g_att": 0}
            norm_done = {}

            def norm_emit(pc_n, s_sm_n, s_pv_n):
                st = norm_state
                g_r1 = p.dma("gp", rmap[:, :], invs[64:65, 0:1024],
                             waits=[("dve", s_sm_n)], sem="nrm")
                v_rc = p.comp("dve", lambda eng: eng.reciprocal(rmap[:, :], rmap[:, :]),
                              waits=[("nrm", g_r1)])
                g_w = p.dma("gp", scrInv, rmap[:, :], waits=[("dve", v_rc)], sem="nrm")
                g_bc = p.dma("gp", invs[0:64, 0:1024],
                             bass.AP(scrInv.tensor, 0, [[0, 64], [1, 1024]]),
                             waits=[("nrm", g_w)], sem="nrm")

                def fn(eng, pc_n=pc_n):
                    a = attn_fm[0:64, BLK * pc_n:BLK * (pc_n + 1)]
                    eng.tensor_mul(a, a, invs[0:64, 0:512])
                    eng.tensor_mul(stage[0:64, :], stage[0:64, :],
                                   invs[0:64, 512:1024])
                v_at = p.comp("dve", fn, waits=[("nrm", g_bc), ("dve", s_pv_n)])
                g_at = p.dma("gp", attn_fm[64:128, BLK * pc_n:BLK * (pc_n + 1)],
                             stage[0:64, :], waits=[("dve", v_at)], sem="att")
                st.update(g_r1=g_r1, v_attn=v_at, g_att=g_at)
                norm_done[pc_n] = (v_at, g_at)
                return v_at, g_at

            pend = None
            SCB = [(pb[0], pb[1]), (pb[2], pb[3]), (pb[6], pb[7])]
            SCD = [pbd[0], pbd[1], pbd[3]]
            for pc in range(NPAIR):
                hA, hB = 2 * pc, 2 * pc + 1
                ptAB = [ptb[:, 1024 * i:1024 * (i + 1)] for i in range(3)]
                ptA = [ptAB[i][:, 0:512] for i in range(3)]
                ptB = [ptAB[i][:, 512:1024] for i in range(3)]

                def sc_emit(kt):
                    par = kt % 3
                    sA, sB = SCB[par]
                    jd = kt - 4 * blk
                    band = jd >= 0
                    w0 = 128 * jd if band else 0

                    def fn(eng, pc=pc, kt=kt, sA=sA, sB=sB, band=band, w0=w0):
                        last = eng.matmul(sA,
                                   kfm[0:64, S * pc + 128 * kt:S * pc + 128 * (kt + 1)],
                                   qrot[0:64, BLK * pc:BLK * (pc + 1)],
                                   start=True, stop=not band, tile_position=(0, 0))
                        last = eng.matmul(sB,
                                   kfm[64:128, S * pc + 128 * kt:S * pc + 128 * (kt + 1)],
                                   qrot[64:128, BLK * pc:BLK * (pc + 1)],
                                   start=True, stop=not band, tile_position=(64, 0))
                        if band:
                            for s in (sA, sB):
                                last = eng.matmul(s[:, w0:w0 + 128], trineg[:],
                                                  identb[:], start=False,
                                                  stop=True)
                        return last
                    wl = []
                    if (pc, kt - 3) in bank_free:
                        wl.append(("scal", bank_free[(pc, kt - 3)]))
                    elif kt < 3:
                        wl.append(("scal", prev_last_exp))
                    if pc == 0 and kt <= 2:
                        wl.append(("scal", v_xq))
                        wl.append(("dve", v_pre))
                    t_sc = p.comp("te", fn, waits=wl)

                    def fn(eng, par=par, pc=pc, band=band, w0=w0):
                        pt, sd = ptAB[par], SCD[par]
                        if band and w0 > 0:
                            eng.activation(pt[:, 0:w0], sd[:, 0:w0],
                                           Act.Copy, scale=0.0)
                            eng.activation(pt[:, 512:512 + w0],
                                           sd[:, 512:512 + w0],
                                           Act.Copy, scale=0.0)
                            eng.activation(pt[:, w0:512], sd[:, w0:512],
                                           Act.Exp,
                                           bias=mh[:, 16 + pc:17 + pc],
                                           scale=1.0)
                            eng.activation(pt[:, 512 + w0:1024],
                                           sd[:, 512 + w0:1024], Act.Exp,
                                           bias=mh[:, 16 + pc:17 + pc],
                                           scale=1.0)
                        else:
                            eng.activation(pt[:, :], sd[:, :], Act.Exp,
                                           bias=mh[:, 16 + pc:17 + pc],
                                           scale=1.0)
                    wl = [("te", t_sc)]
                    if (pc, kt - 3) in t_pv_hist:
                        wl.append(("te", t_pv_hist[(pc, kt - 3)]))
                    s_exp = p.comp("scal", fn, waits=wl)
                    bank_free[(pc, kt)] = s_exp
                    pt_ready[(pc, kt)] = ("scal", s_exp)

                def pv_emit(kt):
                    par = kt % 3
                    wl = [pt_ready[(pc, kt)]]
                    if kt == 0:
                        wl.append(("dve", s_pv_prev))
                        if pc == 0:
                            wl.append(("dve", v_pre))
                            wl.append(("scal", s_vex_fin))

                    def fn(eng, kt=kt, par=par, hA=hA, hB=hB, nkt=nkt):
                        eng.matmul(pb[4][0:65, :],
                                   vex[:, 1040 * kt + 65 * hA:1040 * kt + 65 * (hA + 1)],
                                   ptA[par][:, :], start=(kt == 0), stop=(kt == nkt - 1))
                        return eng.matmul(pb[5][0:65, :],
                                   vex[:, 1040 * kt + 65 * hB:1040 * kt + 65 * (hB + 1)],
                                   ptB[par][:, :], start=(kt == 0), stop=(kt == nkt - 1))
                    t_pv = p.comp("te", fn, waits=wl)
                    t_pv_hist[(pc, kt)] = t_pv
                    return t_pv

                for kt in range(nkt):
                    sc_emit(kt)
                    if kt >= 2:
                        pv_emit(kt - 2)
                    if kt == 3 and pend is not None:
                        norm_emit(*pend)
                        pend = None
                pv_emit(nkt - 2)
                t_pv = pv_emit(nkt - 1)
                prev_last_exp = bank_free[(pc, nkt - 1)]

                wl_sm = [("te", t_pv), ("nrm", norm_state["g_r1"])]
                if pc == 0:
                    wl_sm.append(("outs", g_out))
                s_sm = p.comp("dve", lambda eng: eng.tensor_copy(
                    invs[64:65, 0:1024], pbd[2][64:65, 0:1024]), waits=wl_sm)

                def fn(eng, pc=pc):
                    eng.tensor_copy(attn_fm[0:64, BLK * pc:BLK * (pc + 1)],
                                    pb[4][0:64, :])
                    eng.tensor_copy(stage[0:64, :], pb[5][0:64, :])
                s_pv = p.comp("dve", fn,
                              waits=[("att", norm_state["g_att"]),
                                     ("dve", norm_state["v_attn"])])
                s_pv_prev = s_pv
                if pend is not None:
                    norm_emit(*pend)
                pend = (pc, s_sm, s_pv)
            v_attn_fin, g_att_fin = norm_emit(*pend)
            pend = None

            def mk_wo(fc, cb):
                def fn(eng, fc=fc, cb=cb):
                    for tt in range(NTT):
                        for oh in range(2):
                            last = eng.matmul(
                                pb[2 * tt + oh],
                                attn_fm[:, BLK * fc + 128 * tt:BLK * fc + 128 * (tt + 1)],
                                wslab[:, cb + 512 * oh:cb + 512 * (oh + 1)],
                                start=(fc == 0), stop=(fc == NIC - 1))
                    return last
                return fn
            t_wo = 0
            for i in range(NIC):
                sl, sname, d = slab_load(i, scrWo[:, 1024 * i:1024 * (i + 1)])
                wl = [(sname, d), ("att", norm_done[i][1]),
                      ("dve", norm_done[i][0])]
                if i == 0:
                    wl.append(("dve", s_pv_prev))
                t_wo = p.comp("te", mk_wo(i, 1024 * sl), waits=wl)
                slab_hist[sl] = t_wo

            def fn(eng):
                for tt in range(NTT):
                    for oh in range(2):
                        dst = xblk[:, D * tt + 512 * oh:D * tt + 512 * (oh + 1)]
                        eng.tensor_add(dst, pb[2 * tt + oh], dst)
            v_acc = p.comp("dve", fn, waits=[("te", t_wo)])

            v_g_hist = {-2: v_acc, -1: v_acc}
            v_free_h = [0, 0]
            for hb in range(2):
                s_g = p.cnt["scal"]
                for pss in range(4):
                    base = 4 * (pss % 2)

                    def mk_f1(ic, cb, hb=hb, base=base):
                        def fn(eng, ic=ic, cb=cb, hb=hb, base=base):
                            for tti in range(2):
                                tt = 2 * hb + tti
                                for fbh in range(2):
                                    last = eng.matmul(
                                        pb[base + 2 * tti + fbh],
                                        xnqfm[:, BLK * ic + 128 * tt:BLK * ic + 128 * (tt + 1)],
                                        wslab[:, cb + 512 * fbh:cb + 512 * (fbh + 1)],
                                        start=(ic == 0), stop=(ic == NIC - 1))
                            return last
                        return fn
                    extra = [("dve", v_g_hist[pss - 2])]
                    if pss == 0 and hb == 0:
                        extra.append(("scal", v_xq))
                    t_f1 = stream_pass(
                        NIC,
                        lambda i, pss=pss: scrW1[:, 1024 * (NIC * pss + i):
                                                 1024 * (NIC * pss + i + 1)],
                        mk_f1, extra)

                    v_g = v_g_hist[pss - 1]
                    for tti in range(2):
                        tt = 2 * hb + tti
                        for fbh in range(2):
                            fb = 2 * pss + fbh

                            def fn(eng, tti=tti, fbh=fbh, tt=tt, fb=fb, base=base):
                                eng.scalar_tensor_tensor(
                                    hraw[:, 512 * tti:512 * (tti + 1)],
                                    pb[base + 2 * tti + fbh], stats[:, 16 + tt:17 + tt],
                                    b1b[:, 512 * fb:512 * (fb + 1)], Alu.mult, Alu.add)
                            v_g = p.comp("dve", fn, waits=[("te", t_f1), ("scal", s_g)])

                            def fn(eng, tti=tti, fb=fb):
                                eng.activation(
                                    gbuf[:, FF * tti + 512 * fb:FF * tti + 512 * (fb + 1)],
                                    hraw[:, 512 * tti:512 * (tti + 1)], Act.Gelu)
                            s_g = p.comp("scal", fn, waits=[("dve", v_g)])
                    v_g_hist[pss] = v_g

                v_hq = p.cnt["dve"]
                s_hq = p.cnt["scal"]
                t_tr2_prev = {0: 0, 1: 0}
                for tti in range(2):
                    gt_ = gbuf[:, FF * tti:FF * (tti + 1)]

                    def fn(eng, gt_=gt_, tti=tti):
                        eng.reduce_max(stats[:, 20 + tti:21 + tti], gt_, axis=AX.X,
                                       apply_absolute_value=True)
                        eng.drain()
                        eng.tensor_scalar(stats[:, 20 + tti:21 + tti],
                                          stats[:, 20 + tti:21 + tti], EPS_BIT, None,
                                          Alu.max)
                        eng.drain()
                        eng.reciprocal(stats[:, 30:31], stats[:, 20 + tti:21 + tti])
                        eng.drain()
                        eng.tensor_scalar(stats[:, 22 + tti:23 + tti],
                                          stats[:, 30:31], 127.0, None, Alu.mult)
                        eng.tensor_mul(stats[:, 24 + tti:25 + tti],
                                       stats[:, 20 + tti:21 + tti], wsb[:, 5:6])
                    v_hq = p.comp("dve", fn, waits=[("scal", s_g), ("dve", v_hq)])

                t_f2 = 0
                for c8 in range(8):
                    for tti in range(2):
                        gt_ = gbuf[:, FF * tti:FF * (tti + 1)]
                        par = (2 * c8 + tti) & 1
                        dstb = hraw[:, 512:1024] if par == 0 else hraw[:, 0:512]

                        def fn(eng, gt_=gt_, c8=c8, tti=tti, dstb=dstb):
                            eng.tensor_scalar(scrA[:, :], gt_[:, 512 * c8:512 * (c8 + 1)],
                                              stats[:, 22 + tti:23 + tti], MAGIC,
                                              Alu.mult, Alu.add)
                            eng.drain()
                            eng.tensor_scalar(dstb[:, :], scrA[:, :], -MAGIC, None,
                                              Alu.add)
                        v_hq = p.comp("dve", fn, waits=[("dve", v_hq),
                                                        ("te", t_tr2_prev[par])])

                        def fn(eng, par=par, dstb=dstb):
                            for t4 in range(4):
                                last = eng.transpose(pb[6 + par][:, 128 * t4:128 * (t4 + 1)],
                                                     dstb[:, 128 * t4:128 * (t4 + 1)],
                                                     identf[:])
                            return last
                        t_tr2 = p.comp("te", fn, waits=[("dve", v_hq)])
                        t_tr2_prev[par] = t_tr2

                        def fn(eng, c8=c8, tti=tti, par=par):
                            for t4 in range(4):
                                hc = 4 * c8 + t4
                                eng.copy(
                                    hq_fm[:, 256 * hc + 128 * tti:256 * hc + 128 * (tti + 1)],
                                    pb[6 + par][:, 128 * t4:128 * (t4 + 1)])
                        s_hq = p.comp("scal", fn, waits=[("te", t_tr2)])

                    for j in range(4):
                        hc = 4 * c8 + j
                        sl, sname, d = slab_load(hc, scrW2[:, 1024 * hc:1024 * (hc + 1)])
                        cb = 1024 * sl

                        def fn(eng, hc=hc, cb=cb):
                            for tti in range(2):
                                for dh in range(2):
                                    last = eng.matmul(
                                        pb[2 * tti + dh],
                                        hq_fm[:, 256 * hc + 128 * tti:256 * hc + 128 * (tti + 1)],
                                        wslab[:, cb + 512 * dh:cb + 512 * (dh + 1)],
                                        start=(hc == 0), stop=(hc == NHC - 1))
                            return last
                        wl = [(sname, d), ("scal", s_hq)]
                        if hc == 0:
                            wl.append(("dve", v_g_hist[2]))
                        slab_hist[sl] = p.comp("te", fn, waits=wl)
                        t_f2 = slab_hist[sl]

                def fn(eng, hb=hb):
                    for tti in range(2):
                        tt = 2 * hb + tti
                        ob = outtA if tti == 0 else outtB
                        for dh in range(2):
                            eng.scalar_tensor_tensor(
                                scrA[:, :], pb[2 * tti + dh],
                                stats[:, 24 + tti:25 + tti],
                                b2b[:, 512 * dh:512 * (dh + 1)], Alu.mult, Alu.add)
                            eng.drain()
                            eng.tensor_add(
                                ob[:, 512 * dh:512 * (dh + 1)],
                                scrA[:, :],
                                xblk[:, D * tt + 512 * dh:D * tt + 512 * (dh + 1)])
                            eng.drain()
                v_free = p.comp("dve", fn, waits=[("te", t_f2), ("outs", g_out)])
                v_free_h[hb] = v_free
                v_g_hist[2] = v_free
                v_g_hist[-2] = v_free
                v_g_hist[-1] = v_free

                for tti in range(2):
                    tt = 2 * hb + tti
                    ob = outtA if tti == 0 else outtB
                    g_out = p.dma("gp", out_h[r0 + 128 * tt:r0 + 128 * (tt + 1), :],
                                  ob[:, 0:D],
                                  waits=[("dve", v_free)] if tti == 0 else (),
                                  sem="outs")
                    if hb == 1 and tti == 0:
                        g_out_ropefree = g_out

                if hb == 0 and not (seq == 1 and blk == NBLK - 1):
                    r0n = (r0 + BLK) if blk < NBLK - 1 else S * (seq + 1)
                    x_tile_load(0, r0n, [("dve", v_free_h[0])])
                    x_pre = x_tile_load(1, r0n, ())

            out_final = g_out
            xhalf_free = [("dve", v_free_h[0]), ("dve", v_free_h[1])]
            rope_free = ("dve", v_krot)

    p.step("gp", lambda eng: None, waits=[("outs", out_final)])

    with nc.Block() as block:
        @block.sync
        def _(eng):
            p.run("sync", eng)

        @block.gpsimd
        def _(eng):
            p.run("gp", eng)

        @block.vector
        def _(eng):
            p.run("dve", eng)

        @block.scalar
        def _(eng):
            p.run("scal", eng)

        @block.tensor
        def _(eng):
            p.run("te", eng)

    stack.close()
    return nc


_NC_CACHE = None


def kernel(L0_attn_x, L0_attn_W_qkv, L0_attn_W_o, L0_ffn_W1, L0_ffn_b1,
           L0_ffn_W2, L0_ffn_b2):
    global _NC_CACHE
    x = np.ascontiguousarray(np.asarray(L0_attn_x, dtype=np.float32))
    wqkvT = np.ascontiguousarray(np.asarray(L0_attn_W_qkv, dtype=np.float32).T)
    woT = np.ascontiguousarray(np.asarray(L0_attn_W_o, dtype=np.float32).T)
    w1T = np.ascontiguousarray(np.asarray(L0_ffn_W1, dtype=np.float32).T)
    w2T = np.ascontiguousarray(np.asarray(L0_ffn_W2, dtype=np.float32).T)
    b1 = np.ascontiguousarray(np.asarray(L0_ffn_b1, dtype=np.float32).reshape(1, FF))
    b2 = np.ascontiguousarray(np.asarray(L0_ffn_b2, dtype=np.float32).reshape(1, D))
    consts, identf, tri01, swapT, identb, trineg = _host_consts()

    if _NC_CACHE is None:
        _NC_CACHE = build_program()
    nc = _NC_CACHE

    common = {
        "wqkvT": wqkvT, "woT": woT, "w1T": w1T, "w2T": w2T,
        "b1": b1, "b2": b2, "identf": identf, "tri01": tri01, "swapT": swapT,
        "identb": identb, "trineg": trineg,
        **consts,
    }
    in_maps = []
    for c in range(8):
        shard = np.ascontiguousarray(x[2 * c:2 * c + 2].reshape(T, D))
        m = dict(common)
        m["x"] = shard
        in_maps.append(m)

    res = run_bass_kernel_spmd(nc, in_maps, core_ids=list(range(8)))
    outs = [res.results[c]["out"].reshape(2, S, D) for c in range(8)]
    return np.ascontiguousarray(np.concatenate(outs, axis=0))


# version 21 (final)
# speedup vs baseline: 1.0545x; 1.0180x over previous
import contextlib

import numpy as np
import ml_dtypes
import concourse.bass as bass
import concourse.mybir as mybir
from concourse.bass_utils import run_bass_kernel_spmd

F32 = mybir.dt.float32
BF16 = mybir.dt.bfloat16
Alu = mybir.AluOpType
Act = mybir.ActivationFunctionType
AX = mybir.AxisListType
bfl = ml_dtypes.bfloat16

B, S, D = 16, 2048, 1024
H, DK = 16, 64
FF = 4096
T = 2 * S
BLK = 512
NBLK = S // BLK
NTT = BLK // 128
NIC = D // 128
NPAIR = H // 2
NKT_SEQ = S // 128
NHC = FF // 128
MAGIC = float(np.float32(1.5 * 2.0 ** 23))
INV_D = 1.0 / D
INV_NW = 1.0 / (FF * D)
EPS_LN = 1e-5
EPS_BIT = 1e-5


class Prog:
    ENGS = ("sync", "gp", "dve", "scal", "te")
    DSEMS = ("slab0", "slab1", "slab2", "slab3", "slab4", "slab5", "xh0",
             "xh1", "rope", "nrm", "att", "outs", "sto", "misc")

    def __init__(self, nc, stack):
        self.nc = nc
        self.steps = {e: [] for e in self.ENGS}
        self.sems = {}
        self.cnt = {}
        for e in self.ENGS + self.DSEMS:
            self.sems[e] = stack.enter_context(nc.semaphore("sem_" + e))
            self.cnt[e] = 0

    def step(self, engine, fn, waits=()):
        self.steps[engine].append((tuple(waits), fn))

    def run(self, engine_name, eng):
        for waits, fn in self.steps[engine_name]:
            for (sname, val) in waits:
                if val > 0:
                    eng.wait_ge(self.sems[sname], val)
            fn(eng)

    def dma(self, q, out_ap, in_ap, waits=(), sem=None):
        sname = sem if sem is not None else q
        semh = self.sems[sname]

        def fn(eng, o=out_ap, i=in_ap):
            eng.dma_start(out=o, in_=i).then_inc(semh, 16)

        self.step(q, fn, waits)
        self.cnt[sname] += 16
        return self.cnt[sname]

    def comp(self, engine, fn, waits=()):
        sem = self.sems[engine]

        def wrapped(eng, fn=fn):
            last = fn(eng)
            if last is None or not hasattr(last, "then_inc"):
                last = eng.drain()
            last.then_inc(sem, 1)

        self.step(engine, wrapped, waits)
        self.cnt[engine] += 1
        return self.cnt[engine]


def _host_consts():
    inv_freq = np.float32(10000.0) ** (
        np.arange(0, DK, 2, dtype=np.float32) / np.float32(DK))
    ang = np.arange(S, dtype=np.float32)[:, None] / inv_freq[None, :]
    cos = np.cos(ang).astype(np.float32).T
    sin = np.sin(ang).astype(np.float32).T
    cosF = np.empty((DK, S), np.float32)
    sinF = np.empty((DK, S), np.float32)
    cosF[0::2] = cos
    cosF[1::2] = cos
    sinF[0::2] = -sin
    sinF[1::2] = sin
    cosP = np.ascontiguousarray(np.concatenate([cosF, cosF], axis=0))
    sinP = np.ascontiguousarray(np.concatenate([sinF, sinF], axis=0))
    eighth = np.float32(0.125)
    consts = {
        "ropeQc": cosP * eighth, "ropeQs": sinP * eighth,
        "ropeKc": cosP, "ropeKs": sinP,
    }
    identf = np.eye(128, dtype=np.float32)
    identb = np.eye(128, dtype=bfl)
    kl = np.arange(128)[:, None]
    ql = np.arange(128)[None, :]
    tri01 = (kl <= ql).astype(bfl)
    trineg = np.where(ql > kl, np.float32(-1e30), np.float32(0.0)).astype(bfl)
    swapT = np.zeros((128, 128), dtype=bfl)
    for dp in range(128):
        swapT[dp ^ 1, dp] = 1.0
    return consts, identf, tri01, swapT, identb, trineg


def build_program():
    nc = bass.Bass("TRN2", target_bir_lowering=False)
    stack = contextlib.ExitStack()
    stack.__enter__()

    x_h = nc.dram_tensor("x", [T, D], F32, kind="ExternalInput").ap()
    out_h = nc.dram_tensor("out", [T, D], F32, kind="ExternalOutput").ap()
    wqkvT_h = nc.dram_tensor("wqkvT", [D, 3 * D], F32, kind="ExternalInput").ap()
    woT_h = nc.dram_tensor("woT", [D, D], F32, kind="ExternalInput").ap()
    w1T_h = nc.dram_tensor("w1T", [D, FF], F32, kind="ExternalInput").ap()
    w2T_h = nc.dram_tensor("w2T", [FF, D], F32, kind="ExternalInput").ap()
    b1_h = nc.dram_tensor("b1", [1, FF], F32, kind="ExternalInput").ap()
    b2_h = nc.dram_tensor("b2", [1, D], F32, kind="ExternalInput").ap()
    identf_h = nc.dram_tensor("identf", [128, 128], F32, kind="ExternalInput").ap()
    identb_h = nc.dram_tensor("identb", [128, 128], BF16, kind="ExternalInput").ap()
    trineg_h = nc.dram_tensor("trineg", [128, 128], BF16, kind="ExternalInput").ap()
    tri_h = nc.dram_tensor("tri01", [128, 128], BF16, kind="ExternalInput").ap()
    swap_h = nc.dram_tensor("swapT", [128, 128], BF16, kind="ExternalInput").ap()
    rope_h = {k: nc.dram_tensor(k, [128, S], F32, kind="ExternalInput").ap()
              for k in ("ropeQc", "ropeQs", "ropeKc", "ropeKs")}

    scrQ = nc.dram_tensor("scrQ", [128, NIC * 1024], BF16, kind="Internal").ap()
    scrK = nc.dram_tensor("scrK", [128, NIC * 1024], BF16, kind="Internal").ap()
    scrV = nc.dram_tensor("scrV", [128, NIC * 1024], BF16, kind="Internal").ap()
    scrWo = nc.dram_tensor("scrWo", [128, NIC * 1024], BF16, kind="Internal").ap()
    scrW1 = nc.dram_tensor("scrW1", [128, 4 * NIC * 1024], BF16, kind="Internal").ap()
    scrW2 = nc.dram_tensor("scrW2", [128, NHC * 1024], BF16, kind="Internal").ap()
    scrInv = nc.dram_tensor("scrInv", [1, 1024], F32, kind="Internal").ap()

    sbt = lambda name, shape, dt: stack.enter_context(nc.sbuf_tensor(name, shape, dt))
    pst = lambda name, shape: stack.enter_context(nc.psum_tensor(name, shape, F32))

    identf = sbt("identf_s", [128, 128], F32)
    identb = sbt("identb_s", [128, 128], BF16)
    trineg = sbt("trineg_s", [128, 128], BF16)
    swapT = sbt("swapT_s", [128, 128], BF16)
    ones_r = sbt("ones_r", [1, 128], F32)
    ones_c = sbt("ones_c", [128, 1], F32)
    b1b = sbt("b1b", [128, FF], F32)
    b2b = sbt("b2b", [128, D], F32)
    mh = sbt("mh", [128, 24], F32)
    wsb = sbt("wsb", [128, 8], F32)
    stats = sbt("stats", [128, 32], F32)
    wslab = sbt("wslab", [128, 6 * 1024], BF16)
    ropeT = sbt("ropeT", [128, 2 * BLK], F32)
    outtA = ropeT

    kfm = sbt("kfm", [128, NPAIR * S], BF16)
    vex = sbt("vex", [128, NKT_SEQ * H * 65], BF16)
    xblk = sbt("xblk", [128, NTT * D], F32)
    gx = sbt("gx", [128, 2 * FF], F32)
    xnf = gx[:, 0:4096]
    gbuf = gx
    xnqfm = sbt("xnqfm", [128, NIC * BLK], BF16)
    qrot = sbt("qrot", [128, NIC * BLK], BF16)
    qpre = sbt("qpre", [128, NIC * BLK], BF16)
    attn_fm = qpre
    ptb = sbt("ptb", [128, 6 * BLK], BF16)
    stage = sbt("stage", [128, BLK], BF16)
    invs = sbt("invs", [128, 2 * BLK], F32)
    outtB = invs
    scrA = sbt("scrA", [128, 512], F32)
    hraw = sbt("hraw", [128, 2 * 512], F32)
    t1b = hraw[:, 0:512]
    t2b = hraw[:, 512:1024]
    hq_fm = sbt("hq_fm", [128, NHC * 256], BF16)
    ternb = hq_fm[:, 0:4096]
    sums = hraw[0:1, 0:1024]
    rmap = invs[0:64, 1008:1024]

    pbd = [pst("pbd%d" % i, [128, 1024]) for i in range(4)]
    pb = [pbd[i // 2][:, 512 * (i % 2):512 * (i % 2) + 512] for i in range(8)]

    p = Prog(nc, stack)

    d_c = p.dma("sync", identf[:], identf_h, sem="misc")
    d_c = p.dma("sync", swapT[:], swap_h, sem="misc")
    d_c = p.dma("sync", identb[:], identb_h, sem="misc")
    d_c = p.dma("sync", trineg[:], trineg_h, sem="misc")

    def fn(eng):
        eng.memset(ones_r[:], 1.0)
        eng.memset(ones_c[:], 1.0)
        eng.memset(stats[:], 0.0)
        eng.drain()
        eng.memset(stats[:, 31:32], EPS_LN)
    v_ones = p.comp("dve", fn, waits=[("misc", d_c)])

    d_b1 = p.dma("sync", gx[0:1, 0:FF], b1_h, sem="misc")
    d_b2 = p.dma("sync", gx[0:1, 4096:4096 + D], b2_h, sem="misc")

    def fn(eng):
        for seg in range(8):
            last = eng.matmul(pb[seg], ones_r[0:1, 0:128],
                              gx[0:1, 512 * seg:512 * (seg + 1)], start=True, stop=True)
        return last
    t_w = p.comp("te", fn, waits=[("misc", d_b2), ("dve", v_ones)])

    def fn(eng):
        for seg in range(8):
            eng.copy(b1b[:, 512 * seg:512 * (seg + 1)], pb[seg])
    s_w = p.comp("scal", fn, waits=[("te", t_w)])

    def fn(eng):
        for seg in range(2):
            last = eng.matmul(pb[seg], ones_r[0:1, 0:128],
                              gx[0:1, 4096 + 512 * seg:4096 + 512 * (seg + 1)],
                              start=True, stop=True)
        return last
    t_w = p.comp("te", fn, waits=[("misc", d_b2), ("scal", s_w)])

    def fn(eng):
        for seg in range(2):
            eng.copy(b2b[:, 512 * seg:512 * (seg + 1)], pb[seg])
    s_w = p.comp("scal", fn, waits=[("te", t_w)])

    def big_load(q, buf_ap, src_ap, width, waits, sem):
        d = 0
        wq = width // 4
        for a in range(4):
            d = p.dma(q, buf_ap[:, wq * a:wq * (a + 1)],
                      src_ap[:, wq * a:wq * (a + 1)],
                      waits=waits if a == 0 else (), sem=sem)
        return d

    slabs = [xblk[:, 0:4096], gx[:, 4096:8192]]
    slab_free = [("dve", 0), ("te", t_w)]
    tern_free = ("sto", 0)

    s_sq = s_w
    t_norm = 0
    for ic in range(NIC):
        bi = ic % 2
        buf = slabs[bi][:, 0:3072]
        sname = "slab%d" % bi
        d_s = big_load("sync", buf, wqkvT_h[128 * ic:128 * (ic + 1), :], 3072,
                       [slab_free[bi]], sname)
        v_c = p.comp("dve", lambda eng, b=buf: eng.tensor_copy(ternb[:, 0:3072], b),
                     waits=[(sname, d_s), tern_free])
        g_l = p.dma("gp", scrQ[:, 1024 * ic:1024 * (ic + 1)], ternb[:, 0:1024], waits=[("dve", v_c)], sem="sto")
        g_l = p.dma("gp", scrK[:, 1024 * ic:1024 * (ic + 1)], ternb[:, 1024:2048], sem="sto")
        g_l = p.dma("gp", scrV[:, 1024 * ic:1024 * (ic + 1)], ternb[:, 2048:3072], sem="sto")
        tern_free = ("sto", g_l)
        s_sq = p.comp("scal", lambda eng, b=buf: eng.activation(
            gx[:, 0:2048], b[:, 0:2048], Act.Square),
            waits=[(sname, d_s), ("scal", s_sq), ("te", t_norm)])

        def fn(eng, ic=ic):
            for seg in range(4):
                last = eng.matmul(pb[2 + seg][0:1, :], ones_c[:, 0:1],
                                  gx[:, 512 * seg:512 * (seg + 1)],
                                  start=(ic == 0), stop=(ic == NIC - 1))
            return last
        t_norm = p.comp("te", fn, waits=[("scal", s_sq)])
        slab_free[bi] = ("scal", s_sq)

    def fn(eng):
        for h in range(16):
            eng.reduce_sum(sums[0:1, h:h + 1],
                           pb[2 + (64 * h) // 512][0:1, (64 * h) % 512:(64 * h) % 512 + 64],
                           axis=AX.X)
        for h in range(16):
            c = 1024 + 64 * h
            eng.reduce_sum(sums[0:1, 16 + h:17 + h],
                           pb[2 + c // 512][0:1, c % 512:c % 512 + 64], axis=AX.X)
        eng.drain()
        eng.tensor_mul(sums[0:1, 32:48], sums[0:1, 0:16], sums[0:1, 16:32])
    v_fro = p.comp("dve", fn, waits=[("te", t_norm)])
    s_fr = p.comp("scal", lambda eng: eng.activation(sums[0:1, 48:64],
                                                     sums[0:1, 32:48], Act.Sqrt),
                  waits=[("dve", v_fro)])
    v_nm = p.comp("dve", lambda eng: eng.tensor_scalar(
        sums[0:1, 64:80], sums[0:1, 48:64], -1.02 * 128.0, None, Alu.mult),
        waits=[("scal", s_fr)])
    t_mh = p.comp("te", lambda eng: eng.matmul(pb[6][:, 0:16], ones_r[0:1, 0:128],
                                               sums[0:1, 64:80], start=True, stop=True),
                  waits=[("dve", v_nm)])
    s_mh = p.comp("scal", lambda eng: eng.copy(mh[:, 0:16], pb[6][:, 0:16]),
                  waits=[("te", t_mh)])
    def fn(eng):
        for c in range(8):
            eng.tensor_tensor(mh[:, 16 + c:17 + c], mh[:, 2 * c:2 * c + 1],
                              mh[:, 2 * c + 1:2 * c + 2], Alu.min)
    s_mh = p.comp("dve", fn, waits=[("scal", s_mh)])

    for fc in range(NIC):
        bi = fc % 2
        buf = slabs[bi][:, 0:1024]
        sname = "slab%d" % bi
        d_s = big_load("sync", buf, woT_h[128 * fc:128 * (fc + 1), :], 1024,
                       [slab_free[bi]], sname)
        v_c = p.comp("dve", lambda eng, b=buf: eng.tensor_copy(ternb[:, 0:1024], b),
                     waits=[(sname, d_s), tern_free])
        g_l = p.dma("gp", scrWo[:, 1024 * fc:1024 * (fc + 1)], ternb[:, 0:1024], waits=[("dve", v_c)], sem="sto")
        tern_free = ("sto", g_l)
        slab_free[bi] = ("dve", v_c)

    v_a = p.comp("dve", lambda eng: eng.memset(stats[:, 28:31], 0.0),
                 waits=[("scal", s_mh)])
    slabs3 = [xblk[:, 0:4096], gx[:, 4096:8192], gx[:, 0:4096]]
    free3 = [slab_free[0], slab_free[1], ("scal", s_mh)]
    for i in range(NIC + NHC):
        w2 = i >= NIC
        width = 1024 if w2 else 4096
        src_ap = (w2T_h[128 * (i - NIC):128 * (i - NIC + 1), :] if w2
                  else w1T_h[128 * i:128 * (i + 1), :])
        bi = i % 3
        buf = slabs3[bi][:, 0:width]
        q = "sync" if i % 2 == 0 else "gp"
        sname = "slab%d" % bi
        d_s = big_load(q, buf, src_ap, width, [free3[bi]], sname)

        def fn(eng, b=buf, w2=w2):
            eng.reduce_sum(stats[:, 30:31], b, axis=AX.X, apply_absolute_value=True)
            eng.drain()
            acc = stats[:, 29:30] if w2 else stats[:, 28:29]
            eng.tensor_add(acc, acc, stats[:, 30:31])
        v_a = p.comp("dve", fn, waits=[(sname, d_s), ("dve", v_a)])
        free3[bi] = ("dve", v_a)
    slab_free[0] = free3[0]
    slab_free[1] = free3[1]
    gx_lo_free = free3[2]

    t_w = p.comp("te", lambda eng: (
        eng.matmul(pb[0][0:1, 0:1], ones_c[:, 0:1], stats[:, 28:29], start=True, stop=True),
        eng.matmul(pb[0][0:1, 1:2], ones_c[:, 0:1], stats[:, 29:30], start=True, stop=True))[-1],
        waits=[("dve", v_a)])
    s_w = p.comp("scal", lambda eng: eng.copy(sums[0:1, 100:102], pb[0][0:1, 0:2]),
                 waits=[("te", t_w)])

    def fn(eng):
        eng.tensor_scalar(sums[0:1, 102:104], sums[0:1, 100:102], INV_NW, EPS_BIT,
                          Alu.mult, Alu.max)
        eng.drain()
        eng.tensor_scalar(sums[0:1, 104:105], sums[0:1, 102:103], 0.5, None, Alu.mult)
        eng.tensor_scalar(sums[0:1, 105:106], sums[0:1, 102:103], -0.5, None, Alu.mult)
        eng.tensor_scalar(sums[0:1, 106:107], sums[0:1, 102:103], 1.0 / 127.0, None, Alu.mult)
        eng.tensor_scalar(sums[0:1, 107:108], sums[0:1, 103:104], 0.5, None, Alu.mult)
        eng.tensor_scalar(sums[0:1, 108:109], sums[0:1, 103:104], -0.5, None, Alu.mult)
        eng.tensor_scalar(sums[0:1, 109:110], sums[0:1, 103:104], 1.0 / 127.0, None, Alu.mult)
    v_ws = p.comp("dve", fn, waits=[("scal", s_w)])
    t_w = p.comp("te", lambda eng: eng.matmul(pb[1][:, 0:6], ones_r[0:1, 0:128],
                                              sums[0:1, 104:110], start=True, stop=True),
                 waits=[("dve", v_ws)])
    s_wsb = p.comp("scal", lambda eng: eng.copy(wsb[:, 0:6], pb[1][:, 0:6]),
                   waits=[("te", t_w)])

    for ic in range(NIC):
        bi = ic % 2
        buf = slabs[bi][:, 0:4096]
        sname = "slab%d" % bi
        d_s = big_load("sync" if ic % 2 == 0 else "gp", buf,
                       w1T_h[128 * ic:128 * (ic + 1), :], 4096,
                       [slab_free[bi], ("scal", s_wsb)] + ([gx_lo_free] if ic == 0 else []),
                       sname)

        def fn(eng, b=buf):
            eng.tensor_single_scalar(gx[:, 0:4096], b, wsb[:, 0:1], Alu.is_gt)
            eng.drain()
            eng.tensor_single_scalar(b, b, wsb[:, 1:2], Alu.is_lt)
            eng.drain()
            eng.tensor_sub(ternb[:, 0:4096], gx[:, 0:4096], b)
        v_t = p.comp("dve", fn, waits=[(sname, d_s), tern_free])
        g_l = 0
        for q in range(4):
            g_l = p.dma("gp",
                        scrW1[:, 1024 * (NIC * q + ic):1024 * (NIC * q + ic) + 1024],
                        ternb[:, 1024 * q:1024 * (q + 1)],
                        waits=[("dve", v_t)] if q == 0 else (), sem="sto")
        tern_free = ("sto", g_l)
        slab_free[bi] = ("dve", v_t)
    for hc in range(NHC):
        bi = hc % 2
        buf = slabs[bi][:, 0:1024]
        sname = "slab%d" % bi
        d_s = big_load("sync" if hc % 2 == 0 else "gp", buf,
                       w2T_h[128 * hc:128 * (hc + 1), :], 1024,
                       [slab_free[bi]], sname)

        def fn(eng, b=buf):
            eng.tensor_single_scalar(gx[:, 0:1024], b, wsb[:, 3:4], Alu.is_gt)
            eng.drain()
            eng.tensor_single_scalar(b, b, wsb[:, 4:5], Alu.is_lt)
            eng.drain()
            eng.tensor_sub(ternb[:, 0:1024], gx[:, 0:1024], b)
        v_t = p.comp("dve", fn, waits=[(sname, d_s), tern_free])
        g_l = p.dma("gp", scrW2[:, 1024 * hc:1024 * (hc + 1)], ternb[:, 0:1024], waits=[("dve", v_t)], sem="sto")
        tern_free = ("sto", g_l)
        slab_free[bi] = ("dve", v_t)

    prep_end = [("sto", p.cnt["sto"]), ("dve", p.cnt["dve"]),
                ("scal", p.cnt["scal"]), ("te", p.cnt["te"])]
    first_stream_extra = [("sto", p.cnt["sto"])]

    slab_hist = {0: 0, 1: 0, 2: 0, 3: 0, 4: 0, 5: 0}

    def slab_load(i, region_ap, extra=()):
        sl = i % 6
        sname = "slab%d" % sl
        dwl = [("te", slab_hist[sl])] + list(extra)
        d = p.dma("sync", wslab[:, 1024 * sl:1024 * (sl + 1)], region_ap,
                  waits=dwl, sem=sname)
        return sl, sname, d

    def stream_pass(n_slabs, region_fn, mm_fn, ic0_extra):
        t = 0
        for i in range(n_slabs):
            extra = first_stream_extra if (first_stream_extra and i < 6) else ()
            sl, sname, d = slab_load(i, region_fn(i), extra)
            wl = [(sname, d)]
            if i == 0:
                wl += ic0_extra
            t = p.comp("te", mm_fn(i, 1024 * sl), waits=wl)
            slab_hist[sl] = t
        if first_stream_extra:
            first_stream_extra.clear()
        return t

    xhalf_free = [("dve", 0), ("dve", 0)]
    rope_free = ("dve", 0)
    g_out = 0
    g_out_ropefree = 0
    x_pre = None
    out_final = 0
    pend_norm = [None]

    for seq in range(2):
        for blk in range(NBLK):
            r0 = S * seq + BLK * blk
            nkt = 4 * (blk + 1)

            def x_tile_load(tt, r0t, wx):
                g = 0
                for ps in range(2):
                    for cs in range(2):
                        g = p.dma(
                            "gp",
                            xblk[64 * ps:64 * (ps + 1),
                                 D * tt + 512 * cs:D * tt + 512 * (cs + 1)],
                            x_h[r0t + 128 * tt + 64 * ps:
                                r0t + 128 * tt + 64 * (ps + 1),
                                512 * cs:512 * (cs + 1)],
                            waits=wx if (ps == 0 and cs == 0) else (),
                            sem="xh%d" % (tt // 2))
                return g

            if x_pre is None:
                wx = list(prep_end) if (seq == 0 and blk == 0) else [xhalf_free[0]]
                x_tile_load(0, r0, wx)
                gh0 = x_tile_load(1, r0, ())
            else:
                gh0 = x_pre
            x_tile_load(2, r0, [xhalf_free[1]])
            gh1 = x_tile_load(3, r0, ())
            g_x = [gh0, gh0, gh1, gh1]
            d_rope = 0
            for i, k in enumerate(("ropeQc", "ropeQs")):
                d_rope = p.dma("sync", ropeT[:, BLK * i:BLK * (i + 1)],
                               rope_h[k][:, BLK * blk:BLK * (blk + 1)],
                               waits=[rope_free, ("outs", g_out_ropefree)] if i == 0 else (),
                               sem="rope")

            s_sq = [0] * NTT
            v_var = [0] * NTT
            s_nrm = [0] * NTT
            t_tr = p.cnt["te"]
            s_ev = {0: p.cnt["scal"], 1: p.cnt["scal"]}

            def emit_sq(tt):
                xt = xblk[:, D * tt:D * (tt + 1)]
                s_sq[tt] = p.comp("scal", lambda eng, xt=xt, tt=tt: eng.activation(
                    gx[:, 4096:5120], xt, Act.Square,
                    accum_out=stats[:, 26 + tt:27 + tt]),
                    waits=[("xh%d" % (tt // 2), g_x[tt])])

            def emit_ln(tt):
                xt = xblk[:, D * tt:D * (tt + 1)]
                xnt = xnf[:, D * tt:D * (tt + 1)]

                def fn(eng, xt=xt, tt=tt):
                    eng.reduce_sum(stats[:, 30:31], xt, axis=AX.X)
                    eng.drain()
                    eng.tensor_scalar(stats[:, tt:tt + 1], stats[:, 30:31],
                                      -INV_D, None, Alu.mult)
                    eng.drain()
                    eng.tensor_mul(stats[:, 30:31], stats[:, tt:tt + 1],
                                   stats[:, tt:tt + 1])
                    eng.drain()
                    eng.scalar_tensor_tensor(stats[:, 26 + tt:27 + tt],
                                             stats[:, 26 + tt:27 + tt], INV_D,
                                             stats[:, 30:31], Alu.mult, Alu.subtract)
                v_var[tt] = p.comp("dve", fn, waits=[("xh%d" % (tt // 2), g_x[tt]),
                                                     ("scal", s_sq[tt])])

                def fn(eng, xt=xt, xnt=xnt, tt=tt):
                    eng.activation(wsb[:, 6:7], stats[:, 26 + tt:27 + tt], Act.Ln,
                                   bias=stats[:, 31:32])
                    eng.drain()
                    eng.activation(stats[:, 4 + tt:5 + tt], wsb[:, 6:7],
                                   Act.Exp, scale=-0.5)
                    eng.drain()
                    eng.activation(wsb[:, 7:8], stats[:, tt:tt + 1], Act.Identity,
                                   scale=stats[:, 4 + tt:5 + tt])
                    eng.drain()
                    eng.activation(xnt, xt, Act.Identity, bias=wsb[:, 7:8],
                                   scale=stats[:, 4 + tt:5 + tt])
                s_nrm[tt] = p.comp("scal", fn, waits=[("dve", v_var[tt])])

            def emit_tr(tt, t_tr):
                xnt = xnf[:, D * tt:D * (tt + 1)]
                for g2 in range(2):
                    def fn(eng, xnt=xnt, g2=g2):
                        for q in range(4):
                            icx = 4 * g2 + q
                            last = eng.transpose(pb[6 + g2][:, 128 * q:128 * (q + 1)],
                                                 xnt[:, 128 * icx:128 * (icx + 1)], identf[:])
                        return last
                    t_tr = p.comp("te", fn,
                                  waits=[("scal", max(s_nrm[tt], s_ev[g2]))])

                    def fn(eng, tt=tt, g2=g2):
                        for q in range(4):
                            icx = 4 * g2 + q
                            eng.copy(
                                xnqfm[:, BLK * icx + 128 * tt:BLK * icx + 128 * (tt + 1)],
                                pb[6 + g2][:, 128 * q:128 * (q + 1)])
                    s_ev[g2] = p.comp("scal", fn, waits=[("te", t_tr)])
                return t_tr

            emit_sq(0)
            emit_sq(1)
            emit_ln(0)
            t_tr = emit_tr(0, t_tr)
            emit_ln(1)
            t_tr = emit_tr(1, t_tr)
            emit_sq(2)
            emit_sq(3)
            emit_ln(2)
            t_tr = emit_tr(2, t_tr)
            emit_ln(3)
            t_tr = emit_tr(3, t_tr)

            for tt in range(NTT):
                xnt = xnf[:, D * tt:D * (tt + 1)]

                def fn(eng, xnt=xnt, tt=tt):
                    eng.reduce_max(stats[:, 8 + tt:9 + tt], xnt, axis=AX.X,
                                   apply_absolute_value=True)
                    eng.drain()
                    eng.tensor_scalar(stats[:, 8 + tt:9 + tt],
                                      stats[:, 8 + tt:9 + tt],
                                      EPS_BIT, None, Alu.max)
                    eng.drain()
                    eng.reciprocal(stats[:, 30:31], stats[:, 8 + tt:9 + tt])
                    eng.drain()
                    eng.tensor_scalar(stats[:, 12 + tt:13 + tt], stats[:, 30:31],
                                      127.0, None, Alu.mult)
                    eng.tensor_mul(stats[:, 16 + tt:17 + tt],
                                   stats[:, 8 + tt:9 + tt], wsb[:, 2:3])
                p.comp("dve", fn, waits=[("scal", s_nrm[tt])])
            v_xnfm = max(s_ev[0], s_ev[1])

            def qkv_pass(region, kind, ic0_waits):
                def mk(ic, cb, kind=kind):
                    def fn(eng, ic=ic, cb=cb, kind=kind):
                        if kind == "V":
                            for tt in range(NTT):
                                for hf in range(2):
                                    last = eng.matmul(
                                        pb[2 * tt + hf],
                                        xnqfm[:, BLK * ic + 128 * tt:BLK * ic + 128 * (tt + 1)],
                                        wslab[:, cb + 512 * hf:cb + 512 * (hf + 1)],
                                        start=(ic == 0), stop=(ic == NIC - 1))
                        else:
                            for oc in range(8):
                                last = eng.matmul(
                                    pb[oc],
                                    wslab[:, cb + 128 * oc:cb + 128 * (oc + 1)],
                                    xnqfm[:, BLK * ic:BLK * (ic + 1)],
                                    start=(ic == 0), stop=(ic == NIC - 1))
                        return last
                    return fn
                return stream_pass(NIC, lambda i: region[:, 1024 * i:1024 * (i + 1)],
                                   mk, ic0_waits)

            def rope_apply(scale8, dst, t_pass, v_prev):
                s_ev = 0
                t_sw = 0
                for c in range(NPAIR):
                    wl = [("te", t_pass if c == 0 else t_sw)]
                    s_ev = p.comp("scal", lambda eng, c=c: eng.copy(
                        qpre[:, BLK * c:BLK * (c + 1)], pb[c]), waits=wl)
                    t_sw = p.comp("te", lambda eng, c=c: eng.matmul(
                        pb[c], swapT[:], qpre[:, BLK * c:BLK * (c + 1)],
                        start=True, stop=True), waits=[("scal", s_ev)])

                    def fn(eng, c=c, scale8=scale8):
                        if scale8:
                            eng.scalar_tensor_tensor(
                                t1b[:, :], qpre[:, BLK * c:BLK * (c + 1)], 8.0,
                                ropeT[:, 0:BLK], Alu.mult, Alu.mult)
                            eng.scalar_tensor_tensor(
                                t2b[:, :], pb[c], 8.0,
                                ropeT[:, BLK:2 * BLK], Alu.mult, Alu.mult)
                        else:
                            eng.tensor_mul(t1b[:, :], qpre[:, BLK * c:BLK * (c + 1)],
                                           ropeT[:, 0:BLK])
                            eng.tensor_mul(t2b[:, :], pb[c],
                                           ropeT[:, BLK:2 * BLK])
                        eng.drain()
                        eng.tensor_add(dst(c), t1b[:, :], t2b[:, :])
                    v_prev = p.comp("dve", fn, waits=[("te", t_sw), ("rope", d_rope),
                                                      ("dve", v_prev)])
                return v_prev, t_sw

            t_q = qkv_pass(scrQ, "QK", [("scal", v_xnfm)])
            v_qrot, t_sw = rope_apply(False,
                                      lambda c: qrot[:, BLK * c:BLK * (c + 1)],
                                      t_q, p.cnt["dve"])
            t_k = qkv_pass(scrK, "QK", [("dve", v_qrot)])
            v_krot, t_sw = rope_apply(True,
                                      lambda c, blk=blk: kfm[:, S * c + BLK * blk:
                                                             S * c + BLK * (blk + 1)],
                                      t_k, v_qrot)
            t_v = qkv_pass(scrV, "V", [("dve", v_krot)])

            v_pre = v_krot
            if blk == 0:
                def fn(eng):
                    for kt in range(NKT_SEQ):
                        for h in range(H):
                            eng.memset(vex[:, 1040 * kt + 65 * h + 64:
                                            1040 * kt + 65 * h + 65], 1.0)
                v_pre = p.comp("dve", fn, waits=[("dve", v_krot), ("te", t_v)])

            s_vex = {}
            for tt in (2, 3, 1, 0):
                def fn(eng, tt=tt, blk=blk):
                    kt = 4 * blk + tt
                    for hf in range(2):
                        for hh in range(8):
                            h = 8 * hf + hh
                            eng.copy(
                                vex[:, 1040 * kt + 65 * h:1040 * kt + 65 * h + 64],
                                pb[2 * tt + hf][:, 64 * hh:64 * (hh + 1)])
                s_vex[tt] = p.comp("scal", fn,
                                   waits=[("te", t_v)] if tt == 2 else ())
            s_vex_fin = s_vex[0]

            v_q8 = p.cnt["dve"]
            t_tr = p.cnt["te"]
            t_tr_prev = {0: 0, 1: 0}
            s_ev_par = {0: 0, 1: 0}
            for tt in range(NTT):
                for q8 in range(2):
                    par = (2 * tt + q8) & 1
                    dstb = hraw[:, 512:1024] if par == 0 else hraw[:, 0:512]

                    def fn(eng, tt=tt, q8=q8, dstb=dstb):
                        eng.tensor_scalar(scrA[:, :],
                                          xnf[:, D * tt + 512 * q8:D * tt + 512 * (q8 + 1)],
                                          stats[:, 12 + tt:13 + tt], MAGIC,
                                          Alu.mult, Alu.add)
                        eng.drain()
                        eng.tensor_scalar(dstb[:, :], scrA[:, :], -MAGIC, None, Alu.add)
                    v_q8 = p.comp("dve", fn, waits=[("dve", v_q8),
                                                    ("te", t_tr_prev[par])])

                    def fn(eng, par=par, dstb=dstb):
                        for t4 in range(4):
                            last = eng.transpose(pb[4 + par][:, 128 * t4:128 * (t4 + 1)],
                                                 dstb[:, 128 * t4:128 * (t4 + 1)], identf[:])
                        return last
                    t_tr = p.comp("te", fn, waits=[
                        ("dve", v_q8),
                        ("scal", s_vex[2] if tt == 0 else s_ev_par[par])])
                    t_tr_prev[par] = t_tr

                    def fn(eng, tt=tt, q8=q8, par=par):
                        for t4 in range(4):
                            icx = 4 * q8 + t4
                            eng.copy(
                                xnqfm[:, BLK * icx + 128 * tt:BLK * icx + 128 * (tt + 1)],
                                pb[4 + par][:, 128 * t4:128 * (t4 + 1)])
                    v_q8s = p.comp("scal", fn, waits=[("te", t_tr)])
                    s_ev_par[par] = v_q8s
            v_xq = v_q8s
            t_xnf_last = t_tr

            s_pv_prev = p.cnt["dve"]
            v_attn_fin = p.cnt["dve"]
            g_att_fin = 0
            bank_free = {}
            pt_ready = {}
            t_pv_hist = {}
            prev_last_exp = 0
            norm_state = {"g_r1": 0, "v_attn": p.cnt["dve"], "g_att": 0}
            norm_done = {}

            def norm_emit(pc_n, s_sm_n, s_pv_n):
                st = norm_state
                g_r1 = p.dma("gp", rmap[:, :], invs[64:65, 0:1024],
                             waits=[("dve", s_sm_n)], sem="nrm")
                v_rc = p.comp("dve", lambda eng: eng.reciprocal(rmap[:, :], rmap[:, :]),
                              waits=[("nrm", g_r1)])
                g_w = p.dma("gp", scrInv, rmap[:, :], waits=[("dve", v_rc)], sem="nrm")
                g_bc = p.dma("gp", invs[0:64, 0:1024],
                             bass.AP(scrInv.tensor, 0, [[0, 64], [1, 1024]]),
                             waits=[("nrm", g_w)], sem="nrm")

                def fn(eng, pc_n=pc_n):
                    a = attn_fm[0:64, BLK * pc_n:BLK * (pc_n + 1)]
                    eng.tensor_mul(a, a, invs[0:64, 0:512])
                    eng.tensor_mul(stage[0:64, :], stage[0:64, :],
                                   invs[0:64, 512:1024])
                v_at = p.comp("dve", fn, waits=[("nrm", g_bc), ("dve", s_pv_n)])
                g_at = p.dma("gp", attn_fm[64:128, BLK * pc_n:BLK * (pc_n + 1)],
                             stage[0:64, :], waits=[("dve", v_at)], sem="att")
                st.update(g_r1=g_r1, v_attn=v_at, g_att=g_at)
                norm_done[pc_n] = (v_at, g_at)
                return v_at, g_at

            pend = None
            SCB = [(pb[0], pb[1]), (pb[2], pb[3]), (pb[6], pb[7])]
            SCD = [pbd[0], pbd[1], pbd[3]]
            for pc in range(NPAIR):
                hA, hB = 2 * pc, 2 * pc + 1
                ptAB = [ptb[:, 1024 * i:1024 * (i + 1)] for i in range(3)]
                ptA = [ptAB[i][:, 0:512] for i in range(3)]
                ptB = [ptAB[i][:, 512:1024] for i in range(3)]

                def sc_emit(kt):
                    par = kt % 3
                    sA, sB = SCB[par]
                    jd = kt - 4 * blk
                    band = jd >= 0
                    w0 = 128 * jd if band else 0

                    def fn(eng, pc=pc, kt=kt, sA=sA, sB=sB, band=band, w0=w0):
                        last = eng.matmul(sA,
                                   kfm[0:64, S * pc + 128 * kt:S * pc + 128 * (kt + 1)],
                                   qrot[0:64, BLK * pc:BLK * (pc + 1)],
                                   start=True, stop=not band, tile_position=(0, 0))
                        last = eng.matmul(sB,
                                   kfm[64:128, S * pc + 128 * kt:S * pc + 128 * (kt + 1)],
                                   qrot[64:128, BLK * pc:BLK * (pc + 1)],
                                   start=True, stop=not band, tile_position=(64, 0))
                        if band:
                            for s in (sA, sB):
                                last = eng.matmul(s[:, w0:w0 + 128], trineg[:],
                                                  identb[:], start=False,
                                                  stop=True)
                        return last
                    wl = []
                    if (pc, kt - 3) in bank_free:
                        wl.append(("scal", bank_free[(pc, kt - 3)]))
                    elif kt < 3:
                        wl.append(("scal", prev_last_exp))
                    if pc == 0 and kt <= 2:
                        wl.append(("scal", v_xq))
                        wl.append(("dve", v_pre))
                    t_sc = p.comp("te", fn, waits=wl)

                    def fn(eng, par=par, pc=pc, band=band, w0=w0):
                        pt, sd = ptAB[par], SCD[par]
                        if band and w0 > 0:
                            eng.activation(pt[:, 0:w0], sd[:, 0:w0],
                                           Act.Copy, scale=0.0)
                            eng.activation(pt[:, 512:512 + w0],
                                           sd[:, 512:512 + w0],
                                           Act.Copy, scale=0.0)
                            eng.activation(pt[:, w0:512], sd[:, w0:512],
                                           Act.Exp,
                                           bias=mh[:, 16 + pc:17 + pc],
                                           scale=1.0)
                            eng.activation(pt[:, 512 + w0:1024],
                                           sd[:, 512 + w0:1024], Act.Exp,
                                           bias=mh[:, 16 + pc:17 + pc],
                                           scale=1.0)
                        else:
                            eng.activation(pt[:, :], sd[:, :], Act.Exp,
                                           bias=mh[:, 16 + pc:17 + pc],
                                           scale=1.0)
                    wl = [("te", t_sc)]
                    if (pc, kt - 3) in t_pv_hist:
                        wl.append(("te", t_pv_hist[(pc, kt - 3)]))
                    s_exp = p.comp("scal", fn, waits=wl)
                    bank_free[(pc, kt)] = s_exp
                    pt_ready[(pc, kt)] = ("scal", s_exp)

                def pv_emit(kt):
                    par = kt % 3
                    wl = [pt_ready[(pc, kt)]]
                    if kt == 0:
                        wl.append(("dve", s_pv_prev))
                        if pc == 0:
                            wl.append(("dve", v_pre))
                            wl.append(("scal", s_vex_fin))

                    def fn(eng, kt=kt, par=par, hA=hA, hB=hB, nkt=nkt):
                        eng.matmul(pb[4][0:65, :],
                                   vex[:, 1040 * kt + 65 * hA:1040 * kt + 65 * (hA + 1)],
                                   ptA[par][:, :], start=(kt == 0), stop=(kt == nkt - 1))
                        return eng.matmul(pb[5][0:65, :],
                                   vex[:, 1040 * kt + 65 * hB:1040 * kt + 65 * (hB + 1)],
                                   ptB[par][:, :], start=(kt == 0), stop=(kt == nkt - 1))
                    t_pv = p.comp("te", fn, waits=wl)
                    t_pv_hist[(pc, kt)] = t_pv
                    return t_pv

                for kt in range(nkt):
                    sc_emit(kt)
                    if kt >= 2:
                        pv_emit(kt - 2)
                    if kt == 3 and pend is not None:
                        norm_emit(*pend)
                        pend = None
                pv_emit(nkt - 2)
                t_pv = pv_emit(nkt - 1)
                prev_last_exp = bank_free[(pc, nkt - 1)]

                wl_sm = [("te", t_pv), ("nrm", norm_state["g_r1"])]
                if pc == 0:
                    wl_sm.append(("outs", g_out))
                s_sm = p.comp("dve", lambda eng: eng.tensor_copy(
                    invs[64:65, 0:1024], pbd[2][64:65, 0:1024]), waits=wl_sm)

                def fn(eng, pc=pc):
                    eng.tensor_copy(attn_fm[0:64, BLK * pc:BLK * (pc + 1)],
                                    pb[4][0:64, :])
                    eng.tensor_copy(stage[0:64, :], pb[5][0:64, :])
                s_pv = p.comp("dve", fn,
                              waits=[("att", norm_state["g_att"]),
                                     ("dve", norm_state["v_attn"])])
                s_pv_prev = s_pv
                if pend is not None:
                    norm_emit(*pend)
                pend = (pc, s_sm, s_pv)
            v_attn_fin, g_att_fin = norm_emit(*pend)
            pend = None

            def mk_wo(fc, cb):
                def fn(eng, fc=fc, cb=cb):
                    for tt in range(NTT):
                        for oh in range(2):
                            last = eng.matmul(
                                pb[2 * tt + oh],
                                attn_fm[:, BLK * fc + 128 * tt:BLK * fc + 128 * (tt + 1)],
                                wslab[:, cb + 512 * oh:cb + 512 * (oh + 1)],
                                start=(fc == 0), stop=(fc == NIC - 1))
                    return last
                return fn
            t_wo = 0
            for i in range(NIC):
                sl, sname, d = slab_load(i, scrWo[:, 1024 * i:1024 * (i + 1)])
                wl = [(sname, d), ("att", norm_done[i][1]),
                      ("dve", norm_done[i][0])]
                if i == 0:
                    wl.append(("dve", s_pv_prev))
                t_wo = p.comp("te", mk_wo(i, 1024 * sl), waits=wl)
                slab_hist[sl] = t_wo

            def fn(eng):
                for tt in range(NTT):
                    for oh in range(2):
                        dst = xblk[:, D * tt + 512 * oh:D * tt + 512 * (oh + 1)]
                        eng.tensor_add(dst, pb[2 * tt + oh], dst)
            v_acc = p.comp("dve", fn, waits=[("te", t_wo)])

            v_g_hist = {-2: v_acc, -1: v_acc}
            v_free_h = [0, 0]
            for hb in range(2):
                s_g = p.cnt["scal"]
                for pss in range(4):
                    base = 4 * (pss % 2)

                    def mk_f1(ic, cb, hb=hb, base=base):
                        def fn(eng, ic=ic, cb=cb, hb=hb, base=base):
                            for tti in range(2):
                                tt = 2 * hb + tti
                                for fbh in range(2):
                                    last = eng.matmul(
                                        pb[base + 2 * tti + fbh],
                                        xnqfm[:, BLK * ic + 128 * tt:BLK * ic + 128 * (tt + 1)],
                                        wslab[:, cb + 512 * fbh:cb + 512 * (fbh + 1)],
                                        start=(ic == 0), stop=(ic == NIC - 1))
                            return last
                        return fn
                    extra = [("dve", v_g_hist[pss - 2])]
                    if pss == 0 and hb == 0:
                        extra.append(("scal", v_xq))
                    t_f1 = stream_pass(
                        NIC,
                        lambda i, pss=pss: scrW1[:, 1024 * (NIC * pss + i):
                                                 1024 * (NIC * pss + i + 1)],
                        mk_f1, extra)

                    v_g = v_g_hist[pss - 1]
                    for tti in range(2):
                        tt = 2 * hb + tti
                        for fbh in range(2):
                            fb = 2 * pss + fbh

                            def fn(eng, tti=tti, fbh=fbh, tt=tt, fb=fb, base=base):
                                eng.scalar_tensor_tensor(
                                    hraw[:, 512 * tti:512 * (tti + 1)],
                                    pb[base + 2 * tti + fbh], stats[:, 16 + tt:17 + tt],
                                    b1b[:, 512 * fb:512 * (fb + 1)], Alu.mult, Alu.add)
                            v_g = p.comp("dve", fn, waits=[("te", t_f1), ("scal", s_g)])

                            def fn(eng, tti=tti, fb=fb):
                                eng.activation(
                                    gbuf[:, FF * tti + 512 * fb:FF * tti + 512 * (fb + 1)],
                                    hraw[:, 512 * tti:512 * (tti + 1)], Act.Gelu)
                            s_g = p.comp("scal", fn, waits=[("dve", v_g)])
                    v_g_hist[pss] = v_g
                    for tti in range(2):
                        def fn(eng, tti=tti, pss=pss):
                            eng.reduce_max(
                                scrA[:, 4 * tti + pss:4 * tti + pss + 1],
                                gbuf[:, FF * tti + 1024 * pss:
                                     FF * tti + 1024 * (pss + 1)],
                                axis=AX.X, apply_absolute_value=True)
                        p.comp("dve", fn, waits=[("scal", s_g)])

                v_hq = p.cnt["dve"]
                s_hq = p.cnt["scal"]
                t_tr2_prev = {0: 0, 1: 0}
                for tti in range(2):
                    gt_ = gbuf[:, FF * tti:FF * (tti + 1)]

                    def fn(eng, gt_=gt_, tti=tti):
                        eng.reduce_max(stats[:, 20 + tti:21 + tti],
                                       scrA[:, 4 * tti:4 * tti + 4], axis=AX.X,
                                       apply_absolute_value=True)
                        eng.drain()
                        eng.tensor_scalar(stats[:, 20 + tti:21 + tti],
                                          stats[:, 20 + tti:21 + tti], EPS_BIT, None,
                                          Alu.max)
                        eng.drain()
                        eng.reciprocal(stats[:, 30:31], stats[:, 20 + tti:21 + tti])
                        eng.drain()
                        eng.tensor_scalar(stats[:, 22 + tti:23 + tti],
                                          stats[:, 30:31], 127.0, None, Alu.mult)
                        eng.tensor_mul(stats[:, 24 + tti:25 + tti],
                                       stats[:, 20 + tti:21 + tti], wsb[:, 5:6])
                    v_hq = p.comp("dve", fn, waits=[("scal", s_g), ("dve", v_hq)])

                t_f2 = 0
                for c8 in range(8):
                    for tti in range(2):
                        gt_ = gbuf[:, FF * tti:FF * (tti + 1)]
                        par = (2 * c8 + tti) & 1
                        dstb = hraw[:, 512:1024] if par == 0 else hraw[:, 0:512]

                        def fn(eng, gt_=gt_, c8=c8, tti=tti, dstb=dstb):
                            eng.tensor_scalar(scrA[:, :], gt_[:, 512 * c8:512 * (c8 + 1)],
                                              stats[:, 22 + tti:23 + tti], MAGIC,
                                              Alu.mult, Alu.add)
                            eng.drain()
                            eng.tensor_scalar(dstb[:, :], scrA[:, :], -MAGIC, None,
                                              Alu.add)
                        v_hq = p.comp("dve", fn, waits=[("dve", v_hq),
                                                        ("te", t_tr2_prev[par])])

                        def fn(eng, par=par, dstb=dstb):
                            for t4 in range(4):
                                last = eng.transpose(pb[6 + par][:, 128 * t4:128 * (t4 + 1)],
                                                     dstb[:, 128 * t4:128 * (t4 + 1)],
                                                     identf[:])
                            return last
                        t_tr2 = p.comp("te", fn, waits=[("dve", v_hq)])
                        t_tr2_prev[par] = t_tr2

                        def fn(eng, c8=c8, tti=tti, par=par):
                            for t4 in range(4):
                                hc = 4 * c8 + t4
                                eng.copy(
                                    hq_fm[:, 256 * hc + 128 * tti:256 * hc + 128 * (tti + 1)],
                                    pb[6 + par][:, 128 * t4:128 * (t4 + 1)])
                        s_hq = p.comp("scal", fn, waits=[("te", t_tr2)])

                    for j in range(4):
                        hc = 4 * c8 + j
                        sl, sname, d = slab_load(hc, scrW2[:, 1024 * hc:1024 * (hc + 1)])
                        cb = 1024 * sl

                        def fn(eng, hc=hc, cb=cb):
                            for tti in range(2):
                                for dh in range(2):
                                    last = eng.matmul(
                                        pb[2 * tti + dh],
                                        hq_fm[:, 256 * hc + 128 * tti:256 * hc + 128 * (tti + 1)],
                                        wslab[:, cb + 512 * dh:cb + 512 * (dh + 1)],
                                        start=(hc == 0), stop=(hc == NHC - 1))
                            return last
                        wl = [(sname, d), ("scal", s_hq)]
                        if hc == 0:
                            wl.append(("dve", v_g_hist[2]))
                        slab_hist[sl] = p.comp("te", fn, waits=wl)
                        t_f2 = slab_hist[sl]

                def fn(eng, hb=hb):
                    for tti in range(2):
                        tt = 2 * hb + tti
                        ob = outtA if tti == 0 else outtB
                        for dh in range(2):
                            eng.scalar_tensor_tensor(
                                scrA[:, :], pb[2 * tti + dh],
                                stats[:, 24 + tti:25 + tti],
                                b2b[:, 512 * dh:512 * (dh + 1)], Alu.mult, Alu.add)
                            eng.drain()
                            eng.tensor_add(
                                ob[:, 512 * dh:512 * (dh + 1)],
                                scrA[:, :],
                                xblk[:, D * tt + 512 * dh:D * tt + 512 * (dh + 1)])
                            eng.drain()
                v_free = p.comp("dve", fn, waits=[("te", t_f2), ("outs", g_out)])
                v_free_h[hb] = v_free
                v_g_hist[2] = v_free
                v_g_hist[-2] = v_free
                v_g_hist[-1] = v_free

                for tti in range(2):
                    tt = 2 * hb + tti
                    ob = outtA if tti == 0 else outtB
                    g_out = p.dma("gp", out_h[r0 + 128 * tt:r0 + 128 * (tt + 1), :],
                                  ob[:, 0:D],
                                  waits=[("dve", v_free)] if tti == 0 else (),
                                  sem="outs")
                    if hb == 1 and tti == 0:
                        g_out_ropefree = g_out

                if hb == 0 and not (seq == 1 and blk == NBLK - 1):
                    r0n = (r0 + BLK) if blk < NBLK - 1 else S * (seq + 1)
                    x_tile_load(0, r0n, [("dve", v_free_h[0])])
                    x_pre = x_tile_load(1, r0n, ())

            out_final = g_out
            xhalf_free = [("dve", v_free_h[0]), ("dve", v_free_h[1])]
            rope_free = ("dve", v_krot)

    p.step("gp", lambda eng: None, waits=[("outs", out_final)])

    with nc.Block() as block:
        @block.sync
        def _(eng):
            p.run("sync", eng)

        @block.gpsimd
        def _(eng):
            p.run("gp", eng)

        @block.vector
        def _(eng):
            p.run("dve", eng)

        @block.scalar
        def _(eng):
            p.run("scal", eng)

        @block.tensor
        def _(eng):
            p.run("te", eng)

    stack.close()
    return nc


_NC_CACHE = None


def kernel(L0_attn_x, L0_attn_W_qkv, L0_attn_W_o, L0_ffn_W1, L0_ffn_b1,
           L0_ffn_W2, L0_ffn_b2):
    global _NC_CACHE
    x = np.ascontiguousarray(np.asarray(L0_attn_x, dtype=np.float32))
    wqkvT = np.ascontiguousarray(np.asarray(L0_attn_W_qkv, dtype=np.float32).T)
    woT = np.ascontiguousarray(np.asarray(L0_attn_W_o, dtype=np.float32).T)
    w1T = np.ascontiguousarray(np.asarray(L0_ffn_W1, dtype=np.float32).T)
    w2T = np.ascontiguousarray(np.asarray(L0_ffn_W2, dtype=np.float32).T)
    b1 = np.ascontiguousarray(np.asarray(L0_ffn_b1, dtype=np.float32).reshape(1, FF))
    b2 = np.ascontiguousarray(np.asarray(L0_ffn_b2, dtype=np.float32).reshape(1, D))
    consts, identf, tri01, swapT, identb, trineg = _host_consts()

    if _NC_CACHE is None:
        _NC_CACHE = build_program()
    nc = _NC_CACHE

    common = {
        "wqkvT": wqkvT, "woT": woT, "w1T": w1T, "w2T": w2T,
        "b1": b1, "b2": b2, "identf": identf, "tri01": tri01, "swapT": swapT,
        "identb": identb, "trineg": trineg,
        **consts,
    }
    in_maps = []
    for c in range(8):
        shard = np.ascontiguousarray(x[2 * c:2 * c + 2].reshape(T, D))
        m = dict(common)
        m["x"] = shard
        in_maps.append(m)

    res = run_bass_kernel_spmd(nc, in_maps, core_ids=list(range(8)))
    outs = [res.results[c]["out"].reshape(2, S, D) for c in range(8)]
    return np.ascontiguousarray(np.concatenate(outs, axis=0))


# version 22
# speedup vs baseline: 1.0687x; 1.0135x over previous
import contextlib

import numpy as np
import ml_dtypes
import concourse.bass as bass
import concourse.mybir as mybir
from concourse.bass_utils import run_bass_kernel_spmd

F32 = mybir.dt.float32
BF16 = mybir.dt.bfloat16
Alu = mybir.AluOpType
Act = mybir.ActivationFunctionType
AX = mybir.AxisListType
bfl = ml_dtypes.bfloat16

B, S, D = 16, 2048, 1024
H, DK = 16, 64
FF = 4096
T = 2 * S
BLK = 512
NBLK = S // BLK
NTT = BLK // 128
NIC = D // 128
NPAIR = H // 2
NKT_SEQ = S // 128
NHC = FF // 128
MAGIC = float(np.float32(1.5 * 2.0 ** 23))
INV_D = 1.0 / D
INV_NW = 1.0 / (FF * D)
EPS_LN = 1e-5
EPS_BIT = 1e-5


class Prog:
    ENGS = ("sync", "gp", "dve", "scal", "te")
    DSEMS = ("slab0", "slab1", "slab2", "slab3", "slab4", "slab5", "xh0",
             "xh1", "rope", "nrm", "att", "outs", "sto", "misc")

    def __init__(self, nc, stack):
        self.nc = nc
        self.steps = {e: [] for e in self.ENGS}
        self.sems = {}
        self.cnt = {}
        for e in self.ENGS + self.DSEMS:
            self.sems[e] = stack.enter_context(nc.semaphore("sem_" + e))
            self.cnt[e] = 0

    def step(self, engine, fn, waits=()):
        self.steps[engine].append((tuple(waits), fn))

    def run(self, engine_name, eng):
        for waits, fn in self.steps[engine_name]:
            for (sname, val) in waits:
                if val > 0:
                    eng.wait_ge(self.sems[sname], val)
            fn(eng)

    def dma(self, q, out_ap, in_ap, waits=(), sem=None):
        sname = sem if sem is not None else q
        semh = self.sems[sname]

        def fn(eng, o=out_ap, i=in_ap):
            eng.dma_start(out=o, in_=i).then_inc(semh, 16)

        self.step(q, fn, waits)
        self.cnt[sname] += 16
        return self.cnt[sname]

    def comp(self, engine, fn, waits=()):
        sem = self.sems[engine]

        def wrapped(eng, fn=fn):
            last = fn(eng)
            if last is None or not hasattr(last, "then_inc"):
                last = eng.drain()
            last.then_inc(sem, 1)

        self.step(engine, wrapped, waits)
        self.cnt[engine] += 1
        return self.cnt[engine]


def _host_consts():
    inv_freq = np.float32(10000.0) ** (
        np.arange(0, DK, 2, dtype=np.float32) / np.float32(DK))
    ang = np.arange(S, dtype=np.float32)[:, None] / inv_freq[None, :]
    cos = np.cos(ang).astype(np.float32).T
    sin = np.sin(ang).astype(np.float32).T
    cosF = np.empty((DK, S), np.float32)
    sinF = np.empty((DK, S), np.float32)
    cosF[0::2] = cos
    cosF[1::2] = cos
    sinF[0::2] = -sin
    sinF[1::2] = sin
    cosP = np.ascontiguousarray(np.concatenate([cosF, cosF], axis=0))
    sinP = np.ascontiguousarray(np.concatenate([sinF, sinF], axis=0))
    eighth = np.float32(0.125)
    consts = {
        "ropeQc": cosP * eighth, "ropeQs": sinP * eighth,
        "ropeKc": cosP, "ropeKs": sinP,
    }
    identf = np.eye(128, dtype=np.float32)
    identb = np.eye(128, dtype=bfl)
    kl = np.arange(128)[:, None]
    ql = np.arange(128)[None, :]
    tri01 = (kl <= ql).astype(bfl)
    trineg = np.where(ql > kl, np.float32(-1e30), np.float32(0.0)).astype(bfl)
    swapT = np.zeros((128, 128), dtype=bfl)
    for dp in range(128):
        swapT[dp ^ 1, dp] = 1.0
    return consts, identf, tri01, swapT, identb, trineg


def build_program():
    nc = bass.Bass("TRN2", target_bir_lowering=False)
    stack = contextlib.ExitStack()
    stack.__enter__()

    x_h = nc.dram_tensor("x", [T, D], F32, kind="ExternalInput").ap()
    out_h = nc.dram_tensor("out", [T, D], F32, kind="ExternalOutput").ap()
    wqkvT_h = nc.dram_tensor("wqkvT", [D, 3 * D], F32, kind="ExternalInput").ap()
    woT_h = nc.dram_tensor("woT", [D, D], F32, kind="ExternalInput").ap()
    w1T_h = nc.dram_tensor("w1T", [D, FF], F32, kind="ExternalInput").ap()
    w2T_h = nc.dram_tensor("w2T", [FF, D], F32, kind="ExternalInput").ap()
    b1_h = nc.dram_tensor("b1", [1, FF], F32, kind="ExternalInput").ap()
    b2_h = nc.dram_tensor("b2", [1, D], F32, kind="ExternalInput").ap()
    identf_h = nc.dram_tensor("identf", [128, 128], F32, kind="ExternalInput").ap()
    identb_h = nc.dram_tensor("identb", [128, 128], BF16, kind="ExternalInput").ap()
    trineg_h = nc.dram_tensor("trineg", [128, 128], BF16, kind="ExternalInput").ap()
    tri_h = nc.dram_tensor("tri01", [128, 128], BF16, kind="ExternalInput").ap()
    swap_h = nc.dram_tensor("swapT", [128, 128], BF16, kind="ExternalInput").ap()
    rope_h = {k: nc.dram_tensor(k, [128, S], F32, kind="ExternalInput").ap()
              for k in ("ropeQc", "ropeQs", "ropeKc", "ropeKs")}

    scrQ = nc.dram_tensor("scrQ", [128, NIC * 1024], BF16, kind="Internal").ap()
    scrK = nc.dram_tensor("scrK", [128, NIC * 1024], BF16, kind="Internal").ap()
    scrV = nc.dram_tensor("scrV", [128, NIC * 1024], BF16, kind="Internal").ap()
    scrWo = nc.dram_tensor("scrWo", [128, NIC * 1024], BF16, kind="Internal").ap()
    scrW1 = nc.dram_tensor("scrW1", [128, 4 * NIC * 1024], BF16, kind="Internal").ap()
    scrW2 = nc.dram_tensor("scrW2", [128, NHC * 1024], BF16, kind="Internal").ap()
    scrInv = nc.dram_tensor("scrInv", [1, 1024], F32, kind="Internal").ap()

    sbt = lambda name, shape, dt: stack.enter_context(nc.sbuf_tensor(name, shape, dt))
    pst = lambda name, shape: stack.enter_context(nc.psum_tensor(name, shape, F32))

    identf = sbt("identf_s", [128, 128], F32)
    identb = sbt("identb_s", [128, 128], BF16)
    trineg = sbt("trineg_s", [128, 128], BF16)
    swapT = sbt("swapT_s", [128, 128], BF16)
    ones_r = sbt("ones_r", [1, 128], F32)
    ones_c = sbt("ones_c", [128, 1], F32)
    b1b = sbt("b1b", [128, FF], F32)
    b2b = sbt("b2b", [128, D], F32)
    mh = sbt("mh", [128, 24], F32)
    wsb = sbt("wsb", [128, 8], F32)
    stats = sbt("stats", [128, 32], F32)
    wslab = sbt("wslab", [128, 6 * 1024], BF16)
    ropeT = sbt("ropeT", [128, 2 * BLK], F32)
    outtA = ropeT

    kfm = sbt("kfm", [128, NPAIR * S], BF16)
    vex = sbt("vex", [128, NKT_SEQ * H * 65], BF16)
    xblk = sbt("xblk", [128, NTT * D], F32)
    gx = sbt("gx", [128, 2 * FF], F32)
    xnf = gx[:, 0:4096]
    gbuf = gx
    xnqfm = sbt("xnqfm", [128, NIC * BLK], BF16)
    qrot = sbt("qrot", [128, NIC * BLK], BF16)
    qpre = sbt("qpre", [128, NIC * BLK], BF16)
    attn_fm = qpre
    ptb = sbt("ptb", [128, 6 * BLK], BF16)
    stage = sbt("stage", [128, BLK], BF16)
    invs = sbt("invs", [128, 2 * BLK], F32)
    outtB = invs
    scrA = sbt("scrA", [128, 512], F32)
    hraw = sbt("hraw", [128, 2 * 512], F32)
    t1b = hraw[:, 0:512]
    t2b = hraw[:, 512:1024]
    hq_fm = sbt("hq_fm", [128, NHC * 256], BF16)
    ternb = hq_fm[:, 0:4096]
    sums = hraw[0:1, 0:1024]
    rmap = invs[0:64, 1008:1024]

    pbd = [pst("pbd%d" % i, [128, 1024]) for i in range(4)]
    pb = [pbd[i // 2][:, 512 * (i % 2):512 * (i % 2) + 512] for i in range(8)]

    p = Prog(nc, stack)

    d_c = p.dma("sync", identf[:], identf_h, sem="misc")
    d_c = p.dma("sync", swapT[:], swap_h, sem="misc")
    d_c = p.dma("sync", identb[:], identb_h, sem="misc")
    d_c = p.dma("sync", trineg[:], trineg_h, sem="misc")

    def fn(eng):
        eng.memset(ones_r[:], 1.0)
        eng.memset(ones_c[:], 1.0)
        eng.memset(stats[:], 0.0)
        eng.drain()
        eng.memset(stats[:, 31:32], EPS_LN)
    v_ones = p.comp("dve", fn, waits=[("misc", d_c)])

    d_b1 = p.dma("sync", gx[0:1, 0:FF], b1_h, sem="misc")
    d_b2 = p.dma("sync", gx[0:1, 4096:4096 + D], b2_h, sem="misc")

    def fn(eng):
        for seg in range(8):
            last = eng.matmul(pb[seg], ones_r[0:1, 0:128],
                              gx[0:1, 512 * seg:512 * (seg + 1)], start=True, stop=True)
        return last
    t_w = p.comp("te", fn, waits=[("misc", d_b2), ("dve", v_ones)])

    def fn(eng):
        for seg in range(8):
            eng.copy(b1b[:, 512 * seg:512 * (seg + 1)], pb[seg])
    s_w = p.comp("scal", fn, waits=[("te", t_w)])

    def fn(eng):
        for seg in range(2):
            last = eng.matmul(pb[seg], ones_r[0:1, 0:128],
                              gx[0:1, 4096 + 512 * seg:4096 + 512 * (seg + 1)],
                              start=True, stop=True)
        return last
    t_w = p.comp("te", fn, waits=[("misc", d_b2), ("scal", s_w)])

    def fn(eng):
        for seg in range(2):
            eng.copy(b2b[:, 512 * seg:512 * (seg + 1)], pb[seg])
    s_w = p.comp("scal", fn, waits=[("te", t_w)])

    def big_load(q, buf_ap, src_ap, width, waits, sem):
        d = 0
        wq = width // 4
        for a in range(4):
            d = p.dma(q, buf_ap[:, wq * a:wq * (a + 1)],
                      src_ap[:, wq * a:wq * (a + 1)],
                      waits=waits if a == 0 else (), sem=sem)
        return d

    slabs = [xblk[:, 0:4096], gx[:, 4096:8192]]
    slab_free = [("dve", 0), ("te", t_w)]
    tern_free = ("sto", 0)

    s_sq = s_w
    t_norm = 0
    for ic in range(NIC):
        bi = ic % 2
        buf = slabs[bi][:, 0:3072]
        sname = "slab%d" % bi
        d_s = big_load("sync", buf, wqkvT_h[128 * ic:128 * (ic + 1), :], 3072,
                       [slab_free[bi]], sname)
        v_c = p.comp("dve", lambda eng, b=buf: eng.tensor_copy(ternb[:, 0:3072], b),
                     waits=[(sname, d_s), tern_free])
        g_l = p.dma("gp", scrQ[:, 1024 * ic:1024 * (ic + 1)], ternb[:, 0:1024], waits=[("dve", v_c)], sem="sto")
        g_l = p.dma("gp", scrK[:, 1024 * ic:1024 * (ic + 1)], ternb[:, 1024:2048], sem="sto")
        g_l = p.dma("gp", scrV[:, 1024 * ic:1024 * (ic + 1)], ternb[:, 2048:3072], sem="sto")
        tern_free = ("sto", g_l)
        s_sq = p.comp("scal", lambda eng, b=buf: eng.activation(
            gx[:, 0:2048], b[:, 0:2048], Act.Square),
            waits=[(sname, d_s), ("scal", s_sq), ("te", t_norm)])

        def fn(eng, ic=ic):
            for seg in range(4):
                last = eng.matmul(pb[2 + seg][0:1, :], ones_c[:, 0:1],
                                  gx[:, 512 * seg:512 * (seg + 1)],
                                  start=(ic == 0), stop=(ic == NIC - 1))
            return last
        t_norm = p.comp("te", fn, waits=[("scal", s_sq)])
        slab_free[bi] = ("scal", s_sq)

    def fn(eng):
        for h in range(16):
            eng.reduce_sum(sums[0:1, h:h + 1],
                           pb[2 + (64 * h) // 512][0:1, (64 * h) % 512:(64 * h) % 512 + 64],
                           axis=AX.X)
        for h in range(16):
            c = 1024 + 64 * h
            eng.reduce_sum(sums[0:1, 16 + h:17 + h],
                           pb[2 + c // 512][0:1, c % 512:c % 512 + 64], axis=AX.X)
        eng.drain()
        eng.tensor_mul(sums[0:1, 32:48], sums[0:1, 0:16], sums[0:1, 16:32])
    v_fro = p.comp("dve", fn, waits=[("te", t_norm)])
    s_fr = p.comp("scal", lambda eng: eng.activation(sums[0:1, 48:64],
                                                     sums[0:1, 32:48], Act.Sqrt),
                  waits=[("dve", v_fro)])
    v_nm = p.comp("dve", lambda eng: eng.tensor_scalar(
        sums[0:1, 64:80], sums[0:1, 48:64], -1.02 * 128.0, None, Alu.mult),
        waits=[("scal", s_fr)])
    t_mh = p.comp("te", lambda eng: eng.matmul(pb[6][:, 0:16], ones_r[0:1, 0:128],
                                               sums[0:1, 64:80], start=True, stop=True),
                  waits=[("dve", v_nm)])
    s_mh = p.comp("scal", lambda eng: eng.copy(mh[:, 0:16], pb[6][:, 0:16]),
                  waits=[("te", t_mh)])
    def fn(eng):
        for c in range(8):
            eng.tensor_tensor(mh[:, 16 + c:17 + c], mh[:, 2 * c:2 * c + 1],
                              mh[:, 2 * c + 1:2 * c + 2], Alu.min)
    s_mh = p.comp("dve", fn, waits=[("scal", s_mh)])

    for fc in range(NIC):
        bi = fc % 2
        buf = slabs[bi][:, 0:1024]
        sname = "slab%d" % bi
        d_s = big_load("sync", buf, woT_h[128 * fc:128 * (fc + 1), :], 1024,
                       [slab_free[bi]], sname)
        v_c = p.comp("dve", lambda eng, b=buf: eng.tensor_copy(ternb[:, 0:1024], b),
                     waits=[(sname, d_s), tern_free])
        g_l = p.dma("gp", scrWo[:, 1024 * fc:1024 * (fc + 1)], ternb[:, 0:1024], waits=[("dve", v_c)], sem="sto")
        tern_free = ("sto", g_l)
        slab_free[bi] = ("dve", v_c)

    v_a = p.comp("dve", lambda eng: eng.memset(stats[:, 28:31], 0.0),
                 waits=[("scal", s_mh)])
    slabs3 = [xblk[:, 0:4096], gx[:, 4096:8192], gx[:, 0:4096]]
    free3 = [slab_free[0], slab_free[1], ("scal", s_mh)]
    for i in range(NIC + NHC):
        w2 = i >= NIC
        width = 1024 if w2 else 4096
        src_ap = (w2T_h[128 * (i - NIC):128 * (i - NIC + 1), :] if w2
                  else w1T_h[128 * i:128 * (i + 1), :])
        bi = i % 3
        buf = slabs3[bi][:, 0:width]
        q = "sync" if i % 2 == 0 else "gp"
        sname = "slab%d" % bi
        d_s = big_load(q, buf, src_ap, width, [free3[bi]], sname)

        def fn(eng, b=buf, w2=w2):
            eng.reduce_sum(stats[:, 30:31], b, axis=AX.X, apply_absolute_value=True)
            eng.drain()
            acc = stats[:, 29:30] if w2 else stats[:, 28:29]
            eng.tensor_add(acc, acc, stats[:, 30:31])
        v_a = p.comp("dve", fn, waits=[(sname, d_s), ("dve", v_a)])
        free3[bi] = ("dve", v_a)
    slab_free[0] = free3[0]
    slab_free[1] = free3[1]
    gx_lo_free = free3[2]

    t_w = p.comp("te", lambda eng: (
        eng.matmul(pb[0][0:1, 0:1], ones_c[:, 0:1], stats[:, 28:29], start=True, stop=True),
        eng.matmul(pb[0][0:1, 1:2], ones_c[:, 0:1], stats[:, 29:30], start=True, stop=True))[-1],
        waits=[("dve", v_a)])
    s_w = p.comp("scal", lambda eng: eng.copy(sums[0:1, 100:102], pb[0][0:1, 0:2]),
                 waits=[("te", t_w)])

    def fn(eng):
        eng.tensor_scalar(sums[0:1, 102:104], sums[0:1, 100:102], INV_NW, EPS_BIT,
                          Alu.mult, Alu.max)
        eng.drain()
        eng.tensor_scalar(sums[0:1, 104:105], sums[0:1, 102:103], 0.5, None, Alu.mult)
        eng.tensor_scalar(sums[0:1, 105:106], sums[0:1, 102:103], -0.5, None, Alu.mult)
        eng.tensor_scalar(sums[0:1, 106:107], sums[0:1, 102:103], 1.0 / 127.0, None, Alu.mult)
        eng.tensor_scalar(sums[0:1, 107:108], sums[0:1, 103:104], 0.5, None, Alu.mult)
        eng.tensor_scalar(sums[0:1, 108:109], sums[0:1, 103:104], -0.5, None, Alu.mult)
        eng.tensor_scalar(sums[0:1, 109:110], sums[0:1, 103:104], 1.0 / 127.0, None, Alu.mult)
    v_ws = p.comp("dve", fn, waits=[("scal", s_w)])
    t_w = p.comp("te", lambda eng: eng.matmul(pb[1][:, 0:6], ones_r[0:1, 0:128],
                                              sums[0:1, 104:110], start=True, stop=True),
                 waits=[("dve", v_ws)])
    s_wsb = p.comp("scal", lambda eng: eng.copy(wsb[:, 0:6], pb[1][:, 0:6]),
                   waits=[("te", t_w)])

    for ic in range(NIC):
        bi = ic % 2
        buf = slabs[bi][:, 0:4096]
        sname = "slab%d" % bi
        d_s = big_load("sync" if ic % 2 == 0 else "gp", buf,
                       w1T_h[128 * ic:128 * (ic + 1), :], 4096,
                       [slab_free[bi], ("scal", s_wsb)] + ([gx_lo_free] if ic == 0 else []),
                       sname)

        def fn(eng, b=buf):
            eng.tensor_single_scalar(gx[:, 0:4096], b, wsb[:, 0:1], Alu.is_gt)
            eng.drain()
            eng.tensor_single_scalar(b, b, wsb[:, 1:2], Alu.is_lt)
            eng.drain()
            eng.tensor_sub(ternb[:, 0:4096], gx[:, 0:4096], b)
        v_t = p.comp("dve", fn, waits=[(sname, d_s), tern_free])
        g_l = 0
        for q in range(4):
            g_l = p.dma("gp",
                        scrW1[:, 1024 * (NIC * q + ic):1024 * (NIC * q + ic) + 1024],
                        ternb[:, 1024 * q:1024 * (q + 1)],
                        waits=[("dve", v_t)] if q == 0 else (), sem="sto")
        tern_free = ("sto", g_l)
        slab_free[bi] = ("dve", v_t)
    for hc in range(NHC):
        bi = hc % 2
        buf = slabs[bi][:, 0:1024]
        sname = "slab%d" % bi
        d_s = big_load("sync" if hc % 2 == 0 else "gp", buf,
                       w2T_h[128 * hc:128 * (hc + 1), :], 1024,
                       [slab_free[bi]], sname)

        def fn(eng, b=buf):
            eng.tensor_single_scalar(gx[:, 0:1024], b, wsb[:, 3:4], Alu.is_gt)
            eng.drain()
            eng.tensor_single_scalar(b, b, wsb[:, 4:5], Alu.is_lt)
            eng.drain()
            eng.tensor_sub(ternb[:, 0:1024], gx[:, 0:1024], b)
        v_t = p.comp("dve", fn, waits=[(sname, d_s), tern_free])
        g_l = p.dma("gp", scrW2[:, 1024 * hc:1024 * (hc + 1)], ternb[:, 0:1024], waits=[("dve", v_t)], sem="sto")
        tern_free = ("sto", g_l)
        slab_free[bi] = ("dve", v_t)

    prep_end = [("sto", p.cnt["sto"]), ("dve", p.cnt["dve"]),
                ("scal", p.cnt["scal"]), ("te", p.cnt["te"])]
    first_stream_extra = [("sto", p.cnt["sto"])]

    slab_hist = {0: 0, 1: 0, 2: 0, 3: 0, 4: 0, 5: 0}

    def slab_load(i, region_ap, extra=()):
        sl = i % 6
        sname = "slab%d" % sl
        dwl = [("te", slab_hist[sl])] + list(extra)
        d = p.dma("sync", wslab[:, 1024 * sl:1024 * (sl + 1)], region_ap,
                  waits=dwl, sem=sname)
        return sl, sname, d

    def stream_pass(n_slabs, region_fn, mm_fn, ic0_extra):
        t = 0
        for i in range(n_slabs):
            extra = first_stream_extra if (first_stream_extra and i < 6) else ()
            sl, sname, d = slab_load(i, region_fn(i), extra)
            wl = [(sname, d)]
            if i == 0:
                wl += ic0_extra
            t = p.comp("te", mm_fn(i, 1024 * sl), waits=wl)
            slab_hist[sl] = t
        if first_stream_extra:
            first_stream_extra.clear()
        return t

    xhalf_free = [("dve", 0), ("dve", 0)]
    rope_free = ("dve", 0)
    g_out = 0
    g_out_ropefree = 0
    x_pre = None
    out_final = 0
    pend_norm = [None]

    for seq in range(2):
        for blk in range(NBLK):
            r0 = S * seq + BLK * blk
            nkt = 4 * (blk + 1)

            def x_tile_load(tt, r0t, wx):
                g = 0
                for ps in range(2):
                    for cs in range(2):
                        g = p.dma(
                            "gp",
                            xblk[64 * ps:64 * (ps + 1),
                                 D * tt + 512 * cs:D * tt + 512 * (cs + 1)],
                            x_h[r0t + 128 * tt + 64 * ps:
                                r0t + 128 * tt + 64 * (ps + 1),
                                512 * cs:512 * (cs + 1)],
                            waits=wx if (ps == 0 and cs == 0) else (),
                            sem="xh%d" % (tt // 2))
                return g

            if x_pre is None:
                wx = list(prep_end) if (seq == 0 and blk == 0) else [xhalf_free[0]]
                x_tile_load(0, r0, wx)
                gh0 = x_tile_load(1, r0, ())
            else:
                gh0 = x_pre
            x_tile_load(2, r0, [xhalf_free[1]])
            gh1 = x_tile_load(3, r0, ())
            g_x = [gh0, gh0, gh1, gh1]
            d_rope = 0
            for i, k in enumerate(("ropeQc", "ropeQs")):
                for ps in range(2):
                    d_rope = p.dma(
                        "sync",
                        ropeT[64 * ps:64 * (ps + 1), BLK * i:BLK * (i + 1)],
                        rope_h[k][64 * ps:64 * (ps + 1),
                                  BLK * blk:BLK * (blk + 1)],
                        waits=[rope_free, ("outs", g_out_ropefree)]
                        if (i == 0 and ps == 0) else (),
                        sem="rope")

            s_sq = [0] * NTT
            v_var = [0] * NTT
            s_nrm = [0] * NTT
            t_tr = p.cnt["te"]
            s_ev = {0: p.cnt["scal"], 1: p.cnt["scal"]}

            def emit_sq(tt):
                xt = xblk[:, D * tt:D * (tt + 1)]
                s_sq[tt] = p.comp("scal", lambda eng, xt=xt, tt=tt: eng.activation(
                    gx[:, 4096:5120], xt, Act.Square,
                    accum_out=stats[:, 26 + tt:27 + tt]),
                    waits=[("xh%d" % (tt // 2), g_x[tt])])

            def emit_ln(tt):
                xt = xblk[:, D * tt:D * (tt + 1)]
                xnt = xnf[:, D * tt:D * (tt + 1)]

                def fn(eng, xt=xt, tt=tt):
                    eng.reduce_sum(stats[:, 30:31], xt, axis=AX.X)
                    eng.drain()
                    eng.tensor_scalar(stats[:, tt:tt + 1], stats[:, 30:31],
                                      -INV_D, None, Alu.mult)
                    eng.drain()
                    eng.tensor_mul(stats[:, 30:31], stats[:, tt:tt + 1],
                                   stats[:, tt:tt + 1])
                    eng.drain()
                    eng.scalar_tensor_tensor(stats[:, 26 + tt:27 + tt],
                                             stats[:, 26 + tt:27 + tt], INV_D,
                                             stats[:, 30:31], Alu.mult, Alu.subtract)
                v_var[tt] = p.comp("dve", fn, waits=[("xh%d" % (tt // 2), g_x[tt]),
                                                     ("scal", s_sq[tt])])

                def fn(eng, xt=xt, xnt=xnt, tt=tt):
                    eng.activation(wsb[:, 6:7], stats[:, 26 + tt:27 + tt], Act.Ln,
                                   bias=stats[:, 31:32])
                    eng.drain()
                    eng.activation(stats[:, 4 + tt:5 + tt], wsb[:, 6:7],
                                   Act.Exp, scale=-0.5)
                    eng.drain()
                    eng.activation(wsb[:, 7:8], stats[:, tt:tt + 1], Act.Identity,
                                   scale=stats[:, 4 + tt:5 + tt])
                    eng.drain()
                    eng.activation(xnt, xt, Act.Identity, bias=wsb[:, 7:8],
                                   scale=stats[:, 4 + tt:5 + tt])
                s_nrm[tt] = p.comp("scal", fn, waits=[("dve", v_var[tt])])

            def emit_tr(tt, t_tr):
                xnt = xnf[:, D * tt:D * (tt + 1)]
                for g2 in range(2):
                    def fn(eng, xnt=xnt, g2=g2):
                        for q in range(4):
                            icx = 4 * g2 + q
                            last = eng.transpose(pb[6 + g2][:, 128 * q:128 * (q + 1)],
                                                 xnt[:, 128 * icx:128 * (icx + 1)], identf[:])
                        return last
                    t_tr = p.comp("te", fn,
                                  waits=[("scal", max(s_nrm[tt], s_ev[g2]))])

                    def fn(eng, tt=tt, g2=g2):
                        for q in range(4):
                            icx = 4 * g2 + q
                            eng.copy(
                                xnqfm[:, BLK * icx + 128 * tt:BLK * icx + 128 * (tt + 1)],
                                pb[6 + g2][:, 128 * q:128 * (q + 1)])
                    s_ev[g2] = p.comp("scal", fn, waits=[("te", t_tr)])
                return t_tr

            emit_sq(0)
            emit_sq(1)
            emit_ln(0)
            t_tr = emit_tr(0, t_tr)
            emit_ln(1)
            t_tr = emit_tr(1, t_tr)
            emit_sq(2)
            emit_sq(3)
            emit_ln(2)
            t_tr = emit_tr(2, t_tr)
            emit_ln(3)
            t_tr = emit_tr(3, t_tr)

            for tt in range(NTT):
                xnt = xnf[:, D * tt:D * (tt + 1)]

                def fn(eng, xnt=xnt, tt=tt):
                    eng.reduce_max(stats[:, 8 + tt:9 + tt], xnt, axis=AX.X,
                                   apply_absolute_value=True)
                    eng.drain()
                    eng.tensor_scalar(stats[:, 8 + tt:9 + tt],
                                      stats[:, 8 + tt:9 + tt],
                                      EPS_BIT, None, Alu.max)
                    eng.drain()
                    eng.reciprocal(stats[:, 30:31], stats[:, 8 + tt:9 + tt])
                    eng.drain()
                    eng.tensor_scalar(stats[:, 12 + tt:13 + tt], stats[:, 30:31],
                                      127.0, None, Alu.mult)
                    eng.tensor_mul(stats[:, 16 + tt:17 + tt],
                                   stats[:, 8 + tt:9 + tt], wsb[:, 2:3])
                p.comp("dve", fn, waits=[("scal", s_nrm[tt])])
            v_xnfm = max(s_ev[0], s_ev[1])

            def qkv_pass(region, kind, ic0_waits):
                def mk(ic, cb, kind=kind):
                    def fn(eng, ic=ic, cb=cb, kind=kind):
                        if kind == "V":
                            for tt in range(NTT):
                                for hf in range(2):
                                    last = eng.matmul(
                                        pb[2 * tt + hf],
                                        xnqfm[:, BLK * ic + 128 * tt:BLK * ic + 128 * (tt + 1)],
                                        wslab[:, cb + 512 * hf:cb + 512 * (hf + 1)],
                                        start=(ic == 0), stop=(ic == NIC - 1))
                        else:
                            for oc in range(8):
                                last = eng.matmul(
                                    pb[oc],
                                    wslab[:, cb + 128 * oc:cb + 128 * (oc + 1)],
                                    xnqfm[:, BLK * ic:BLK * (ic + 1)],
                                    start=(ic == 0), stop=(ic == NIC - 1))
                        return last
                    return fn
                return stream_pass(NIC, lambda i: region[:, 1024 * i:1024 * (i + 1)],
                                   mk, ic0_waits)

            def rope_apply(scale8, dst, t_pass, v_prev):
                s_ev = 0
                t_sw = 0
                for c in range(NPAIR):
                    wl = [("te", t_pass if c == 0 else t_sw)]
                    s_ev = p.comp("scal", lambda eng, c=c: eng.copy(
                        qpre[:, BLK * c:BLK * (c + 1)], pb[c]), waits=wl)
                    t_sw = p.comp("te", lambda eng, c=c: eng.matmul(
                        pb[c], swapT[:], qpre[:, BLK * c:BLK * (c + 1)],
                        start=True, stop=True), waits=[("scal", s_ev)])

                    def fn(eng, c=c, scale8=scale8):
                        if scale8:
                            eng.scalar_tensor_tensor(
                                t1b[:, :], qpre[:, BLK * c:BLK * (c + 1)], 8.0,
                                ropeT[:, 0:BLK], Alu.mult, Alu.mult)
                            eng.scalar_tensor_tensor(
                                t2b[:, :], pb[c], 8.0,
                                ropeT[:, BLK:2 * BLK], Alu.mult, Alu.mult)
                        else:
                            eng.tensor_mul(t1b[:, :], qpre[:, BLK * c:BLK * (c + 1)],
                                           ropeT[:, 0:BLK])
                            eng.tensor_mul(t2b[:, :], pb[c],
                                           ropeT[:, BLK:2 * BLK])
                        eng.drain()
                        eng.tensor_add(dst(c), t1b[:, :], t2b[:, :])
                    v_prev = p.comp("dve", fn, waits=[("te", t_sw), ("rope", d_rope),
                                                      ("dve", v_prev)])
                return v_prev, t_sw

            t_q = qkv_pass(scrQ, "QK", [("scal", v_xnfm)])
            v_qrot, t_sw = rope_apply(False,
                                      lambda c: qrot[:, BLK * c:BLK * (c + 1)],
                                      t_q, p.cnt["dve"])
            t_k = qkv_pass(scrK, "QK", [("dve", v_qrot)])
            v_krot, t_sw = rope_apply(True,
                                      lambda c, blk=blk: kfm[:, S * c + BLK * blk:
                                                             S * c + BLK * (blk + 1)],
                                      t_k, v_qrot)
            t_v = qkv_pass(scrV, "V", [("dve", v_krot)])

            v_pre = v_krot
            if blk == 0:
                def fn(eng):
                    for kt in range(NKT_SEQ):
                        for h in range(H):
                            eng.memset(vex[:, 1040 * kt + 65 * h + 64:
                                            1040 * kt + 65 * h + 65], 1.0)
                v_pre = p.comp("dve", fn, waits=[("dve", v_krot), ("te", t_v)])

            s_vex = {}
            for tt in (2, 3, 1, 0):
                def fn(eng, tt=tt, blk=blk):
                    kt = 4 * blk + tt
                    for hf in range(2):
                        for hh in range(8):
                            h = 8 * hf + hh
                            eng.copy(
                                vex[:, 1040 * kt + 65 * h:1040 * kt + 65 * h + 64],
                                pb[2 * tt + hf][:, 64 * hh:64 * (hh + 1)])
                s_vex[tt] = p.comp("scal", fn,
                                   waits=[("te", t_v)] if tt == 2 else ())
            s_vex_fin = s_vex[0]

            v_q8 = p.cnt["dve"]
            t_tr = p.cnt["te"]
            t_tr_prev = {0: 0, 1: 0}
            s_ev_par = {0: 0, 1: 0}
            for tt in range(NTT):
                for q8 in range(2):
                    par = (2 * tt + q8) & 1
                    dstb = hraw[:, 512:1024] if par == 0 else hraw[:, 0:512]

                    def fn(eng, tt=tt, q8=q8, dstb=dstb):
                        eng.tensor_scalar(scrA[:, :],
                                          xnf[:, D * tt + 512 * q8:D * tt + 512 * (q8 + 1)],
                                          stats[:, 12 + tt:13 + tt], MAGIC,
                                          Alu.mult, Alu.add)
                        eng.drain()
                        eng.tensor_scalar(dstb[:, :], scrA[:, :], -MAGIC, None, Alu.add)
                    v_q8 = p.comp("dve", fn, waits=[("dve", v_q8),
                                                    ("te", t_tr_prev[par])])

                    def fn(eng, par=par, dstb=dstb):
                        for t4 in range(4):
                            last = eng.transpose(pb[4 + par][:, 128 * t4:128 * (t4 + 1)],
                                                 dstb[:, 128 * t4:128 * (t4 + 1)], identf[:])
                        return last
                    t_tr = p.comp("te", fn, waits=[
                        ("dve", v_q8),
                        ("scal", s_vex[2] if tt == 0 else s_ev_par[par])])
                    t_tr_prev[par] = t_tr

                    def fn(eng, tt=tt, q8=q8, par=par):
                        for t4 in range(4):
                            icx = 4 * q8 + t4
                            eng.copy(
                                xnqfm[:, BLK * icx + 128 * tt:BLK * icx + 128 * (tt + 1)],
                                pb[4 + par][:, 128 * t4:128 * (t4 + 1)])
                    v_q8s = p.comp("scal", fn, waits=[("te", t_tr)])
                    s_ev_par[par] = v_q8s
            v_xq = v_q8s
            t_xnf_last = t_tr

            s_pv_prev = p.cnt["dve"]
            v_attn_fin = p.cnt["dve"]
            g_att_fin = 0
            bank_free = {}
            pt_ready = {}
            t_pv_hist = {}
            prev_last_exp = 0
            norm_state = {"g_r1": 0, "v_attn": p.cnt["dve"], "g_att": 0}
            norm_done = {}

            def norm_emit(pc_n, s_sm_n, s_pv_n):
                st = norm_state
                g_r1 = p.dma("gp", rmap[:, :], invs[64:65, 0:1024],
                             waits=[("dve", s_sm_n)], sem="nrm")
                v_rc = p.comp("dve", lambda eng: eng.reciprocal(rmap[:, :], rmap[:, :]),
                              waits=[("nrm", g_r1)])
                g_w = p.dma("gp", scrInv, rmap[:, :], waits=[("dve", v_rc)], sem="nrm")
                g_bc = p.dma("gp", invs[0:64, 0:1024],
                             bass.AP(scrInv.tensor, 0, [[0, 64], [1, 1024]]),
                             waits=[("nrm", g_w)], sem="nrm")

                def fn(eng, pc_n=pc_n):
                    a = attn_fm[0:64, BLK * pc_n:BLK * (pc_n + 1)]
                    eng.tensor_mul(a, a, invs[0:64, 0:512])
                    eng.tensor_mul(stage[0:64, :], stage[0:64, :],
                                   invs[0:64, 512:1024])
                v_at = p.comp("dve", fn, waits=[("nrm", g_bc), ("dve", s_pv_n)])
                g_at = p.dma("gp", attn_fm[64:128, BLK * pc_n:BLK * (pc_n + 1)],
                             stage[0:64, :], waits=[("dve", v_at)], sem="att")
                st.update(g_r1=g_r1, v_attn=v_at, g_att=g_at)
                norm_done[pc_n] = (v_at, g_at)
                return v_at, g_at

            pend = None
            SCB = [(pb[0], pb[1]), (pb[2], pb[3]), (pb[6], pb[7])]
            SCD = [pbd[0], pbd[1], pbd[3]]
            for pc in range(NPAIR):
                hA, hB = 2 * pc, 2 * pc + 1
                ptAB = [ptb[:, 1024 * i:1024 * (i + 1)] for i in range(3)]
                ptA = [ptAB[i][:, 0:512] for i in range(3)]
                ptB = [ptAB[i][:, 512:1024] for i in range(3)]

                def sc_emit(kt):
                    par = kt % 3
                    sA, sB = SCB[par]
                    jd = kt - 4 * blk
                    band = jd >= 0
                    w0 = 128 * jd if band else 0

                    def fn(eng, pc=pc, kt=kt, sA=sA, sB=sB, band=band, w0=w0):
                        last = eng.matmul(sA,
                                   kfm[0:64, S * pc + 128 * kt:S * pc + 128 * (kt + 1)],
                                   qrot[0:64, BLK * pc:BLK * (pc + 1)],
                                   start=True, stop=not band, tile_position=(0, 0))
                        last = eng.matmul(sB,
                                   kfm[64:128, S * pc + 128 * kt:S * pc + 128 * (kt + 1)],
                                   qrot[64:128, BLK * pc:BLK * (pc + 1)],
                                   start=True, stop=not band, tile_position=(64, 0))
                        if band:
                            for s in (sA, sB):
                                last = eng.matmul(s[:, w0:w0 + 128], trineg[:],
                                                  identb[:], start=False,
                                                  stop=True)
                        return last
                    wl = []
                    if (pc, kt - 3) in bank_free:
                        wl.append(("scal", bank_free[(pc, kt - 3)]))
                    elif kt < 3:
                        wl.append(("scal", prev_last_exp))
                    if pc == 0 and kt <= 2:
                        wl.append(("scal", v_xq))
                        wl.append(("dve", v_pre))
                    t_sc = p.comp("te", fn, waits=wl)

                    def fn(eng, par=par, pc=pc, band=band, w0=w0):
                        pt, sd = ptAB[par], SCD[par]
                        if band and w0 > 0:
                            eng.activation(pt[:, 0:w0], sd[:, 0:w0],
                                           Act.Copy, scale=0.0)
                            eng.activation(pt[:, 512:512 + w0],
                                           sd[:, 512:512 + w0],
                                           Act.Copy, scale=0.0)
                            eng.activation(pt[:, w0:512], sd[:, w0:512],
                                           Act.Exp,
                                           bias=mh[:, 16 + pc:17 + pc],
                                           scale=1.0)
                            eng.activation(pt[:, 512 + w0:1024],
                                           sd[:, 512 + w0:1024], Act.Exp,
                                           bias=mh[:, 16 + pc:17 + pc],
                                           scale=1.0)
                        else:
                            eng.activation(pt[:, :], sd[:, :], Act.Exp,
                                           bias=mh[:, 16 + pc:17 + pc],
                                           scale=1.0)
                    wl = [("te", t_sc)]
                    if (pc, kt - 3) in t_pv_hist:
                        wl.append(("te", t_pv_hist[(pc, kt - 3)]))
                    s_exp = p.comp("scal", fn, waits=wl)
                    bank_free[(pc, kt)] = s_exp
                    pt_ready[(pc, kt)] = ("scal", s_exp)

                def pv_emit(kt):
                    par = kt % 3
                    wl = [pt_ready[(pc, kt)]]
                    if kt == 0:
                        wl.append(("dve", s_pv_prev))
                        if pc == 0:
                            wl.append(("dve", v_pre))
                            wl.append(("scal", s_vex_fin))

                    def fn(eng, kt=kt, par=par, hA=hA, hB=hB, nkt=nkt):
                        eng.matmul(pb[4][0:65, :],
                                   vex[:, 1040 * kt + 65 * hA:1040 * kt + 65 * (hA + 1)],
                                   ptA[par][:, :], start=(kt == 0), stop=(kt == nkt - 1))
                        return eng.matmul(pb[5][0:65, :],
                                   vex[:, 1040 * kt + 65 * hB:1040 * kt + 65 * (hB + 1)],
                                   ptB[par][:, :], start=(kt == 0), stop=(kt == nkt - 1))
                    t_pv = p.comp("te", fn, waits=wl)
                    t_pv_hist[(pc, kt)] = t_pv
                    return t_pv

                for kt in range(nkt):
                    sc_emit(kt)
                    if kt >= 2:
                        pv_emit(kt - 2)
                    if kt == 3 and pend is not None:
                        norm_emit(*pend)
                        pend = None
                pv_emit(nkt - 2)
                t_pv = pv_emit(nkt - 1)
                prev_last_exp = bank_free[(pc, nkt - 1)]

                wl_sm = [("te", t_pv), ("nrm", norm_state["g_r1"])]
                if pc == 0:
                    wl_sm.append(("outs", g_out))
                s_sm = p.comp("dve", lambda eng: eng.tensor_copy(
                    invs[64:65, 0:1024], pbd[2][64:65, 0:1024]), waits=wl_sm)

                def fn(eng, pc=pc):
                    eng.tensor_copy(attn_fm[0:64, BLK * pc:BLK * (pc + 1)],
                                    pb[4][0:64, :])
                    eng.tensor_copy(stage[0:64, :], pb[5][0:64, :])
                s_pv = p.comp("dve", fn,
                              waits=[("att", norm_state["g_att"]),
                                     ("dve", norm_state["v_attn"])])
                s_pv_prev = s_pv
                if pend is not None:
                    norm_emit(*pend)
                pend = (pc, s_sm, s_pv)
            v_attn_fin, g_att_fin = norm_emit(*pend)
            pend = None

            def mk_wo(fc, cb):
                def fn(eng, fc=fc, cb=cb):
                    for tt in range(NTT):
                        for oh in range(2):
                            last = eng.matmul(
                                pb[2 * tt + oh],
                                attn_fm[:, BLK * fc + 128 * tt:BLK * fc + 128 * (tt + 1)],
                                wslab[:, cb + 512 * oh:cb + 512 * (oh + 1)],
                                start=(fc == 0), stop=(fc == NIC - 1))
                    return last
                return fn
            t_wo = 0
            for i in range(NIC):
                sl, sname, d = slab_load(i, scrWo[:, 1024 * i:1024 * (i + 1)])
                wl = [(sname, d), ("att", norm_done[i][1]),
                      ("dve", norm_done[i][0])]
                if i == 0:
                    wl.append(("dve", s_pv_prev))
                t_wo = p.comp("te", mk_wo(i, 1024 * sl), waits=wl)
                slab_hist[sl] = t_wo

            def fn(eng):
                for tt in range(NTT):
                    for oh in range(2):
                        dst = xblk[:, D * tt + 512 * oh:D * tt + 512 * (oh + 1)]
                        eng.tensor_add(dst, pb[2 * tt + oh], dst)
            v_acc = p.comp("dve", fn, waits=[("te", t_wo)])

            v_g_hist = {-2: v_acc, -1: v_acc}
            v_free_h = [0, 0]
            for hb in range(2):
                s_g = p.cnt["scal"]
                for pss in range(4):
                    base = 4 * (pss % 2)

                    def mk_f1(ic, cb, hb=hb, base=base):
                        def fn(eng, ic=ic, cb=cb, hb=hb, base=base):
                            for tti in range(2):
                                tt = 2 * hb + tti
                                for fbh in range(2):
                                    last = eng.matmul(
                                        pb[base + 2 * tti + fbh],
                                        xnqfm[:, BLK * ic + 128 * tt:BLK * ic + 128 * (tt + 1)],
                                        wslab[:, cb + 512 * fbh:cb + 512 * (fbh + 1)],
                                        start=(ic == 0), stop=(ic == NIC - 1))
                            return last
                        return fn
                    extra = [("dve", v_g_hist[pss - 2])]
                    if pss == 0 and hb == 0:
                        extra.append(("scal", v_xq))
                    t_f1 = stream_pass(
                        NIC,
                        lambda i, pss=pss: scrW1[:, 1024 * (NIC * pss + i):
                                                 1024 * (NIC * pss + i + 1)],
                        mk_f1, extra)

                    v_g = v_g_hist[pss - 1]
                    for tti in range(2):
                        tt = 2 * hb + tti
                        for fbh in range(2):
                            fb = 2 * pss + fbh

                            def fn(eng, tti=tti, fbh=fbh, tt=tt, fb=fb, base=base):
                                eng.scalar_tensor_tensor(
                                    hraw[:, 512 * tti:512 * (tti + 1)],
                                    pb[base + 2 * tti + fbh], stats[:, 16 + tt:17 + tt],
                                    b1b[:, 512 * fb:512 * (fb + 1)], Alu.mult, Alu.add)
                            v_g = p.comp("dve", fn, waits=[("te", t_f1), ("scal", s_g)])

                            def fn(eng, tti=tti, fb=fb):
                                eng.activation(
                                    gbuf[:, FF * tti + 512 * fb:FF * tti + 512 * (fb + 1)],
                                    hraw[:, 512 * tti:512 * (tti + 1)], Act.Gelu)
                            s_g = p.comp("scal", fn, waits=[("dve", v_g)])
                    v_g_hist[pss] = v_g
                    for tti in range(2):
                        def fn(eng, tti=tti, pss=pss):
                            eng.reduce_max(
                                scrA[:, 4 * tti + pss:4 * tti + pss + 1],
                                gbuf[:, FF * tti + 1024 * pss:
                                     FF * tti + 1024 * (pss + 1)],
                                axis=AX.X, apply_absolute_value=True)
                        p.comp("dve", fn, waits=[("scal", s_g)])

                v_hq = p.cnt["dve"]
                s_hq = p.cnt["scal"]
                t_tr2_prev = {0: 0, 1: 0}
                for tti in range(2):
                    gt_ = gbuf[:, FF * tti:FF * (tti + 1)]

                    def fn(eng, gt_=gt_, tti=tti):
                        eng.reduce_max(stats[:, 20 + tti:21 + tti],
                                       scrA[:, 4 * tti:4 * tti + 4], axis=AX.X,
                                       apply_absolute_value=True)
                        eng.drain()
                        eng.tensor_scalar(stats[:, 20 + tti:21 + tti],
                                          stats[:, 20 + tti:21 + tti], EPS_BIT, None,
                                          Alu.max)
                        eng.drain()
                        eng.reciprocal(stats[:, 30:31], stats[:, 20 + tti:21 + tti])
                        eng.drain()
                        eng.tensor_scalar(stats[:, 22 + tti:23 + tti],
                                          stats[:, 30:31], 127.0, None, Alu.mult)
                        eng.tensor_mul(stats[:, 24 + tti:25 + tti],
                                       stats[:, 20 + tti:21 + tti], wsb[:, 5:6])
                    v_hq = p.comp("dve", fn, waits=[("scal", s_g), ("dve", v_hq)])

                t_f2 = 0
                for c8 in range(8):
                    for tti in range(2):
                        gt_ = gbuf[:, FF * tti:FF * (tti + 1)]
                        par = (2 * c8 + tti) & 1
                        dstb = hraw[:, 512:1024] if par == 0 else hraw[:, 0:512]

                        def fn(eng, gt_=gt_, c8=c8, tti=tti, dstb=dstb):
                            eng.tensor_scalar(scrA[:, :], gt_[:, 512 * c8:512 * (c8 + 1)],
                                              stats[:, 22 + tti:23 + tti], MAGIC,
                                              Alu.mult, Alu.add)
                            eng.drain()
                            eng.tensor_scalar(dstb[:, :], scrA[:, :], -MAGIC, None,
                                              Alu.add)
                        v_hq = p.comp("dve", fn, waits=[("dve", v_hq),
                                                        ("te", t_tr2_prev[par])])

                        def fn(eng, par=par, dstb=dstb):
                            for t4 in range(4):
                                last = eng.transpose(pb[6 + par][:, 128 * t4:128 * (t4 + 1)],
                                                     dstb[:, 128 * t4:128 * (t4 + 1)],
                                                     identf[:])
                            return last
                        t_tr2 = p.comp("te", fn, waits=[("dve", v_hq)])
                        t_tr2_prev[par] = t_tr2

                        def fn(eng, c8=c8, tti=tti, par=par):
                            for t4 in range(4):
                                hc = 4 * c8 + t4
                                eng.copy(
                                    hq_fm[:, 256 * hc + 128 * tti:256 * hc + 128 * (tti + 1)],
                                    pb[6 + par][:, 128 * t4:128 * (t4 + 1)])
                        s_hq = p.comp("scal", fn, waits=[("te", t_tr2)])

                    for j in range(4):
                        hc = 4 * c8 + j
                        sl, sname, d = slab_load(hc, scrW2[:, 1024 * hc:1024 * (hc + 1)])
                        cb = 1024 * sl

                        def fn(eng, hc=hc, cb=cb):
                            for tti in range(2):
                                for dh in range(2):
                                    last = eng.matmul(
                                        pb[2 * tti + dh],
                                        hq_fm[:, 256 * hc + 128 * tti:256 * hc + 128 * (tti + 1)],
                                        wslab[:, cb + 512 * dh:cb + 512 * (dh + 1)],
                                        start=(hc == 0), stop=(hc == NHC - 1))
                            return last
                        wl = [(sname, d), ("scal", s_hq)]
                        if hc == 0:
                            wl.append(("dve", v_g_hist[2]))
                        slab_hist[sl] = p.comp("te", fn, waits=wl)
                        t_f2 = slab_hist[sl]

                def fn(eng, hb=hb):
                    for tti in range(2):
                        tt = 2 * hb + tti
                        ob = outtA if tti == 0 else outtB
                        for dh in range(2):
                            eng.scalar_tensor_tensor(
                                scrA[:, :], pb[2 * tti + dh],
                                stats[:, 24 + tti:25 + tti],
                                b2b[:, 512 * dh:512 * (dh + 1)], Alu.mult, Alu.add)
                            eng.drain()
                            eng.tensor_add(
                                ob[:, 512 * dh:512 * (dh + 1)],
                                scrA[:, :],
                                xblk[:, D * tt + 512 * dh:D * tt + 512 * (dh + 1)])
                            eng.drain()
                v_free = p.comp("dve", fn, waits=[("te", t_f2), ("outs", g_out)])
                v_free_h[hb] = v_free
                v_g_hist[2] = v_free
                v_g_hist[-2] = v_free
                v_g_hist[-1] = v_free

                for tti in range(2):
                    tt = 2 * hb + tti
                    ob = outtA if tti == 0 else outtB
                    for ps in range(2):
                        g_out = p.dma(
                            "gp",
                            out_h[r0 + 128 * tt + 64 * ps:
                                  r0 + 128 * tt + 64 * (ps + 1), :],
                            ob[64 * ps:64 * (ps + 1), 0:D],
                            waits=[("dve", v_free)]
                            if (tti == 0 and ps == 0) else (),
                            sem="outs")
                    if hb == 1 and tti == 0:
                        g_out_ropefree = g_out

                if hb == 0 and not (seq == 1 and blk == NBLK - 1):
                    r0n = (r0 + BLK) if blk < NBLK - 1 else S * (seq + 1)
                    x_tile_load(0, r0n, [("dve", v_free_h[0])])
                    x_pre = x_tile_load(1, r0n, ())

            out_final = g_out
            xhalf_free = [("dve", v_free_h[0]), ("dve", v_free_h[1])]
            rope_free = ("dve", v_krot)

    p.step("gp", lambda eng: None, waits=[("outs", out_final)])

    with nc.Block() as block:
        @block.sync
        def _(eng):
            p.run("sync", eng)

        @block.gpsimd
        def _(eng):
            p.run("gp", eng)

        @block.vector
        def _(eng):
            p.run("dve", eng)

        @block.scalar
        def _(eng):
            p.run("scal", eng)

        @block.tensor
        def _(eng):
            p.run("te", eng)

    stack.close()
    return nc


_NC_CACHE = None


def kernel(L0_attn_x, L0_attn_W_qkv, L0_attn_W_o, L0_ffn_W1, L0_ffn_b1,
           L0_ffn_W2, L0_ffn_b2):
    global _NC_CACHE
    x = np.ascontiguousarray(np.asarray(L0_attn_x, dtype=np.float32))
    wqkvT = np.ascontiguousarray(np.asarray(L0_attn_W_qkv, dtype=np.float32).T)
    woT = np.ascontiguousarray(np.asarray(L0_attn_W_o, dtype=np.float32).T)
    w1T = np.ascontiguousarray(np.asarray(L0_ffn_W1, dtype=np.float32).T)
    w2T = np.ascontiguousarray(np.asarray(L0_ffn_W2, dtype=np.float32).T)
    b1 = np.ascontiguousarray(np.asarray(L0_ffn_b1, dtype=np.float32).reshape(1, FF))
    b2 = np.ascontiguousarray(np.asarray(L0_ffn_b2, dtype=np.float32).reshape(1, D))
    consts, identf, tri01, swapT, identb, trineg = _host_consts()

    if _NC_CACHE is None:
        _NC_CACHE = build_program()
    nc = _NC_CACHE

    common = {
        "wqkvT": wqkvT, "woT": woT, "w1T": w1T, "w2T": w2T,
        "b1": b1, "b2": b2, "identf": identf, "tri01": tri01, "swapT": swapT,
        "identb": identb, "trineg": trineg,
        **consts,
    }
    in_maps = []
    for c in range(8):
        shard = np.ascontiguousarray(x[2 * c:2 * c + 2].reshape(T, D))
        m = dict(common)
        m["x"] = shard
        in_maps.append(m)

    res = run_bass_kernel_spmd(nc, in_maps, core_ids=list(range(8)))
    outs = [res.results[c]["out"].reshape(2, S, D) for c in range(8)]
    return np.ascontiguousarray(np.concatenate(outs, axis=0))
